# version 19
# baseline (speedup 1.0000x reference)
import numpy as np
from contextlib import ExitStack

import concourse.bass as bass
import concourse.mybir as mybir
from concourse.bass_utils import run_bass_kernel_spmd

F32 = mybir.dt.float32
BF16 = mybir.dt.bfloat16
I32 = mybir.dt.int32
AF = mybir.ActivationFunctionType
ALU = mybir.AluOpType
AX = mybir.AxisListType

D = 1024
NCH = D // 128
NCORES = 8
LN_EPS = 1e-5
RMS_EPS = 1e-6
NEG_BIG = -1.0e30

ENGS = ("sync", "tensor", "vector", "scalar", "gpsimd")


class Prog:

    def __init__(self, nc):
        self.nc = nc
        self.q = {e: [] for e in ENGS}
        self.cnt = {}
        self.sems = {}
        self._es = ExitStack()
        for name in ("tensor", "vector", "scalar", "gpsimd"):
            self.sems[name] = self._es.enter_context(nc.semaphore("sem_" + name))
            self.cnt[name] = 0
        self._waited = {}

    def chan(self, name):
        if name not in self.sems:
            self.sems[name] = self._es.enter_context(self.nc.semaphore("ch_" + name))
            self.cnt[name] = 0
        return name

    def _push_wait(self, eng, sem_name, val):
        if val is None or val <= 0:
            return
        key = (eng, sem_name)
        if self._waited.get(key, -1) >= val:
            return
        self._waited[key] = val
        h = self.sems[sem_name]
        self.q[eng].append(lambda E, h=h, v=val: E.wait_ge(h, v))

    def op(self, eng, build, waits=(), inc="auto", inc_k=1, chain=True):
        if inc == "auto":
            inc = eng if eng in ("vector", "scalar", "gpsimd") else None
        if chain and eng in ("vector", "scalar", "gpsimd") and self.cnt[eng] > 0:
            self._push_wait(eng, eng, self.cnt[eng])
        for sem_name, val in waits:
            self._push_wait(eng, sem_name, val)
        ev = None
        h = None
        if inc is not None:
            self.cnt[inc] += inc_k
            ev = self.cnt[inc]
            h = self.sems[inc]

        def closure(E, build=build, h=h, k=inc_k):
            inst = build(E)
            if h is not None:
                inst.then_inc(h, k)
        self.q[eng].append(closure)
        return ev

    def mm(self, build, waits=(), inc=False):
        return self.op("tensor", build, waits=waits,
                       inc=("tensor" if inc else None), chain=False)

    def dma(self, queue, chan, out, in_, waits=()):
        self.chan(chan)
        waits = list(waits) + [(chan, self.cnt[chan])]
        return self.op(queue, lambda E, o=out, i=in_: E.dma_start(out=o, in_=i),
                       waits=waits, inc=chan, inc_k=16, chain=False)

    def raw(self, eng, fn, waits=()):
        for sem_name, val in waits:
            self._push_wait(eng, sem_name, val)
        self.q[eng].append(fn)

    def barrier_all(self):
        snap = dict(self.cnt)
        for eng in ENGS:
            for sem_name, val in snap.items():
                self._push_wait(eng, sem_name, val)

    def ev(self, sem_name):
        return self.cnt[sem_name]

    def finalize(self, block):
        block.sync(lambda E: [f(E) for f in self.q["sync"]])
        block.tensor(lambda E: [f(E) for f in self.q["tensor"]])
        block.vector(lambda E: [f(E) for f in self.q["vector"]])
        block.scalar(lambda E: [f(E) for f in self.q["scalar"]])
        block.gpsimd(lambda E: [f(E) for f in self.q["gpsimd"]])

    def close(self):
        self._es.close()


def bc8(ap2d, n, w):
    return ap2d.rearrange("p (a t) -> p a t", a=1).broadcast_to([128, n, w])


def v8(tile, n, w):
    return tile[:, 0:n * w].rearrange("p (a t) -> p a t", a=n)


def ccol(tile, c, width, sl=None):
    if sl is None:
        return tile[:, c * width:(c + 1) * width]
    start, stop = sl
    return tile[:, c * width + start: c * width + stop]



def _feat_tile(v):
    return np.ascontiguousarray(v.reshape(-1, 128).T)


def prep_host(inputs):
    g = {}
    g["w1t"] = np.ascontiguousarray(inputs["L0_ffn_w1"].transpose(0, 2, 1))
    g["w2t"] = np.ascontiguousarray(inputs["L0_ffn_w2"].transpose(0, 2, 1))
    b1 = np.asarray(inputs["L0_ffn_b1"])
    g["b1s"] = np.ascontiguousarray(
        b1.reshape(8, 16, 128).transpose(2, 0, 1).reshape(128, 128))
    g["b2s"] = np.ascontiguousarray(np.asarray(inputs["L0_ffn_b2"]))
    g["nws"] = _feat_tile(np.asarray(inputs["L0_ffn_norm_weight"]))
    gwT = np.asarray(inputs["L0_ffn_gate_w"]).T
    g["gwt"] = np.ascontiguousarray(gwT.reshape(8, 128, 8).transpose(1, 0, 2).reshape(128, 64))
    g["l0wt"] = np.ascontiguousarray(np.asarray(inputs["L0_attn_W"]).T)
    g["l0b"] = _feat_tile(np.asarray(inputs["L0_attn_b"]))
    g["l0g"] = _feat_tile(np.asarray(inputs["L0_attn_gamma"]))
    g["l0bt"] = _feat_tile(np.asarray(inputs["L0_attn_beta"]))
    g["decay"] = np.asarray(inputs["L0_attn_decay"], np.float32).reshape(1, 1)
    for k in ("q", "k", "v", "o"):
        g[f"{k}wt"] = np.ascontiguousarray(np.asarray(inputs[f"L1_attn_{k}_w"]).T)
        g[f"{k}b"] = _feat_tile(np.asarray(inputs[f"L1_attn_{k}_b"]))
    g["f1t"] = np.ascontiguousarray(np.asarray(inputs["L1_ffn_w1_w"]).T)
    g["f2t"] = np.ascontiguousarray(np.asarray(inputs["L1_ffn_w2_w"]).T)
    g["h1t"] = np.ascontiguousarray(np.asarray(inputs["L1_ffn_h1_w"]).T)
    g["h2t"] = np.ascontiguousarray(np.asarray(inputs["L1_ffn_h2_w"]).T)
    g["f3t"] = np.ascontiguousarray(np.asarray(inputs["L1_ffn_w3_w"]).T)
    g["s1s"] = _feat_tile(np.asarray(inputs["L1_ffn_w1_s"]))
    g["s2s"] = _feat_tile(np.asarray(inputs["L1_ffn_w2_s"]))
    g["sh1"] = _feat_tile(np.asarray(inputs["L1_ffn_h1_s"]))
    g["sh2"] = _feat_tile(np.asarray(inputs["L1_ffn_h2_s"]))
    g["s3s"] = _feat_tile(np.asarray(inputs["L1_ffn_w3_s"]))
    g["identf"] = np.eye(128, dtype=np.float32)
    tj = np.arange(128)[None, :] - np.arange(128)[:, None]
    tri01 = (tj >= 0).astype(np.float32)
    g["m1tri"] = (tj * tri01).astype(np.float32)
    g["tri01"] = tri01
    g["iota_row"] = (np.arange(4096, dtype=np.float32) + 1.0)[None, :]
    g["ones_col"] = np.ones((128, 1), np.float32)
    g["ones_row"] = np.ones((1, 128), np.float32)
    g["selrows"] = np.kron(np.eye(8, dtype=np.float32), np.ones((1, 128), np.float32))
    return g


def shard_x(x):
    out = []
    for c in range(NCORES):
        b, h = c // 2, c % 2
        out.append(np.ascontiguousarray(np.asarray(x)[b, h * 4096:(h + 1) * 4096, :]))
    return out


def fm_view(hbm_ap):
    return hbm_ap.rearrange("(c p) t -> p c t", p=128)



def emit_ln_stats(P, sb, xt, TT, pre_waits, eps_ap, extra_rms=False):
    P.op("scalar", lambda E: E.activation(
        sb["xq"][:, 0:NCH * TT], xt[:, 0:NCH * TT], AF.Copy), waits=pre_waits)
    ev_sc = P.op("scalar", lambda E: E.activation(
        sb["sq"][:, 0:NCH * TT], xt[:, 0:NCH * TT], AF.Square))
    ev_mm = None
    for c in range(NCH):
        P.mm(lambda E, c=c: E.matmul(
            sb["pst1"][0:1, 0:TT], sb["onesb"][:, :], ccol(sb["xq"], c, TT),
            start=(c == 0), stop=(c == NCH - 1)),
            waits=[("scalar", ev_sc)] + list(sb.get("stat_war", []))
                  + list(sb.get("const_waits", [])))
    for c in range(NCH):
        ev_mm = P.mm(lambda E, c=c: E.matmul(
            sb["pst2"][0:1, 0:TT], sb["onesb"][:, :], ccol(sb["sq"], c, TT),
            start=(c == 0), stop=(c == NCH - 1)),
            inc=(c == NCH - 1))
    rows = sb["rows"]
    mu = rows[0:1, 0:TT]
    musq = rows[0:1, TT:2 * TT]
    var = rows[0:1, 2 * TT:3 * TT]
    P.op("vector", lambda E: E.tensor_scalar(
        mu, sb["pst1"][0:1, 0:TT], 1.0 / D, None, op0=ALU.mult),
        waits=[("tensor", ev_mm)])
    P.op("vector", lambda E: E.tensor_mul(musq, mu, mu))
    ev_var = P.op("vector", lambda E: E.scalar_tensor_tensor(
        var, sb["pst2"][0:1, 0:TT], 1.0 / D, musq, op0=ALU.mult, op1=ALU.subtract))
    lnv = rows[0:1, 3 * TT:4 * TT]
    rstd = rows[0:1, 4 * TT:5 * TT]
    ev1 = P.op("scalar", lambda E: E.activation(lnv, var, AF.Ln, bias=eps_ap),
               waits=[("vector", ev_var)] + list(sb.get("const_waits", [])))
    ev_rstd = P.op("scalar", lambda E: E.activation(rstd, lnv, AF.Exp, scale=-0.5))
    out = {"mu": mu, "var": var, "rstd": rstd, "ev_rstd": ev_rstd,
           "ev_var": ev_var, "ev_stats_mm": ev_mm, "ev_rows_v": ev_var}
    if extra_rms:
        vr = rows[0:1, 5 * TT:6 * TT]
        rr = rows[0:1, 6 * TT:7 * TT]
        P.op("vector", lambda E: E.tensor_mul(vr, var, rstd),
             waits=[("scalar", ev_rstd)])
        ev3 = P.op("vector", lambda E: E.tensor_mul(vr, vr, rstd))
        ev4 = P.op("scalar", lambda E: E.activation(vr, vr, AF.Ln, bias=sb["eps_rms"]),
                   waits=[("vector", ev3)])
        P.op("scalar", lambda E: E.activation(vr, vr, AF.Exp, scale=-0.5))
        ev_rr = P.op("vector", lambda E: E.tensor_mul(rr, rstd, vr),
                     waits=[("scalar", P.ev("scalar"))])
        out["rr"] = rr
        out["ev_rr"] = ev_rr
    return out


def emit_bcast(P, ps_dst, row_ap, sb, TT, waits):
    return P.mm(lambda E: E.matmul(ps_dst[:, 0:TT], sb["ones_rowf"][:, :],
                                   row_ap, start=True, stop=True),
                waits=waits, inc=True)



def alloc_s1(nc, es, T, sfx=""):
    H2 = min(1024, T // 2)
    TT = min(512, H2)
    sb = {"H2": H2, "TT": TT, "NT": H2 // TT, "NG": T // H2}
    sb["tb"] = es.enter_context(nc.sbuf_tensor("s1_tb" + sfx, [128, NCH * H2], BF16))
    sb["accs"] = es.enter_context(nc.sbuf_tensor("s1_acc" + sfx, [128, NCH * H2], F32))
    sb["selrows"] = es.enter_context(nc.sbuf_tensor("s1_selrows" + sfx, [8, 1024], F32))
    sb["ctpack"] = es.enter_context(nc.sbuf_tensor("s1_ctpack" + sfx, [8, H2], F32))
    sb["w1sb"] = es.enter_context(nc.sbuf_tensor("s1_w1" + sfx, [128, NCH * 2048], BF16))
    sb["w2sb"] = es.enter_context(nc.sbuf_tensor("s1_w2" + sfx, [128, 16 * 1024], BF16))
    sb["b1sb"] = es.enter_context(nc.sbuf_tensor("s1_b1" + sfx, [128, 128], F32))
    sb["b2sb"] = es.enter_context(nc.sbuf_tensor("s1_b2" + sfx, [8, 1024], F32))
    sb["nwsb"] = es.enter_context(nc.sbuf_tensor("s1_nw" + sfx, [128, NCH], F32))
    sb["gwsb"] = es.enter_context(nc.sbuf_tensor("s1_gw" + sfx, [128, 64], BF16))
    sb["xt"] = es.enter_context(nc.sbuf_tensor("s1_xt" + sfx, [128, NCH * TT], F32))
    sb["xq"] = es.enter_context(nc.sbuf_tensor("s1_xq" + sfx, [128, NCH * TT], BF16))
    sb["sq"] = es.enter_context(nc.sbuf_tensor("s1_sq" + sfx, [128, NCH * TT], BF16))
    sb["sc"] = es.enter_context(nc.sbuf_tensor("s1_sc" + sfx, [128, TT], F32))
    sb["uc"] = es.enter_context(nc.sbuf_tensor("s1_uc" + sfx, [128, TT], F32))
    sb["ot"] = es.enter_context(nc.sbuf_tensor("s1_ot" + sfx, [128, TT], F32))
    sb["cbc"] = es.enter_context(nc.sbuf_tensor("s1_cbc" + sfx, [128, TT], F32))
    sb["he"] = es.enter_context(nc.sbuf_tensor("s1_he" + sfx, [128, 16 * TT], BF16))
    sb["rows"] = es.enter_context(nc.sbuf_tensor("s1_rows" + sfx, [1, 8 * TT], F32))
    sb["ctm"] = es.enter_context(nc.sbuf_tensor("s1_ctm" + sfx, [128, 8], F32))
    sb["gsc"] = es.enter_context(nc.sbuf_tensor("s1_gsc" + sfx, [128, 16], F32))
    for nm in ("pst1", "pst2", "pbc1", "pbc2", "ph0", "ph1", "po0", "po1"):
        sb[nm] = es.enter_context(nc.psum_tensor("s1_" + sfx + nm, [128, 512], F32))
    return sb


def emit_s1(P, sb, cm, io, T, tag=""):
    H2, TT, NT, NG = sb["H2"], sb["TT"], sb["NT"], sb["NG"]
    sb["onesb"] = cm["ones_colb"]
    sb["ones_rowf"] = cm["ones_rowf"]
    xin3 = fm_view(io["xin"])
    xout3 = fm_view(io["xout"])

    P.dma("sync", "c_const", sb["b1sb"][:, :], io["b1s"])
    P.dma("sync", "c_const", sb["b2sb"][:, :], io["b2s"])
    P.dma("sync", "c_const", sb["nwsb"][:, :], io["nws"])
    P.dma("sync", "c_const", sb["selrows"][:, :], io["selrows"])
    P.dma("gpsimd", "c_cg", sb["gwsb"][:, :].rearrange("p (c e) -> p c e", c=NCH),
          io["gwt"].rearrange("p (c e) -> p c e", c=NCH))
    cw = [("c_const", P.ev("c_const")), ("c_cg", P.ev("c_cg"))]
    sb["const_waits"] = cw

    def load_w1(e, waits):
        return P.dma("gpsimd", "s1_w1",
                     sb["w1sb"][:, :].rearrange("p (c m) -> p c m", c=NCH),
                     io["w1t"][e].rearrange("(c p) m -> p c m", p=128), waits=waits)

    def load_w2(e, waits):
        return P.dma("gpsimd", "s1_w2",
                     sb["w2sb"][:, :].rearrange("p (c m) -> p c m", c=16),
                     io["w2t"][e].rearrange("(c p) m -> p c m", p=128), waits=waits)

    ev_w1, ev_w2 = load_w1(0, []), load_w2(0, [])

    st = {"last_stats_mm": 0, "last_rows_v": 0, "xt_free_v": 0, "xt_free_s": 0,
          "xt_free_dma": 0, "rows_free_t": 0}

    for h in range(NG):
        half_off = h * H2
        ev_accz = P.op("gpsimd", lambda E: E.memset(sb["accs"][:, :], 0.0),
                       waits=[("vector", st.get("accs_read_v", 0))])
        evA_tb = 0
        evA_ct = 0
        ev_x_tiles = []
        for i in range(NT):
            toff = half_off + i * TT
            ev_x = P.dma("sync", "s1_x",
                         sb["xt"][:, :].rearrange("p (c t) -> p c t", c=NCH),
                         xin3[:, :, toff:toff + TT],
                         waits=[("vector", st["xt_free_v"]), ("scalar", st["xt_free_s"]),
                                ("s1_xo", st.get("xout_done", 0))])
            ev_x_tiles.append(ev_x)
            sb["stat_war"] = [("vector", st["last_rows_v"])]
            pre = [("s1_x", ev_x), ("tensor", st["last_stats_mm"]),
                   ("tensor", st["rows_free_t"])]
            si = emit_ln_stats(P, sb, sb["xt"], TT, pre, sb["eps_ln"], extra_rms=True)
            st["last_stats_mm"] = si["ev_stats_mm"]
            st["last_rows_v"] = si["ev_var"]
            ev_b1 = emit_bcast(P, sb["pbc1"], si["mu"], sb, TT,
                               waits=[("vector", si["ev_var"]),
                                      ("vector", st.get("pbc1_free", 0))] + cw)
            ev_b2 = emit_bcast(P, sb["pbc2"], si["rr"], sb, TT,
                               waits=[("vector", si["ev_rr"]),
                                      ("vector", st.get("pbc2_free", 0))])
            st["rows_free_t"] = ev_b2
            P.op("vector", lambda E: E.tensor_tensor(
                v8(sb["xt"], NCH, TT), v8(sb["xt"], NCH, TT),
                bc8(sb["pbc1"][:, 0:TT], NCH, TT), op=ALU.subtract),
                waits=[("tensor", ev_b2)])
            for c in range(NCH):
                ev_tb = P.op("vector", lambda E, c=c, o=c * H2 + i * TT: E.scalar_tensor_tensor(
                    sb["tb"][:, o: o + TT],
                    ccol(sb["xt"], c, TT), sb["nwsb"][:, c:c + 1], sb["pbc2"][:, 0:TT],
                    op0=ALU.mult, op1=ALU.mult),
                    waits=[("tensor", st.get("tb_free", 0))])
            st["pbc1_free"] = ev_tb
            st["pbc2_free"] = ev_tb
            st["xt_free_v"] = ev_tb
            st["xt_free_s"] = P.ev("scalar")
            evA_tb = ev_tb
            for s_ in range(TT // 128):
                suba = i * TT + s_ * 128
                lg = sb["po0"][:, s_ * 8:(s_ + 1) * 8]
                for c in range(NCH):
                    ev_lg = P.mm(lambda E, c=c, s_=s_, lg=lg, suba=suba: E.matmul(
                        lg, sb["tb"][:, c * H2 + suba: c * H2 + suba + 128],
                        sb["gwsb"][:, c * 8:(c + 1) * 8],
                        start=(c == 0), stop=(c == NCH - 1)),
                        waits=[("vector", ev_tb),
                               ("vector", st.get("po0_free", 0)),
                               ("scalar", st.get("po0_free_s", 0))] + cw,
                        inc=(c == NCH - 1))
                g = sb["gsc"]
                negm1, m1, m2, dm, em, den = (g[:, 0:1], g[:, 1:2], g[:, 2:3],
                                              g[:, 3:4], g[:, 4:5], g[:, 5:6])
                ge2 = sb["ctm"]
                P.op("vector", lambda E, lg=lg: E.reduce_max(
                    negm1, lg, axis=AX.X, negate=True), waits=[("tensor", ev_lg)])
                P.op("vector", lambda E, lg=lg: E.reduce_max(m1, lg, axis=AX.X))
                P.op("vector", lambda E, lg=lg: E.tensor_scalar(
                    sb["ctm"][:, :], lg, m1, None, op0=ALU.is_ge))
                P.op("vector", lambda E, lg=lg: E.scalar_tensor_tensor(
                    sb["ctm"][:, :], sb["ctm"][:, :], NEG_BIG, lg,
                    op0=ALU.mult, op1=ALU.add))
                P.op("vector", lambda E: E.reduce_max(m2, sb["ctm"][:, :], axis=AX.X))
                P.op("vector", lambda E: E.tensor_sub(dm, m2, m1))
                ev_v = P.op("vector", lambda E, lg=lg: E.tensor_scalar(
                    sb["ctm"][:, :], lg, m2, None, op0=ALU.is_ge))
                ev_em = P.op("scalar", lambda E: E.activation(em, dm, AF.Exp),
                             waits=[("vector", ev_v)])
                ev_num = P.op("scalar", lambda E, lg=lg, s_=s_: E.activation(
                    sb["gsc"][:, 8:16], lg, AF.Exp, bias=negm1))
                P.op("vector", lambda E: E.tensor_scalar(
                    den, em, 1.0, None, op0=ALU.add), waits=[("scalar", ev_em)])
                P.op("vector", lambda E: E.reciprocal(den, den))
                P.op("vector", lambda E: E.tensor_mul(
                    sb["ctm"][:, :], sb["ctm"][:, :], sb["gsc"][:, 8:16]),
                    waits=[("scalar", ev_num)])
                ev_ctm = P.op("vector", lambda E: E.tensor_scalar(
                    sb["ctm"][:, :], sb["ctm"][:, :], den, None, op0=ALU.mult))
                st["po0_free"] = ev_ctm
                st["po0_free_s"] = ev_num
                ev_tp = P.mm(lambda E: E.transpose(
                    sb["po1"][0:8, 0:128], sb["ctm"][:, :], cm["identf"][:, :]),
                    waits=[("vector", ev_ctm),
                           ("vector", st.get("po1_free", 0))] + cw, inc=True)
                ev_cp = P.op("vector", lambda E, suba=suba: E.tensor_copy(
                    sb["ctpack"][:, suba:suba + 128], sb["po1"][0:8, 0:128]),
                    waits=[("tensor", ev_tp), ("tensor", st.get("ctpack_free", 0))])
                st["po1_free"] = ev_cp
                evA_ct = ev_cp
        ph_evac = {}
        po_cons = {}
        for e in range(8):
            ev_w1f = None
            ev_w2f = None
            for i in range(NT):
                toff_h = i * TT
                ev_cb = P.mm(lambda E, e=e, toff_h=toff_h: E.matmul(
                    sb["pbc2"][:, 0:TT], sb["selrows"][:, e * 128:(e + 1) * 128],
                    sb["ctpack"][:, toff_h:toff_h + TT],
                    start=True, stop=True),
                    waits=[("vector", evA_ct), ("vector", st.get("pbc2_free", 0))] + cw,
                    inc=True)
                ev_cbsb = P.op("vector", lambda E: E.tensor_copy(
                    sb["cbc"][:, 0:TT], sb["pbc2"][:, 0:TT]),
                    waits=[("tensor", ev_cb)])
                st["pbc2_free"] = ev_cbsb
                ev_he_last = None
                for hc in range(16):
                    bank = sb["ph0"] if hc % 2 == 0 else sb["ph1"]
                    gidx = ((h * 8 + e) * NT + i) * 16 + hc
                    w_first = [("s1_w1", ev_w1), ("vector", evA_tb),
                               ("scalar", ph_evac.get(gidx - 2, 0))]
                    for c in range(NCH):
                        ev_hmm = P.mm(lambda E, hc=hc, c=c, bank=bank,
                                      toff_h=toff_h: E.matmul(
                            bank[:, 0:TT],
                            sb["w1sb"][:, c * 2048 + hc * 128: c * 2048 + hc * 128 + 128],
                            sb["tb"][:, c * H2 + toff_h: c * H2 + toff_h + TT],
                            start=(c == 0), stop=(c == NCH - 1)),
                            waits=(w_first if c == 0 else []),
                            inc=(c == NCH - 1))
                    ev_he_last = P.op("scalar", lambda E, hc=hc, bank=bank, e=e:
                                      E.activation(
                        ccol(sb["he"], hc, TT), bank[:, 0:TT], AF.Gelu,
                        bias=sb["b1sb"][:, e * 16 + hc: e * 16 + hc + 1]),
                        waits=[("tensor", ev_hmm),
                               ("tensor", st.get("he_free", 0))])
                    ph_evac[gidx] = ev_he_last
                    if i == NT - 1 and hc == 15:
                        ev_w1f = ev_hmm
                for c in range(NCH):
                    bidx = c % 2
                    bank = sb["po0"] if bidx == 0 else sb["po1"]
                    o_first = [("s1_w2", ev_w2), ("scalar", ev_he_last),
                               ("vector", po_cons.get(bidx, 0)),
                               ("scalar", st.get("po_free_s", 0))]
                    for hc in range(16):
                        ev_omm = P.mm(lambda E, c=c, hc=hc, bank=bank: E.matmul(
                            bank[:, 0:TT],
                            sb["w2sb"][:, hc * 1024 + c * 128: hc * 1024 + c * 128 + 128],
                            ccol(sb["he"], hc, TT),
                            start=(hc == 0), stop=(hc == 15)),
                            waits=(o_first if hc == 0 else []),
                            inc=(hc == 15))
                    P.op("vector", lambda E, bank=bank: E.tensor_mul(
                        sb["ot"][:, 0:TT], bank[:, 0:TT], sb["cbc"][:, 0:TT]),
                        waits=[("tensor", ev_omm), ("vector", ev_cbsb),
                               ("gpsimd", ev_accz)])
                    ev_acc = P.op("vector", lambda E, c=c, toff_h=toff_h: E.tensor_add(
                        sb["accs"][:, c * H2 + toff_h: c * H2 + toff_h + TT],
                        sb["accs"][:, c * H2 + toff_h: c * H2 + toff_h + TT],
                        sb["ot"][:, 0:TT]))
                    po_cons[bidx] = ev_acc
                    if i == NT - 1 and c == NCH - 1:
                        ev_w2f = ev_omm
                st["he_free"] = ev_omm
            ne = e + 1 if e < 7 else (0 if h < NG - 1 else None)
            if ne is not None:
                ev_w1 = load_w1(ne, [("tensor", ev_w1f)])
                ev_w2 = load_w2(ne, [("tensor", ev_w2f)])
        st["accs_write_done"] = po_cons.get(1, po_cons.get(0, 0))
        st["po0_free"] = po_cons.get(0, 0)
        st["po1_free"] = po_cons.get(1, 0)
        st["po0_free_s"] = 0
        st["ph_bank_free_s"] = max(ph_evac.values()) if ph_evac else 0
        for i in range(NT):
            toff = half_off + i * TT
            ev_x = P.dma("sync", "s1_x",
                         sb["xt"][:, :].rearrange("p (c t) -> p c t", c=NCH),
                         xin3[:, :, toff:toff + TT],
                         waits=[("vector", st["xt_free_v"]), ("scalar", st["xt_free_s"]),
                                ("s1_xo", st.get("xout_done", 0))])
            sb["stat_war"] = [("vector", st["last_rows_v"])]
            pre = [("s1_x", ev_x), ("tensor", st["last_stats_mm"]),
                   ("tensor", st["rows_free_t"])]
            si = emit_ln_stats(P, sb, sb["xt"], TT, pre, sb["eps_ln"], extra_rms=False)
            st["last_stats_mm"] = si["ev_stats_mm"]
            st["last_rows_v"] = si["ev_var"]
            prow = sb["rows"][0:1, 5 * TT:6 * TT]
            qrow = sb["rows"][0:1, 6 * TT:7 * TT]
            P.op("vector", lambda E: E.tensor_scalar(
                prow, si["rstd"], 0.5, 1.0, op0=ALU.mult, op1=ALU.add),
                waits=[("scalar", si["ev_rstd"])])
            evq = P.op("vector", lambda E: E.tensor_mul(qrow, si["mu"], si["rstd"]))
            evq = P.op("vector", lambda E: E.tensor_scalar(
                qrow, qrow, -0.5, None, op0=ALU.mult))
            ev_b1 = emit_bcast(P, sb["pbc1"], prow, sb, TT,
                               waits=[("vector", evq),
                                      ("vector", st.get("pbc1_free", 0))])
            ev_b2 = emit_bcast(P, sb["pbc2"], qrow, sb, TT,
                               waits=[("vector", st.get("pbc2_free", 0))])
            st["rows_free_t"] = ev_b2
            accv = sb["accs"][:, i * TT: i * TT + TT].rearrange(
                "p t -> p 1 t").broadcast_to([128, 1, TT]) if False else None
            acc8 = sb["accs"][:, :].rearrange("p (c g) -> p c g", c=NCH)[
                :, :, i * TT:(i + 1) * TT]
            P.op("vector", lambda E: E.tensor_tensor(
                v8(sb["xt"], NCH, TT), v8(sb["xt"], NCH, TT),
                bc8(sb["pbc1"][:, 0:TT], NCH, TT), op=ALU.mult),
                waits=[("tensor", ev_b2), ("vector", st["accs_write_done"])])
            P.op("vector", lambda E, acc8=acc8: E.scalar_tensor_tensor(
                v8(sb["xt"], NCH, TT), acc8, 0.5, v8(sb["xt"], NCH, TT),
                op0=ALU.mult, op1=ALU.add))
            ev_base = P.op("vector", lambda E: E.tensor_tensor(
                v8(sb["xt"], NCH, TT), v8(sb["xt"], NCH, TT),
                bc8(sb["pbc2"][:, 0:TT], NCH, TT), op=ALU.add))
            ev_xn = None
            for c in range(NCH):
                pbias = sb["pst1"] if c % 2 == 0 else sb["pst2"]
                ev_bias = P.mm(lambda E, c=c, i=i, pbias=pbias: E.matmul(
                    pbias[:, 0:TT], sb["b2sb"][:, c * 128:(c + 1) * 128],
                    sb["ctpack"][:, i * TT:(i + 1) * TT],
                    start=True, stop=True),
                    waits=[("vector", evA_ct), ("vector", si["ev_var"]),
                           ("vector", st.get("pbias_free_%d" % (c % 2), 0))],
                    inc=True)
                ev_xn = P.op("vector", lambda E, c=c, pbias=pbias: E.scalar_tensor_tensor(
                    ccol(sb["xt"], c, TT), pbias[:, 0:TT], 0.5, ccol(sb["xt"], c, TT),
                    op0=ALU.mult, op1=ALU.add),
                    waits=[("tensor", ev_bias)])
                st["pbias_free_%d" % (c % 2)] = ev_xn
            st["pbc1_free"] = ev_xn
            st["pbc2_free"] = ev_xn
            st["accs_read_v"] = ev_xn
            st["last_rows_v"] = ev_xn
            st["xt_free_v"] = ev_xn
            st["xt_free_s"] = P.ev("scalar")
            ev_store = P.dma("sync", "s1_xo", xout3[:, :, toff:toff + TT],
                             sb["xt"][:, :].rearrange("p (c t) -> p c t", c=NCH),
                             waits=[("vector", ev_xn)])
            st["xout_done"] = ev_store
            st["xt_free_dma"] = ev_store
        st["ctpack_free"] = P.ev("tensor")
        st["tb_free"] = P.ev("tensor")
        st["prev_w2_last"] = st.get("w2_free", 0)
    return st



def alloc_pp(nc, es, T, sfx=""):
    sb = {}
    for p in range(2):
        sb[f"xtm{p}"] = es.enter_context(nc.sbuf_tensor(f"pp_xtm{p}" + sfx, [128, D], F32))
        sb[f"xfm{p}"] = es.enter_context(nc.sbuf_tensor(f"pp_xfm{p}" + sfx, [128, D], F32))
    for b in range(4):
        sb[f"pt{b}"] = es.enter_context(nc.psum_tensor(f"pp_pt{b}" + sfx, [128, 512], F32))
    return sb


def emit_pre(P, sb, cm, io, T, reverse=False):
    nt = T // 128
    ld = {}
    tp_done = {0: 0, 1: 0}
    cp_done = {0: 0, 1: 0}
    st_done = {0: 0, 1: 0}

    def load(jj):
        par = jj % 2
        xtm = sb[f"xtm{par}"]
        if reverse:
            srcv = fm_view(io["xin"])[:, :, jj * 128:(jj + 1) * 128]
            dstv = xtm[:, :].rearrange("p (c t) -> p c t", c=NCH)
        else:
            srcv = io["xin"][jj * 128:(jj + 1) * 128, :]
            dstv = xtm[:, :]
        ld[jj] = P.dma("sync", f"pp_x{par}", dstv, srcv,
                       waits=[("tensor", tp_done[par])])

    load(0)
    load(1)
    for jj in range(nt):
        par = jj % 2
        xtm, xfm = sb[f"xtm{par}"], sb[f"xfm{par}"]
        banks = (sb["pt0"], sb["pt1"]) if par == 0 else (sb["pt2"], sb["pt3"])
        evs = []
        ev_t4 = {}
        for half in range(2):
            bank = banks[half]
            ev_t = None
            for q in range(4):
                c = half * 4 + q
                ev_t = P.mm(lambda E, c=c, bank=bank, q=q, xtm=xtm: E.transpose(
                    bank[:, q * 128:q * 128 + 128], ccol(xtm, c, 128),
                    cm["identf"][:, :]),
                    waits=([(f"pp_x{par}", ld[jj]), ("vector", cp_done[par])]
                           + cm["const_waits"] if q == 0 else []),
                    inc=(q == 3))
            ev_t4[half] = ev_t
            evs.append(P.op("vector", lambda E, half=half, bank=bank, xfm=xfm:
                            E.tensor_copy(
                xfm[:, half * 512:half * 512 + 512], bank[:, 0:512]),
                waits=[("tensor", ev_t), (f"pp_o{par}", st_done[par])]))
        tp_done[par] = P.ev("tensor")
        cp_done[par] = evs[-1]
        if jj + 2 < nt:
            load(jj + 2)
        if reverse:
            st_done[par] = P.dma("sync", f"pp_o{par}",
                                 io["xout"][jj * 128:(jj + 1) * 128, :],
                                 xfm[:, :], waits=[("vector", evs[-1])])
        else:
            st_done[par] = P.dma("sync", f"pp_o{par}",
                                 fm_view(io["xout"])[:, :, jj * 128:(jj + 1) * 128],
                                 xfm[:, :].rearrange("p (c t) -> p c t", c=NCH),
                                 waits=[("vector", evs[-1])])
    return st_done[(nt - 1) % 2]


def alloc_common(nc, es):
    cm = {}
    cm["identf"] = es.enter_context(nc.sbuf_tensor("cm_identf", [128, 128], F32))
    cm["ones_rowf"] = es.enter_context(nc.sbuf_tensor("cm_onesrow", [1, 128], F32))
    cm["ones_colb"] = es.enter_context(nc.sbuf_tensor("cm_onescol", [128, 1], BF16))
    cm["fconsts"] = es.enter_context(nc.sbuf_tensor("cm_fconsts", [1, 8], F32))
    cm["identb"] = es.enter_context(nc.sbuf_tensor("cm_identb", [128, 128], BF16))
    return cm



def build_launch(decls, emit, t_core, debug=False):
    nc = bass.Bass("TRN2", target_bir_lowering=False, debug=debug)
    aps = {}
    for name, (shape, dt_, kind) in decls.items():
        h = nc.dram_tensor(name, list(shape), dt_, kind=kind)
        aps[name] = h.ap()
    es = ExitStack()
    P = Prog(nc)
    try:
        cm = alloc_common(nc, es)
        with nc.Block() as blk:
            P.dma("sync", "c_const", cm["identf"][:, :], aps["identf"])
            P.dma("sync", "c_const", cm["ones_rowf"][:, :], aps["ones_row"])
            P.dma("gpsimd", "c_cg", cm["ones_colb"][:, :], aps["ones_col"])
            P.dma("gpsimd", "c_cg", cm["identb"][:, :], aps["identf"])
            P.dma("sync", "c_const", cm["fconsts"][:, :], aps["fconsts"])
            cm["const_waits"] = [("c_const", P.ev("c_const")),
                                 ("c_cg", P.ev("c_cg"))]
            emit(P, cm, aps, es, nc)
            P.barrier_all()
            P.finalize(blk)
    finally:
        es.close()
        P.close()
    return nc


CONST_DECL = {
    "identf": ((128, 128), F32, "ExternalInput"),
    "ones_row": ((1, 128), F32, "ExternalInput"),
    "ones_col": ((128, 1), F32, "ExternalInput"),
    "fconsts": ((1, 8), F32, "ExternalInput"),
}


def build_pre(t_core, reverse=False):
    decls = dict(CONST_DECL)
    if reverse:
        decls["xin"] = ((D, t_core), F32, "ExternalInput")
        decls["xout"] = ((t_core, D), F32, "ExternalOutput")
    else:
        decls["xin"] = ((t_core, D), F32, "ExternalInput")
        decls["xout"] = ((D, t_core), F32, "ExternalOutput")

    def emit(P, cm, aps, es, nc):
        sb = alloc_pp(nc, es, t_core)
        emit_pre(P, sb, cm, aps, t_core, reverse=reverse)
    return build_launch(decls, emit, t_core)


def build_s1(t_core):
    decls = dict(CONST_DECL)
    decls.update({
        "xin": ((D, t_core), F32, "ExternalInput"),
        "xout": ((D, t_core), F32, "ExternalOutput"),
        "w1t": ((8, D, 2048), F32, "ExternalInput"),
        "w2t": ((8, 2048, D), F32, "ExternalInput"),
        "b1s": ((128, 128), F32, "ExternalInput"),
        "b2s": ((8, D), F32, "ExternalInput"),
        "nws": ((128, NCH), F32, "ExternalInput"),
        "gwt": ((128, 64), F32, "ExternalInput"),
        "selrows": ((8, 1024), F32, "ExternalInput"),
    })

    def emit(P, cm, aps, es, nc):
        sb = alloc_s1(nc, es, t_core)
        sb["const_waits"] = cm["const_waits"]
        sb["eps_ln"] = cm["fconsts"][0:1, 0:1]
        sb["eps_rms"] = cm["fconsts"][0:1, 1:2]
        emit_s1(P, sb, cm, aps, t_core)
    return build_launch(decls, emit, t_core)



def alloc_s2(nc, es, T, sfx=""):
    sb = {"T": T}
    TT = min(512, T)
    sb["TT"] = TT
    sb["xt0"] = es.enter_context(nc.sbuf_tensor("s2_xt0" + sfx, [128, NCH * TT], F32))
    sb["xt1"] = es.enter_context(nc.sbuf_tensor("s2_xt1" + sfx, [128, NCH * TT], F32))
    sb["xq"] = es.enter_context(nc.sbuf_tensor("s2_xq" + sfx, [128, NCH * TT], BF16))
    sb["sq"] = es.enter_context(nc.sbuf_tensor("s2_sq" + sfx, [128, NCH * TT], BF16))
    sb["u2f"] = es.enter_context(nc.sbuf_tensor("s2_u2f" + sfx, [128, NCH * TT], F32))
    sb["ztm"] = es.enter_context(nc.sbuf_tensor("s2_ztm" + sfx, [128, 4 * D], BF16))
    sb["ssb"] = es.enter_context(nc.sbuf_tensor("s2_ssb" + sfx, [128, NCH * TT], BF16))
    sb["ysb"] = es.enter_context(nc.sbuf_tensor("s2_ysb" + sfx, [128, NCH * TT], F32))
    sb["sc"] = es.enter_context(nc.sbuf_tensor("s2_sc" + sfx, [128, TT], F32))
    sb["uc"] = es.enter_context(nc.sbuf_tensor("s2_uc" + sfx, [128, TT], F32))
    sb["rows"] = es.enter_context(nc.sbuf_tensor("s2_rows" + sfx, [1, 7 * TT], F32))
    sb["wl0"] = es.enter_context(nc.sbuf_tensor("s2_wl0" + sfx, [128, NCH * D], BF16))
    sb["l0b"] = es.enter_context(nc.sbuf_tensor("s2_l0b" + sfx, [128, NCH], F32))
    sb["l0g"] = es.enter_context(nc.sbuf_tensor("s2_l0g" + sfx, [128, NCH], F32))
    sb["l0bt"] = es.enter_context(nc.sbuf_tensor("s2_l0bt" + sfx, [128, NCH], F32))
    sb["m1tri"] = es.enter_context(nc.sbuf_tensor("s2_m1tri" + sfx, [128, 128], F32))
    sb["tri01"] = es.enter_context(nc.sbuf_tensor("s2_tri01" + sfx, [128, 128], F32))
    sb["iota"] = es.enter_context(nc.sbuf_tensor("s2_iota" + sfx, [1, T], F32))
    sb["avec"] = es.enter_context(nc.sbuf_tensor("s2_avec" + sfx, [1, T], F32))
    sb["amatb"] = es.enter_context(nc.sbuf_tensor("s2_amatb" + sfx, [128, 128], BF16))
    sb["avbc"] = es.enter_context(nc.sbuf_tensor("s2_avbc" + sfx, [128, 128], F32))
    sb["agt"] = es.enter_context(nc.sbuf_tensor("s2_agt" + sfx, [128, TT], F32))
    sb["scal"] = es.enter_context(nc.sbuf_tensor("s2_scal" + sfx, [1, 8], F32))
    sb["lnac"] = es.enter_context(nc.sbuf_tensor("s2_lnac" + sfx, [128, 1], F32))
    sb["a128c"] = es.enter_context(nc.sbuf_tensor("s2_a128c" + sfx, [128, 1], F32))
    sb["car0"] = es.enter_context(nc.sbuf_tensor("s2_car0" + sfx, [128, NCH], F32))
    sb["car1"] = es.enter_context(nc.sbuf_tensor("s2_car1" + sfx, [128, NCH], F32))
    sb["cin"] = es.enter_context(nc.sbuf_tensor("s2_cin" + sfx, [128, NCH], F32))
    sb["cinb"] = es.enter_context(nc.sbuf_tensor("s2_cinb" + sfx, [128, NCH], BF16))
    sb["wssb"] = es.enter_context(nc.sbuf_tensor("s2_wssb" + sfx, [128, NCH], F32))
    sb["amatf"] = es.enter_context(nc.sbuf_tensor("s2_amatf" + sfx, [128, 128], F32))
    for nm in ("pst1", "pst2", "pbc1", "pbc2", "pS0", "pS1", "pY0", "pY1"):
        sb[nm] = es.enter_context(nc.psum_tensor("s2_" + nm + sfx, [128, 512], F32))
    return sb


def emit_s2(P, sb, cm, io, T, carry_only=False, cin_ev=None):
    TT = sb["TT"]
    NT2 = T // TT
    NSB = TT // 128
    sb["onesb"] = cm["ones_colb"]
    sb["ones_rowf"] = cm["ones_rowf"]
    xin3 = fm_view(io["xin"])
    xout3 = fm_view(io["xout"])
    P.dma("sync", "c_const", sb["l0b"][:, :], io["l0b"])
    P.dma("sync", "c_const", sb["l0g"][:, :], io["l0g"])
    P.dma("sync", "c_const", sb["l0bt"][:, :], io["l0bt"])
    P.dma("sync", "c_const", sb["m1tri"][:, :], io["m1tri"])
    P.dma("sync", "c_const", sb["tri01"][:, :], io["tri01"])
    P.dma("sync", "c_const", sb["iota"][:, :], io["iota_row"][0:1, 0:T])
    P.dma("sync", "c_const", sb["scal"][0:1, 0:1], io["decay"])
    if "carry_in" in io:
        P.dma("sync", "c_const", sb["cin"][:, :], io["carry_in"])
    P.dma("gpsimd", "c_cg", sb["wl0"][:, :].rearrange("p (c e) -> p c e", c=NCH),
          io["l0wt"].rearrange("(c p) e -> p c e", p=128))
    cw = [("c_const", P.ev("c_const")), ("c_cg", P.ev("c_cg"))]
    sb["const_waits"] = cw

    a11 = sb["scal"][0:1, 1:2]
    lna = sb["scal"][0:1, 2:3]
    P.op("scalar", lambda E: E.activation(a11, sb["scal"][0:1, 0:1], AF.Sigmoid),
         waits=cw)
    ev_lna = P.op("scalar", lambda E: E.activation(lna, a11, AF.Ln))
    ev_bl = P.mm(lambda E: E.matmul(sb["pbc1"][:, 0:1], cm["ones_rowf"][:, :], lna,
                                    start=True, stop=True),
                 waits=[("scalar", ev_lna)] + cw, inc=True)
    ev_lnac = P.op("vector", lambda E: E.tensor_copy(sb["lnac"][:, :],
                                                     sb["pbc1"][:, 0:1]),
                   waits=[("tensor", ev_bl)])
    P.op("vector", lambda E: E.tensor_scalar(
        sb["amatf"][:, :], sb["m1tri"][:, :], sb["lnac"][:, 0:1], None, op0=ALU.mult))
    ev_am = P.op("scalar", lambda E: E.activation(sb["amatf"][:, :], sb["amatf"][:, :],
                                                  AF.Exp),
                 waits=[("vector", P.ev("vector"))])
    ev_amb = P.op("vector", lambda E: E.tensor_mul(sb["amatb"][:, :], sb["amatf"][:, :],
                                                   sb["tri01"][:, :]),
                  waits=[("scalar", ev_am)])
    P.op("vector", lambda E: E.tensor_scalar(
        sb["avec"][:, :], sb["iota"][:, :], lna, None, op0=ALU.mult),
        waits=[("scalar", ev_lna)])
    ev_av = P.op("scalar", lambda E: E.activation(sb["avec"][:, :], sb["avec"][:, :],
                                                  AF.Exp),
                 waits=[("vector", P.ev("vector"))])
    ev_b1 = P.mm(lambda E: E.matmul(sb["pbc1"][:, 0:128], cm["ones_rowf"][:, :],
                                    sb["avec"][0:1, 0:128], start=True, stop=True),
                 waits=[("scalar", ev_av), ("vector", ev_lnac)], inc=True)
    ev_avbc = P.op("vector", lambda E: E.tensor_copy(sb["avbc"][:, :],
                                                     sb["pbc1"][:, 0:128]),
                   waits=[("tensor", ev_b1)])
    ev_a128 = P.op("vector", lambda E: E.tensor_copy(sb["a128c"][:, :],
                                                     sb["avbc"][:, 127:128]))
    P.op("vector", lambda E: E.memset(sb["car0"][:, :], 0.0))
    P.op("vector", lambda E: E.memset(sb["car1"][:, :], 0.0))
    if carry_only:
        ev_ws = P.ev("vector")
    else:
        ev_ws = P.op("vector", lambda E: E.tensor_copy(sb["cinb"][:, :],
                                                       sb["cin"][:, :]),
                     waits=([("mb_c", cin_ev)] if cin_ev else []))
    ev_cz = ev_ws
    if not carry_only:
        ev_ws_mm = None
        for ec in range(NCH):
            for c in range(NCH):
                ev_ws_mm = P.mm(lambda E, ec=ec, c=c: E.matmul(
                    sb["pS0"][:, ec:ec + 1],
                    sb["wl0"][:, c * D + ec * 128: c * D + ec * 128 + 128],
                    sb["cinb"][:, c:c + 1],
                    start=(c == 0), stop=(c == NCH - 1)),
                    waits=([("vector", ev_cz)] + cw if ec == 0 and c == 0 else []),
                    inc=(ec == NCH - 1 and c == NCH - 1))
        ev_ws = P.op("vector", lambda E: E.tensor_copy(sb["wssb"][:, :],
                                                       sb["pS0"][:, 0:NCH]),
                     waits=[("tensor", ev_ws_mm)])

    st = {"stats_mm": 0, "rows_v": 0, "rows_t": 0,
          "pbc1_free": ("vector", ev_avbc), "pbc2_free": ("vector", 0),
          "pS_free": ("vector", ev_ws), "pY_free": ("vector", 0),
          "x_free": {0: ev_a128, 1: ev_a128},
          "x_free_s": 0, "xo": 0, "ztm_free": 0, "u2f_free_t": 0,
          "ssb_free_t": 0, "ysb_free_s": 0, "xbufs": (sb["xt0"], sb["xt1"])}
    gsb = [0]

    for j in range(NT2):
        toff = j * TT
        par = j % 2
        xt = st["xbufs"][par]
        xch = "s2_x0" if par == 0 else "s2_x1"
        ev_x = P.dma("sync", xch,
                     xt[:, :].rearrange("p (c t) -> p c t", c=NCH),
                     xin3[:, :, toff:toff + TT],
                     waits=[("vector", st["x_free"][par]),
                            ("scalar", st["x_free_s"]), ("s2_xo", st["xo"])])
        sb["stat_war"] = [("vector", st["rows_v"])]
        pre = [(xch, ev_x), ("tensor", st["stats_mm"]), ("tensor", st["rows_t"])]
        si = emit_ln_stats(P, sb, xt, TT, pre, sb["eps_ln"])
        st["stats_mm"] = si["ev_stats_mm"]
        st["rows_v"] = si["ev_var"]
        evb1 = emit_bcast(P, sb["pbc1"], si["mu"], sb, TT,
                          waits=[("vector", si["ev_var"]), st["pbc1_free"]])
        evb2 = emit_bcast(P, sb["pbc2"], si["rstd"], sb, TT,
                          waits=[("scalar", si["ev_rstd"]), st["pbc2_free"]])
        st["rows_t"] = evb2
        P.op("vector", lambda E, xt=xt: E.tensor_tensor(
            v8(sb["u2f"], NCH, TT), v8(xt, NCH, TT),
            bc8(sb["pbc1"][:, 0:TT], NCH, TT), op=ALU.subtract),
            waits=[("tensor", evb2), ("tensor", st["u2f_free_t"])])
        ev_u2 = P.op("vector", lambda E: E.tensor_tensor(
            v8(sb["u2f"], NCH, TT), v8(sb["u2f"], NCH, TT),
            bc8(sb["pbc2"][:, 0:TT], NCH, TT), op=ALU.mult))
        st["pbc1_free"] = ("vector", ev_u2)
        ev_ztm = {}
        for grp in range(NSB * 2):
            s_ = grp // 2
            half = grp % 2
            bank = sb["pbc1"] if grp % 2 == 0 else sb["pbc2"]
            ev_tp = None
            for q in range(4):
                c = half * 4 + q
                ev_tp = P.mm(lambda E, c=c, s_=s_, bank=bank, q=q,
                             xoff=c * TT + s_ * 128: E.transpose(
                    bank[:, q * 128:q * 128 + 128], sb["u2f"][:, xoff:xoff + 128],
                    cm["identf"][:, :]),
                    waits=([("vector", ev_u2),
                            st.get(("tpc", grp % 2), ("vector", 0))]
                           if q == 0 else []),
                    inc=(q == 3))
            evc = P.op("scalar", lambda E, s_=s_, half=half, bank=bank:
                       E.activation(
                sb["ztm"][:, s_ * D + half * 512: s_ * D + half * 512 + 512],
                bank[:, 0:512], AF.Copy),
                waits=[("tensor", ev_tp), ("tensor", st["ztm_free"])])
            st[("tpc", grp % 2)] = ("scalar", evc)
            ev_ztm[s_] = evc
        st["u2f_free_t"] = P.ev("tensor")
        st["pbc1_free"] = ("scalar", P.ev("scalar"))
        st["pbc2_free"] = ("scalar", P.ev("scalar"))
        ev_sevac_last = None
        for s_ in range(NSB):
            banks = (sb["pS0"], sb["pS1"]) if s_ % 2 == 0 else (sb["pY0"], sb["pY1"])
            bfree = "pS_free" if s_ % 2 == 0 else "pY_free"
            ev_sm = None
            for c in range(NCH):
                bank = banks[0] if c < 4 else banks[1]
                ev_sm = P.mm(lambda E, c=c, s_=s_, bank=bank: E.matmul(
                    bank[:, (c % 4) * 128:(c % 4) * 128 + 128],
                    sb["ztm"][:, s_ * D + c * 128: s_ * D + c * 128 + 128],
                    sb["amatb"][:, :],
                    start=True, stop=True),
                    waits=[("scalar", ev_ztm[s_]), ("vector", ev_amb),
                           st[bfree]],
                    inc=(c == NCH - 1))
            g = gsb[0]
            cprev = sb["car0"] if g % 2 == 0 else sb["car1"]
            cnext = sb["car1"] if g % 2 == 0 else sb["car0"]
            gsb[0] += 1
            ev_cu = None
            for half in range(2):
                c0 = half * 4
                bank = banks[half]
                ev_cu = P.op("vector", lambda E, bank=bank, c0=c0, cn=cnext,
                             cp=cprev: E.scalar_tensor_tensor(
                    cn[:, c0:c0 + 4], cp[:, c0:c0 + 4], sb["a128c"][:, 0:1],
                    bank[:, 127:512:128], op0=ALU.mult, op1=ALU.add),
                    waits=[("tensor", ev_sm)])
            if carry_only:
                st[bfree] = ("vector", ev_cu)
                ev_sevac_last = ev_cu
                continue
            for c in range(NCH):
                bank = banks[0] if c < 4 else banks[1]
                col = (c % 4) * 128
                ev_sevac_last = P.op("vector", lambda E, c=c, s_=s_, bank=bank,
                                     col=col, cp=cprev: E.scalar_tensor_tensor(
                    ccol(sb["ssb"], c, TT)[:, s_ * 128:s_ * 128 + 128],
                    sb["avbc"][:, :], cp[:, c:c + 1],
                    bank[:, col:col + 128], op0=ALU.mult, op1=ALU.add),
                    waits=[("tensor", st["ssb_free_t"])])
            st[bfree] = ("vector", ev_sevac_last)
        st["ztm_free"] = P.ev("tensor")
        if carry_only:
            st["x_free"][par] = ev_u2
            st["x_free_s"] = P.ev("scalar")
            continue
        ev_ag = P.mm(lambda E, toff=toff: E.matmul(
            sb["pbc2"][:, 0:TT], cm["ones_rowf"][:, :],
            sb["avec"][0:1, toff:toff + TT], start=True, stop=True),
            waits=[st["pbc2_free"]], inc=True)
        ev_agt = P.op("vector", lambda E: E.tensor_copy(sb["agt"][:, :],
                                                        sb["pbc2"][:, 0:TT]),
                      waits=[("tensor", ev_ag)])
        st["pbc2_free"] = ("vector", ev_agt)
        ev_yev = None
        for ec in range(NCH):
            bank = sb["pS0"] if ec % 2 == 0 else sb["pS1"]
            ev_ym = None
            for c in range(NCH):
                ev_ym = P.mm(lambda E, ec=ec, c=c, bank=bank: E.matmul(
                    bank[:, 0:TT],
                    sb["wl0"][:, c * D + ec * 128: c * D + ec * 128 + 128],
                    ccol(sb["ssb"], c, TT),
                    start=(c == 0), stop=(c == NCH - 1)),
                    waits=([("vector", ev_sevac_last), st["pS_free"],
                            ("vector", st.get(("yf", ec % 2), 0)),
                            ("scalar", st.get(("yfs", ec % 2), 0))]
                           if c == 0 else []),
                    inc=(c == NCH - 1))
            ev_ye1 = P.op("vector", lambda E, ec=ec, bank=bank: E.scalar_tensor_tensor(
                ccol(sb["ysb"], ec, TT), sb["agt"][:, :], sb["wssb"][:, ec:ec + 1],
                bank[:, 0:TT], op0=ALU.mult, op1=ALU.add),
                waits=[("tensor", ev_ym), ("vector", ev_agt), ("vector", ev_ws),
                       ("scalar", st["ysb_free_s"])])
            st[("yf", ec % 2)] = ev_ye1
            ev_yev = P.op("scalar", lambda E, ec=ec: E.activation(
                ccol(sb["ysb"], ec, TT), ccol(sb["ysb"], ec, TT), AF.Identity,
                bias=sb["l0b"][:, ec:ec + 1]),
                waits=[("vector", ev_ye1)])
            st[("yfs", ec % 2)] = ev_yev
        st["pS_free"] = ("scalar", ev_yev)
        st["ssb_free_t"] = P.ev("tensor")
        sb["stat_war"] = [("vector", st["rows_v"])]
        pre2 = [("scalar", ev_yev), ("tensor", st["stats_mm"]),
                ("tensor", st["rows_t"])]
        si2 = emit_ln_stats(P, sb, sb["ysb"], TT, pre2, sb["eps_ln"])
        st["stats_mm"] = si2["ev_stats_mm"]
        st["rows_v"] = si2["ev_var"]
        st["ysb_free_s"] = P.ev("scalar")
        evc1 = emit_bcast(P, sb["pbc1"], si2["mu"], sb, TT,
                          waits=[("vector", si2["ev_var"]),
                                 st["pbc1_free"]])
        evc2 = emit_bcast(P, sb["pbc2"], si2["rstd"], sb, TT,
                          waits=[("scalar", si2["ev_rstd"]),
                                 st["pbc2_free"]])
        st["rows_t"] = evc2
        P.op("vector", lambda E: E.tensor_tensor(
            v8(sb["ysb"], NCH, TT), v8(sb["ysb"], NCH, TT),
            bc8(sb["pbc1"][:, 0:TT], NCH, TT), op=ALU.subtract),
            waits=[("tensor", evc2), ("scalar", P.ev("scalar"))])
        P.op("vector", lambda E: E.tensor_tensor(
            v8(sb["ysb"], NCH, TT), v8(sb["ysb"], NCH, TT),
            bc8(sb["pbc2"][:, 0:TT], NCH, TT), op=ALU.mult))
        ev_xn = None
        for ec in range(NCH):
            P.op("scalar", lambda E, ec=ec: E.activation(
                ccol(sb["ysb"], ec, TT), ccol(sb["ysb"], ec, TT), AF.Identity,
                scale=sb["l0g"][:, ec:ec + 1], bias=sb["l0bt"][:, ec:ec + 1]),
                waits=[("vector", P.ev("vector"))])
            ev_xn = P.op("vector", lambda E, ec=ec, xt=xt: E.tensor_add(
                ccol(xt, ec, TT), ccol(xt, ec, TT), ccol(sb["ysb"], ec, TT)),
                waits=[("scalar", P.ev("scalar"))])
        st["pbc1_free"] = ("vector", ev_xn)
        st["pbc2_free"] = ("vector", ev_xn)
        st["rows_v"] = ev_xn
        st["x_free"][par] = ev_xn
        st["x_free_s"] = P.ev("scalar")
        st["xo"] = P.dma("sync", "s2_xo", xout3[:, :, toff:toff + TT],
                         xt[:, :].rearrange("p (c t) -> p c t", c=NCH),
                         waits=[("vector", ev_xn)])
    cfin = sb["car0"] if gsb[0] % 2 == 0 else sb["car1"]
    if "carry_out" in io:
        P.dma("sync", "s2_co", io["carry_out"], cfin[:, :],
              waits=[("vector", P.ev("vector"))])
    st["cfin"] = cfin
    st["cfin_ev"] = P.ev("vector")
    return st


def build_s2(t_core):
    decls = dict(CONST_DECL)
    decls.update({
        "xin": ((D, t_core), F32, "ExternalInput"),
        "xout": ((D, t_core), F32, "ExternalOutput"),
        "carry_in": ((128, NCH), F32, "ExternalInput"),
        "carry_out": ((128, NCH), F32, "ExternalOutput"),
        "l0wt": ((D, D), F32, "ExternalInput"),
        "l0b": ((128, NCH), F32, "ExternalInput"),
        "l0g": ((128, NCH), F32, "ExternalInput"),
        "l0bt": ((128, NCH), F32, "ExternalInput"),
        "m1tri": ((128, 128), F32, "ExternalInput"),
        "tri01": ((128, 128), F32, "ExternalInput"),
        "iota_row": ((1, 4096), F32, "ExternalInput"),
        "decay": ((1, 1), F32, "ExternalInput"),
    })
    import os
    if os.environ.get("S2_DEBUG"):
        decls["dbg_y"] = ((D, t_core), F32, "ExternalOutput")
        decls["dbg_s"] = ((D, t_core), BF16, "ExternalOutput")
        decls["dbg_u"] = ((D, t_core), BF16, "ExternalOutput")

    def emit(P, cm, aps, es, nc):
        sb = alloc_s2(nc, es, t_core)
        sb["eps_ln"] = cm["fconsts"][0:1, 0:1]
        sb["eps_rms"] = cm["fconsts"][0:1, 1:2]
        emit_s2(P, sb, cm, aps, t_core)
    return build_launch(decls, emit, t_core)



def alloc_s4(nc, es, T, sfx=""):
    sb = {"T": T}
    sb["w"] = es.enter_context(nc.sbuf_tensor("s4_w" + sfx, [128, 32 * 1024], BF16))
    sb["inp"] = es.enter_context(nc.sbuf_tensor("s4_in" + sfx, [128, 32 * 512], BF16))
    sb["inp2"] = es.enter_context(nc.sbuf_tensor("s4_in2" + sfx, [128, 32 * 512], BF16))
    sb["xt"] = es.enter_context(nc.sbuf_tensor("s4_xt" + sfx, [128, NCH * 512], F32))
    sb["xq"] = es.enter_context(nc.sbuf_tensor("s4_xq" + sfx, [128, NCH * 512], BF16))
    sb["sq"] = es.enter_context(nc.sbuf_tensor("s4_sq" + sfx, [128, NCH * 512], BF16))
    sb["rows"] = es.enter_context(nc.sbuf_tensor("s4_rows" + sfx, [1, 8 * 512], F32))
    sb["sc"] = es.enter_context(nc.sbuf_tensor("s4_sc" + sfx, [128, 512], F32))
    sb["va"] = es.enter_context(nc.sbuf_tensor("s4_va" + sfx, [128, 512], F32))
    sb["ob0"] = es.enter_context(nc.sbuf_tensor("s4_ob0" + sfx, [128, 512], BF16))
    sb["ob1"] = es.enter_context(nc.sbuf_tensor("s4_ob1" + sfx, [128, 512], BF16))
    sb["scl"] = es.enter_context(nc.sbuf_tensor("s4_scl" + sfx, [128, 4 * 32 + 16], F32))
    for nm in ("pst1", "pst2", "pbc1", "pbc2", "pA0", "pA1", "pB0", "pB1"):
        sb[nm] = es.enter_context(nc.psum_tensor("s4_" + sfx + nm, [128, 512], F32))
    return sb


def emit_s4(P, sb, cm, io, T):
    sb["onesb"] = cm["ones_colb"]
    sb["ones_rowf"] = cm["ones_rowf"]
    NTT = T // 512
    xin3 = fm_view(io["xin"])
    xout3 = fm_view(io["xout"])
    u43 = io["u4"].rearrange("(c p) t -> p c t", p=128)
    ha3 = io["ha"].rearrange("(c p) t -> p c t", p=128)
    hb3 = io["hb"].rearrange("(c p) t -> p c t", p=128)
    P.dma("sync", "c_const", sb["scl"][:, 0:32], io["s1s"])
    P.dma("sync", "c_const", sb["scl"][:, 32:64], io["s2s"])
    P.dma("sync", "c_const", sb["scl"][:, 64:96], io["sh1"])
    P.dma("sync", "c_const", sb["scl"][:, 96:128], io["sh2"])
    P.dma("sync", "c_const", sb["scl"][:, 128:136], io["s3s"])
    cw = [("c_const", P.ev("c_const")), ("c_cg", P.ev("c_cg"))]
    sb["const_waits"] = cw
    ev_s3h = P.op("vector", lambda E: E.tensor_scalar(
        sb["scl"][:, 128:136], sb["scl"][:, 128:136], 0.5, None, op0=ALU.mult),
        waits=cw)

    st = {"x_free_v": 0, "x_free_s": 0, "stats_mm": 0, "rows_v": 0, "rows_t": 0,
          "pbc1_free": 0, "pbc2_free": 0, "in_free": 0, "w_free": 0}

    WHALF = 32 * 512
    wbufs = (sb["w"][:, 0:WHALF], sb["w"][:, WHALF:2 * WHALF])
    wch = ("s4_wA", "s4_wB")
    wread = {0: 0, 1: 0}
    wev = {}

    def mk_load_s1(ob):
        def go(wb, par):
            P.dma("gpsimd", wch[par],
                  wb[:, 0:NCH * 512].rearrange("p (c m) -> p c m", c=NCH),
                  io["f1t"][:, ob * 512:(ob + 1) * 512].rearrange(
                      "(c p) m -> p c m", p=128),
                  waits=[("tensor", wread[par])])
            return P.dma("gpsimd", wch[par],
                  wb[:, NCH * 512:NCH * 1024].rearrange("p (c m) -> p c m", c=NCH),
                  io["f2t"][:, ob * 512:(ob + 1) * 512].rearrange(
                      "(c p) m -> p c m", p=128))
        return ("s1", ob), go

    def mk_load_mat(skey, wt_nm, ob):
        def go(wb, par):
            return P.dma("gpsimd", wch[par],
                  wb[:, 0:32 * 512].rearrange("p (c m) -> p c m", c=32),
                  io[wt_nm][:, ob * 512:(ob + 1) * 512].rearrange(
                      "(c p) m -> p c m", p=128),
                  waits=[("tensor", wread[par])])
        return (skey, ob), go

    def mk_load_w3(half):
        def go(wb, par):
            return P.dma("gpsimd", wch[par],
                  wb[:, 0:16 * 1024].rearrange("p (c m) -> p c m", c=16),
                  io["f3t"][half * 2048:(half + 1) * 2048, :].rearrange(
                      "(c p) m -> p c m", p=128),
                  waits=[("tensor", wread[par])])
        return ("w3", half), go

    wplan = []
    for ob in range(8):
        wplan.append(mk_load_s1(ob))
    for ob in range(8):
        wplan.append(mk_load_mat("s2", "h1t", ob))
    for ob in range(8):
        wplan.append(mk_load_mat("s3", "h2t", ob))
    wplan.append(mk_load_w3(0))
    wplan.append(mk_load_w3(1))
    wp_idx = [0]
    WDEFER = {9, 17, 25}

    def issue_next_w(flush=False):
        if wp_idx[0] >= len(wplan):
            return
        i = wp_idx[0]
        if (i in WDEFER) != flush:
            return
        wp_idx[0] += 1
        key, go = wplan[i]
        wev[key] = go(wbufs[i % 2], i % 2)

    issue_next_w()
    issue_next_w()

    gin = {"iread": {0: 0, 1: 0}, "ev": {}}

    def load_in_g(key, src, nch, ob, tt, extra_waits=()):
        ipar = (ob * NTT + tt) % 2
        buf = sb["inp"] if ipar == 0 else sb["inp2"]
        gin["ev"][(key, ob, tt)] = P.dma(
            "gpsimd", "s4_iA" if ipar == 0 else "s4_iB",
            buf[:, 0:nch * 512].rearrange("p (c t) -> p c t", c=nch),
            src[:, :, tt * 512:tt * 512 + 512],
            waits=[("tensor", gin["iread"][ipar])] + list(extra_waits))

    ev_u_store = 0
    for tt in range(NTT):
        toff = tt * 512
        ev_x = P.dma("sync", "s4_x",
                     sb["xt"][:, :].rearrange("p (c t) -> p c t", c=NCH),
                     xin3[:, :, toff:toff + 512],
                     waits=[("vector", st["x_free_v"]), ("scalar", st["x_free_s"])])
        sb["stat_war"] = [("vector", st["rows_v"])]
        pre = [("s4_x", ev_x), ("tensor", st["stats_mm"]), ("tensor", st["rows_t"])]
        si = emit_ln_stats(P, sb, sb["xt"], 512, pre, sb["eps_ln"])
        st["stats_mm"] = si["ev_stats_mm"]
        st["rows_v"] = si["ev_var"]
        evb1 = emit_bcast(P, sb["pbc1"], si["mu"], sb, 512,
                          waits=[("vector", si["ev_var"]), ("vector", st["pbc1_free"])])
        evb2 = emit_bcast(P, sb["pbc2"], si["rstd"], sb, 512,
                          waits=[("scalar", si["ev_rstd"]), ("vector", st["pbc2_free"])])
        st["rows_t"] = evb2
        P.op("vector", lambda E: E.tensor_tensor(
            v8(sb["xt"], NCH, 512), v8(sb["xt"], NCH, 512),
            bc8(sb["pbc1"][:, 0:512], NCH, 512), op=ALU.subtract),
            waits=[("tensor", evb2), ("s4_us", ev_u_store)])
        ev_u = P.op("vector", lambda E: E.tensor_tensor(
            v8(sb["xq"], NCH, 512), v8(sb["xt"], NCH, 512),
            bc8(sb["pbc2"][:, 0:512], NCH, 512), op=ALU.mult))
        st["pbc1_free"] = ev_u
        st["pbc2_free"] = ev_u
        st["x_free_v"] = ev_u
        st["x_free_s"] = P.ev("scalar")
        ev_u_store = P.dma("sync", "s4_us", u43[:, :, toff:toff + 512],
                           sb["xq"][:, :].rearrange("p (c t) -> p c t", c=NCH),
                           waits=[("vector", ev_u)])
        if tt == 0:
            load_in_g("s1", u43, NCH, 0, 0,
                      extra_waits=[("s4_us", ev_u_store)])
    P.barrier_all()

    def mm_stage(skey, src3, nch_in, dst3, scol, act, extra=None, nxt=None):
        TO = 512
        NB = 4096 // TO
        inch = ("s4_iA", "s4_iB")
        inbufs = (sb["inp"][:, 0:nch_in * 512], sb["inp2"][:, 0:nch_in * 512])
        iread = gin["iread"]
        obev = {}
        st0ev = [None]
        issue_next_w(flush=True)
        if (skey, 0, 0) not in gin["ev"]:
            load_in_g(skey, src3, nch_in, 0, 0)
        for ob in range(NB):
            par = ob % 2
            wb = wbufs[par]
            for tt in range(NTT):
                toff = tt * 512
                ipar = (ob * NTT + tt) % 2
                if tt + 1 < NTT:
                    load_in_g(skey, src3, nch_in, ob, tt + 1)
                elif ob + 1 < NB:
                    load_in_g(skey, src3, nch_in, ob + 1, 0)
                elif nxt is not None and st0ev[0] is not None:
                    load_in_g(nxt[0], nxt[1], nxt[2], 0, 0,
                              extra_waits=[("s4_os", st0ev[0])])
                ev_in = gin["ev"][(skey, ob, tt)]
                for oc in range(TO // 128):
                    pa = sb["pA0"] if oc % 2 == 0 else sb["pA1"]
                    fw = [(wch[par], wev[(skey, ob)]), (inch[ipar], ev_in),
                          ("scalar", obev.get(("ev_sA", oc % 2), 0)),
                          ("vector", obev.get(("ev_o", oc % 2), 0))]
                    ib_cur = inbufs[ipar]
                    for c in range(nch_in):
                        ev_mm = P.mm(lambda E, c=c, oc=oc, pa=pa, wb=wb,
                                     ib=ib_cur: E.matmul(
                            pa[:, 0:512],
                            wb[:, c * TO + oc * 128: c * TO + oc * 128 + 128],
                            ib[:, c * 512:(c + 1) * 512],
                            start=(c == 0), stop=(c == nch_in - 1)),
                            waits=(fw if c == 0 else []),
                            inc=(c == nch_in - 1))
                    if extra == "stage1":
                        pb = sb["pB0"] if oc % 2 == 0 else sb["pB1"]
                        for c in range(nch_in):
                            ev_mm2 = P.mm(lambda E, c=c, oc=oc, pb=pb, wb=wb,
                                          ib=ib_cur: E.matmul(
                                pb[:, 0:512],
                                wb[:, (nch_in + c) * TO + oc * 128:
                                    (nch_in + c) * TO + oc * 128 + 128],
                                ib[:, c * 512:(c + 1) * 512],
                                start=(c == 0), stop=(c == nch_in - 1)),
                                waits=([("vector", obev.get(("ev_o", oc % 2), 0))]
                                       if c == 0 else []),
                                inc=(c == nch_in - 1))
                        ev_sA = P.op("scalar", lambda E, oc=oc, ob=ob, pa=pa:
                                     E.activation(
                            sb["va"][:, :], pa[:, 0:512], AF.Silu,
                            scale=sb["scl"][:, ob * 4 + oc: ob * 4 + oc + 1]),
                            waits=[("tensor", ev_mm),
                                   ("vector", obev.get("last_o", 0))])
                        obuf = sb["ob0"] if (tt + oc) % 2 == 0 else sb["ob1"]
                        ev_o = P.op("vector", lambda E, oc=oc, ob=ob, pb=pb,
                                    obuf=obuf: E.scalar_tensor_tensor(
                            obuf[:, :], pb[:, 0:512],
                            sb["scl"][:, 32 + ob * 4 + oc: 32 + ob * 4 + oc + 1],
                            sb["va"][:, :], op0=ALU.mult, op1=ALU.mult),
                            waits=[("tensor", ev_mm2), ("scalar", ev_sA),
                                   ("s4_os", obev.get(("st", obuf is sb["ob1"]), 0))])
                    else:
                        obuf = sb["ob0"] if (tt + oc) % 2 == 0 else sb["ob1"]
                        ev_sA = None
                        ev_o = P.op("scalar", lambda E, oc=oc, ob=ob, pa=pa,
                                    obuf=obuf: E.activation(
                            obuf[:, :], pa[:, 0:512], act,
                            scale=sb["scl"][:, scol + ob * 4 + oc:
                                            scol + ob * 4 + oc + 1]),
                            waits=[("tensor", ev_mm),
                                   ("s4_os", obev.get(("st", obuf is sb["ob1"]), 0))])
                    obev[("ev_sA", oc % 2)] = P.ev("scalar")
                    obev[("ev_o", oc % 2)] = P.ev("vector")
                    obev["last_o"] = P.ev("vector")
                    who = "vector" if extra == "stage1" else "scalar"
                    ev_st = P.dma("sync", "s4_os",
                                  dst3[:, ob * 4 + oc, toff:toff + 512],
                                  obuf[:, :], waits=[(who, ev_o)])
                    obev[("st", obuf is sb["ob1"])] = ev_st
                if ob == NB - 1 and tt == 0:
                    st0ev[0] = ev_st
                iread[ipar] = P.ev("tensor")
            wread[par] = P.ev("tensor")
            issue_next_w()
        P.barrier_all()

    mm_stage("s1", u43, NCH, ha3, 0, None, extra="stage1", nxt=("s2", ha3, 32))
    mm_stage("s2", ha3, 32, hb3, 64, AF.Gelu, nxt=("s3", hb3, 32))
    mm_stage("s3", hb3, 32, ha3, 96, AF.Gelu, nxt=("s4", ha3, 32))

    issue_next_w(flush=True)
    ev_store = 0
    if ("s4", 0, 0) not in gin["ev"]:
        load_in_g("s4", ha3, 32, 0, 0)
    for tt in range(NTT):
        toff = tt * 512
        if tt + 1 < NTT:
            load_in_g("s4", ha3, 32, 0, tt + 1)
        ev_in = gin["ev"][("s4", 0, tt)]
        ib4 = sb["inp"] if tt % 2 == 0 else sb["inp2"]
        ev_x = P.dma("sync", "s4_x",
                     sb["xt"][:, :].rearrange("p (c t) -> p c t", c=NCH),
                     xin3[:, :, toff:toff + 512],
                     waits=[("vector", st.get("x4_free", 0)), ("s4_xo", ev_store)])
        ev_xn = None
        for oc in range(8):
            pa = sb["pA0"] if oc % 2 == 0 else sb["pA1"]
            fw = [("s4_wA", wev[("w3", 0)]), ("s4_wB", wev[("w3", 1)]),
                  ("s4_iA" if tt % 2 == 0 else "s4_iB", ev_in),
                  ("vector", st.get(("s4o", oc % 2), 0))]
            for c in range(32):
                ev_mm = P.mm(lambda E, c=c, oc=oc, pa=pa, ib=ib4: E.matmul(
                    pa[:, 0:512],
                    sb["w"][:, c * 1024 + oc * 128: c * 1024 + oc * 128 + 128],
                    ib[:, c * 512:(c + 1) * 512],
                    start=(c == 0), stop=(c == 31)),
                    waits=(fw if c == 0 else []),
                    inc=(c == 31))
            ev_xn = P.op("vector", lambda E, oc=oc, pa=pa: E.scalar_tensor_tensor(
                ccol(sb["xt"], oc, 512), pa[:, 0:512],
                sb["scl"][:, 128 + oc:128 + oc + 1],
                ccol(sb["xt"], oc, 512), op0=ALU.mult, op1=ALU.add),
                waits=[("tensor", ev_mm), ("s4_x", ev_x), ("vector", ev_s3h)])
            st[("s4o", oc % 2)] = ev_xn
        st["x4_free"] = ev_xn
        gin["iread"][tt % 2] = P.ev("tensor")
        ev_store = P.dma("sync", "s4_xo", xout3[:, :, toff:toff + 512],
                         sb["xt"][:, :].rearrange("p (c t) -> p c t", c=NCH),
                         waits=[("vector", ev_xn)])
    return st


def build_s4(t_core):
    decls = dict(CONST_DECL)
    decls.update({
        "xin": ((D, t_core), F32, "ExternalInput"),
        "xout": ((D, t_core), F32, "ExternalOutput"),
        "f1t": ((D, 4096), F32, "ExternalInput"),
        "f2t": ((D, 4096), F32, "ExternalInput"),
        "h1t": ((4096, 4096), F32, "ExternalInput"),
        "h2t": ((4096, 4096), F32, "ExternalInput"),
        "f3t": ((4096, D), F32, "ExternalInput"),
        "s1s": ((128, 32), F32, "ExternalInput"),
        "s2s": ((128, 32), F32, "ExternalInput"),
        "sh1": ((128, 32), F32, "ExternalInput"),
        "sh2": ((128, 32), F32, "ExternalInput"),
        "s3s": ((128, 8), F32, "ExternalInput"),
    })

    def emit(P, cm, aps, es, nc):
        aps = dict(aps)
        aps["u4"] = nc.dram_tensor("u4", [D, t_core], BF16).ap()
        aps["ha"] = nc.dram_tensor("ha", [4096, t_core], BF16).ap()
        aps["hb"] = nc.dram_tensor("hb", [4096, t_core], BF16).ap()
        sb = alloc_s4(nc, es, t_core)
        sb["eps_ln"] = cm["fconsts"][0:1, 0:1]
        sb["eps_rms"] = cm["fconsts"][0:1, 1:2]
        emit_s4(P, sb, cm, aps, t_core)
    return build_launch(decls, emit, t_core)



def alloc_s5(nc, es, T, sfx=""):
    sb = {"T": T}
    for w in ("qwtS", "kwtS", "vwtS", "owtS"):
        sb[w] = es.enter_context(nc.sbuf_tensor("s5_" + sfx + w, [128, NCH * D], BF16))
    sb["mA0"] = es.enter_context(nc.sbuf_tensor("s5_mA0" + sfx, [128, 384], BF16))
    sb["mA1"] = es.enter_context(nc.sbuf_tensor("s5_mA1" + sfx, [128, 384], BF16))
    sb["mB"] = es.enter_context(nc.sbuf_tensor("s5_mB" + sfx, [128, 384], BF16))
    sb["qb8"] = es.enter_context(nc.sbuf_tensor("s5_qb8" + sfx, [128, NCH], F32))
    sb["kb"] = es.enter_context(nc.sbuf_tensor("s5_kb" + sfx, [128, NCH], F32))
    sb["ob_"] = es.enter_context(nc.sbuf_tensor("s5_ob" + sfx, [128, NCH], F32))
    sb["vbrow"] = es.enter_context(nc.sbuf_tensor("s5_vbrow" + sfx, [1, D], F32))
    sb["vbb"] = es.enter_context(nc.sbuf_tensor("s5_vbb" + sfx, [128, D], F32))
    sb["xt0"] = es.enter_context(nc.sbuf_tensor("s5_xt0" + sfx, [128, D], F32))
    sb["xt1"] = es.enter_context(nc.sbuf_tensor("s5_xt1" + sfx, [128, D], F32))
    sb["xq"] = es.enter_context(nc.sbuf_tensor("s5_xq" + sfx, [128, D], BF16))
    sb["sq"] = es.enter_context(nc.sbuf_tensor("s5_sq" + sfx, [128, D], BF16))
    sb["u5b0"] = es.enter_context(nc.sbuf_tensor("s5_u5b0" + sfx, [128, D], BF16))
    sb["u5b1"] = es.enter_context(nc.sbuf_tensor("s5_u5b1" + sfx, [128, D], BF16))
    sb["rows"] = es.enter_context(nc.sbuf_tensor("s5_rows" + sfx, [1, 7 * 128], F32))
    sb["sc"] = es.enter_context(nc.sbuf_tensor("s5_sc" + sfx, [128, 128], F32))
    sb["qsb0"] = es.enter_context(nc.sbuf_tensor("s5_qsb0" + sfx, [64, 16 * 128], BF16))
    sb["qsb1"] = es.enter_context(nc.sbuf_tensor("s5_qsb1" + sfx, [64, 16 * 128], BF16))
    sb["kr"] = es.enter_context(nc.sbuf_tensor("s5_kr" + sfx, [64, 4 * 16 * 128], BF16))
    sb["vr"] = es.enter_context(nc.sbuf_tensor("s5_vr" + sfx, [128, 4 * D], BF16))
    sb["aosb0"] = es.enter_context(nc.sbuf_tensor("s5_aosb0" + sfx, [128, D], BF16))
    sb["aosb1"] = es.enter_context(nc.sbuf_tensor("s5_aosb1" + sfx, [128, D], BF16))
    sb["mss"] = es.enter_context(nc.sbuf_tensor("s5_mss" + sfx, [128, 4 * 384], F32))
    sb["pw"] = es.enter_context(nc.sbuf_tensor("s5_pw" + sfx, [128, 4 * 384], BF16))
    sb["wT"] = es.enter_context(nc.sbuf_tensor("s5_wT" + sfx, [128, 2 * 384], BF16))
    sb["rsc"] = es.enter_context(nc.sbuf_tensor("s5_rsc" + sfx, [128, 16], F32))
    for nm in ("pst1", "pQK0", "pQK1", "psc0", "psc1", "pO"):
        sb[nm] = es.enter_context(nc.psum_tensor("s5_" + sfx + nm, [128, 512], F32))
    sb["pst2"] = sb["pst1"][:, 256:512]
    sb["ptb5a"] = es.enter_context(nc.psum_tensor("s5_ptb5a" + sfx, [128, 1024], BF16))
    sb["ptb5b"] = es.enter_context(nc.psum_tensor("s5_ptb5b" + sfx, [128, 1024], BF16))
    return sb


def emit_s5(P, sb, cm, io, T):
    sb["onesb"] = cm["ones_colb"]
    sb["ones_rowf"] = cm["ones_rowf"]
    NT5 = T // 128
    xin3 = fm_view(io["xin"])
    xout3 = fm_view(io["xout"])
    halo3 = fm_view(io["halo_x"])
    for nm, ap in (("qb8", io["qb"]), ("kb", io["kb"]), ("ob_", io["obias"]),
                   ("vbrow", io["vbrow"])):
        P.dma("sync", "c_const", sb[nm][:, :], ap)
    for nm, ap in (("mA0", io["maskA0"]), ("mA1", io["maskA1"]), ("mB", io["maskB"])):
        P.dma("gpsimd", "c_cg", sb[nm][:, :], ap)
    for w, ap in (("qwtS", io["qwt"]), ("kwtS", io["kwt"]),
                  ("vwtS", io["vwt"]), ("owtS", io["owt"])):
        P.dma("gpsimd", "c_cg", sb[w][:, :].rearrange("p (c e) -> p c e", c=NCH),
              ap.rearrange("(c p) e -> p c e", p=128))
    cw = [("c_const", P.ev("c_const")), ("c_cg", P.ev("c_cg"))]
    sb["const_waits"] = cw
    ev_qb8 = P.op("vector", lambda E: E.tensor_scalar(
        sb["qb8"][:, :], sb["qb8"][:, :], 0.125, None, op0=ALU.mult), waits=cw)
    ev_vb1 = P.mm(lambda E: E.matmul(sb["pQK0"][:, 0:512], cm["ones_rowf"][:, :],
                                     sb["vbrow"][0:1, 0:512], start=True, stop=True),
                  waits=cw, inc=True)
    ev_vb2 = P.mm(lambda E: E.matmul(sb["pQK1"][:, 0:512], cm["ones_rowf"][:, :],
                                     sb["vbrow"][0:1, 512:1024], start=True, stop=True),
                  inc=True)
    P.op("vector", lambda E: E.tensor_copy(sb["vbb"][:, 0:512], sb["pQK0"][:, 0:512]),
         waits=[("tensor", ev_vb1)])
    ev_vbb = P.op("vector", lambda E: E.tensor_copy(sb["vbb"][:, 512:1024],
                                                    sb["pQK1"][:, 0:512]),
                  waits=[("tensor", ev_vb2)])

    st = {"stats_mm": 0, "rows_v": 0, "rows_t": 0,
          "pbc1_free": ev_vbb, "pbc2_free": ev_vbb,
          "x_free": {0: 0, 1: 0}, "x_free_s": {0: 0, 1: 0},
          "xo": {0: 0, 1: 0}, "u5_free_t": {0: 0, 1: 0}}
    sb["pbc1"] = sb["pQK0"]
    sb["pbc2"] = sb["pQK1"]
    kv_state = {"kslot_free": {}, "vslot_free": {}, "pQK_free_s": 0,
                "pQK_free_v": ev_vbb}
    head_ev = {}

    def emit_ln_u5(src3, toff, par):
        xt = sb["xt0"] if par == 0 else sb["xt1"]
        u5b = sb["u5b0"] if par == 0 else sb["u5b1"]
        xch = "s5_x0" if par == 0 else "s5_x1"
        ev_x = P.dma("sync", xch,
                     xt[:, :].rearrange("p (c t) -> p c t", c=NCH),
                     src3[:, :, toff:toff + 128],
                     waits=[("vector", st["x_free"][par]),
                            ("scalar", st["x_free_s"][par]),
                            ("s5_xo", st["xo"][par])])
        sb["stat_war"] = [("vector", st["rows_v"])]
        pre = [(xch, ev_x), ("tensor", st["stats_mm"]), ("tensor", st["rows_t"])]
        si = emit_ln_stats(P, sb, xt, 128, pre, sb["eps_ln"])
        st["stats_mm"] = si["ev_stats_mm"]
        st["rows_v"] = si["ev_var"]
        evb1 = emit_bcast(P, sb["pbc1"], si["mu"], sb, 128,
                          waits=[("vector", si["ev_var"]), ("vector", st["pbc1_free"]),
                                 ("scalar", kv_state["pQK_free_s"])])
        evb2 = emit_bcast(P, sb["pbc2"], si["rstd"], sb, 128,
                          waits=[("scalar", si["ev_rstd"]), ("vector", st["pbc2_free"])])
        st["rows_t"] = evb2
        P.op("vector", lambda E, xt=xt, u5b=u5b: E.tensor_tensor(
            v8(u5b, NCH, 128), v8(xt, NCH, 128),
            bc8(sb["pbc1"][:, 0:128], NCH, 128), op=ALU.subtract),
            waits=[("tensor", evb2), ("tensor", st["u5_free_t"][par])])
        ev_u = P.op("vector", lambda E, u5b=u5b: E.tensor_tensor(
            v8(u5b, NCH, 128), v8(u5b, NCH, 128),
            bc8(sb["pbc2"][:, 0:128], NCH, 128), op=ALU.mult))
        st["pbc1_free"] = ev_u
        st["pbc2_free"] = ev_u
        st["x_free"][par] = ev_u
        st["x_free_s"][par] = P.ev("scalar")
        return ev_u

    def emit_kv(ev_u, slot, par):
        u5b = sb["u5b0"] if par == 0 else sb["u5b1"]
        ev_ke = None
        bank_s = {0: kv_state["pQK_free_s"], 1: kv_state["pQK_free_s"]}
        for ec in range(NCH):
            bank = sb["pQK0"] if ec % 2 == 0 else sb["pQK1"]
            fw = [("vector", ev_u), ("scalar", bank_s[ec % 2]),
                  ("vector", kv_state["pQK_free_v"]),
                  ("vector", st["pbc1_free"])] + cw
            for c in range(NCH):
                ev_km = P.mm(lambda E, ec=ec, c=c, bank=bank, u5b=u5b: E.matmul(
                    bank[:, 0:128],
                    sb["kwtS"][:, c * D + ec * 128: c * D + ec * 128 + 128],
                    ccol(u5b, c, 128),
                    start=(c == 0), stop=(c == NCH - 1)),
                    waits=(fw if c == 0 else []), inc=(c == NCH - 1))
            for hh in range(2):
                h = 2 * ec + hh
                ev_ke = P.op("scalar", lambda E, ec=ec, hh=hh, h=h, slot=slot,
                             bank=bank: E.activation(
                    sb["kr"][:, slot * 2048 + h * 128: slot * 2048 + h * 128 + 128],
                    bank[hh * 64:(hh + 1) * 64, 0:128], AF.Identity,
                    bias=sb["kb"][hh * 64:(hh + 1) * 64, ec:ec + 1]),
                    waits=[("tensor", ev_km),
                           ("tensor", kv_state["kslot_free"].get((slot, h), 0))])
            bank_s[ec % 2] = ev_ke
        kv_state["pQK_free_s"] = ev_ke
        ev_ve = None
        for g in range(2):
            bank = sb["pQK0"] if g == 0 else sb["pQK1"]
            for c in range(NCH):
                ev_vm = P.mm(lambda E, g=g, c=c, bank=bank, u5b=u5b: E.matmul(
                    bank[:, 0:512],
                    ccol(u5b, c, 128),
                    sb["vwtS"][:, c * D + g * 512: c * D + g * 512 + 512],
                    start=(c == 0), stop=(c == NCH - 1)),
                    waits=([("scalar", ev_ke), ("scalar", bank_s[g]),
                            ("tensor", kv_state["vslot_free"].get((slot, g), 0))]
                           if c == 0 else []),
                    inc=(c == NCH - 1))
            ev_ve = P.op("vector", lambda E, g=g, slot=slot, bank=bank: E.tensor_add(
                sb["vr"][:, slot * D + g * 512: slot * D + g * 512 + 512],
                bank[:, 0:512], sb["vbb"][:, g * 512:(g + 1) * 512]),
                waits=[("tensor", ev_vm), ("vector", ev_vbb)])
        kv_state["pQK_free_v"] = ev_ve
        st["u5_free_t"][par] = P.ev("tensor")
        return ev_ve

    def emit_qproj(par):
        u5b = sb["u5b0"] if par == 0 else sb["u5b1"]
        qsb = sb["qsb0"] if par == 0 else sb["qsb1"]
        ev_qe = None
        qbank_s = {0: kv_state["pQK_free_s"], 1: kv_state["pQK_free_s"]}
        for ec in range(NCH):
            bank = sb["pQK0"] if ec % 2 == 0 else sb["pQK1"]
            fw = [("scalar", qbank_s[ec % 2]), ("vector", kv_state["pQK_free_v"])]
            for c in range(NCH):
                ev_qm = P.mm(lambda E, ec=ec, c=c, bank=bank, u5b=u5b: E.matmul(
                    bank[:, 0:128],
                    sb["qwtS"][:, c * D + ec * 128: c * D + ec * 128 + 128],
                    ccol(u5b, c, 128),
                    start=(c == 0), stop=(c == NCH - 1)),
                    waits=(fw if c == 0 else []), inc=(c == NCH - 1))
            for hh in range(2):
                h = 2 * ec + hh
                ev_qe = P.op("scalar", lambda E, ec=ec, hh=hh, h=h, bank=bank,
                             qsb=qsb: E.activation(
                    qsb[:, h * 128:(h + 1) * 128],
                    bank[hh * 64:(hh + 1) * 64, 0:128], AF.Identity, scale=0.125,
                    bias=sb["qb8"][hh * 64:(hh + 1) * 64, ec:ec + 1]),
                    waits=[("tensor", ev_qm), ("vector", ev_qb8),
                           ("tensor", head_ev.get(("q_free", par, h), 0))])
            qbank_s[ec % 2] = ev_qe
        kv_state["pQK_free_s"] = ev_qe
        return ev_qe

    for hb in range(2):
        ev_u = emit_ln_u5(halo3, hb * 128, hb)
        emit_kv(ev_u, 2 + hb, hb)

    for pair in range(NT5 // 2):
        j0 = pair * 2
        info = {}
        for jj in (j0, j0 + 1):
            par = jj % 2
            ev_u = emit_ln_u5(xin3, jj * 128, par)
            ev_kv = emit_kv(ev_u, jj % 4, par)
            ev_qe = emit_qproj(par)
            info[jj] = {"kv": ev_kv, "qe": ev_qe,
                        "mask": sb["mA0"] if jj == 0 else (sb["mA1"] if jj == 1
                                                           else sb["mB"]),
                        "slots": [(jj - 2) % 4, (jj - 1) % 4, jj % 4]}
        elems = [(j0, h) for h in range(16)] + [(j0 + 1, h) for h in range(16)]
        NS = len(elems)
        if pair == 0:
            sb["_pipe"] = dict(sc={}, red={}, rc={}, tp={}, ov={}, wc={},
                               pw2={}, ma={}, exp={}, oev={})
        pp = sb["_pipe"]
        B = pair * NS
        sc_done, red_done, rc_done = pp["sc"], pp["red"], pp["rc"]
        tp_done, ov_done, wc_done = pp["tp"], pp["ov"], pp["wc"]
        pw2_done, ma_done, exp_done, oev_done = (pp["pw2"], pp["ma"],
                                                 pp["exp"], pp["oev"])

        def emit_scores(s):
            jj, h = elems[s]
            psc = sb["psc0"] if s % 2 == 0 else sb["psc1"]
            qsb = sb["qsb0"] if jj % 2 == 0 else sb["qsb1"]
            fw0 = [("scalar", info[jj]["qe"]),
                   ("scalar", exp_done.get(B + s - 2, 0)),
                   ("vector", head_ev.get(("psc", s % 2), 0))]
            if B + s < 2:
                fw0 += sb["const_waits"]
            ev = None
            for wi, slot in enumerate(info[jj]["slots"]):
                P.mm(lambda E, h=h, wi=wi, slot=slot, psc=psc, qsb=qsb:
                     E.matmul(
                    psc[:, wi * 128:(wi + 1) * 128],
                    qsb[:, h * 128:(h + 1) * 128],
                    sb["kr"][:, slot * 2048 + h * 128: slot * 2048 + h * 128 + 128],
                    start=True, stop=False),
                    waits=(fw0 if wi == 0 else []), inc=False)
                ev = P.mm(lambda E, wi=wi, psc=psc, mk=info[jj]["mask"]: E.matmul(
                    psc[:, wi * 128:(wi + 1) * 128], cm["identb"][:, :],
                    mk[:, wi * 128:(wi + 1) * 128],
                    start=False, stop=True), inc=(wi == 2))
            sc_done[B + s] = ev

        def emit_softmax(s):
            jj, h = elems[s]
            psc = sb["psc0"] if s % 2 == 0 else sb["psc1"]
            par = s % 4
            pw = sb["pw"][:, par * 384:(par + 1) * 384]
            nx = sb["rsc"][:, par * 4:par * 4 + 1]
            rs = sb["rsc"][:, par * 4 + 1:par * 4 + 2]
            ri = sb["rsc"][:, par * 4 + 2:par * 4 + 3]
            red_done[B + s] = P.op("vector", lambda E, psc=psc, nx=nx: E.reduce_max(
                nx, psc[:, 0:384], axis=AX.X, negate=True),
                waits=[("tensor", sc_done[B + s]),
                       ("scalar", exp_done.get(B + s - 4, 0))])
            exp_done[B + s] = P.op("scalar", lambda E, psc=psc, pw=pw, nx=nx, rs=rs:
                               E.activation(pw, psc[:, 0:384], AF.Exp, bias=nx,
                                            accum_out=rs),
                               waits=[("vector", red_done[B + s]),
                                      ("vector", rc_done.get(B + s - 4, 0)),
                                      ("tensor", tp_done.get(B + s - 4, 0))])
        def emit_sm2(s):
            par = s % 4
            pw = sb["pw"][:, par * 384:(par + 1) * 384]
            rs = sb["rsc"][:, par * 4 + 1:par * 4 + 2]
            ri = sb["rsc"][:, par * 4 + 2:par * 4 + 3]
            rc_done[B + s] = P.op("vector", lambda E, rs=rs, ri=ri: E.reciprocal(ri, rs),
                              waits=[("scalar", exp_done[B + s]),
                                     ("vector", pw2_done.get(B + s - 4, 0))])
            pw2_done[B + s] = P.op("vector", lambda E, pw=pw, ri=ri: E.tensor_scalar(
                pw, pw, ri, None, op0=ALU.mult))

        def emit_tp(s):
            par = s % 2
            bank = sb["ptb5a"] if par == 0 else sb["ptb5b"]
            pw = sb["pw"][:, (s % 4) * 384:(s % 4) * 384 + 384]
            ev = None
            for wi in range(3):
                ev = P.mm(lambda E, wi=wi, bank=bank, pw=pw: E.transpose(
                    bank[:, wi * 128:(wi + 1) * 128],
                    pw[:, wi * 128:(wi + 1) * 128], cm["identb"][:, :]),
                    waits=([("vector", pw2_done[B + s]),
                            ("scalar", wc_done.get(B + s - 2, 0))] if wi == 0 else []),
                    inc=(wi == 2))
            tp_done[B + s] = ev
            wc_done[B + s] = P.op("scalar", lambda E, par=par, bank=bank: E.activation(
                sb["wT"][:, par * 384:(par + 1) * 384], bank[:, 0:384], AF.Copy),
                waits=[("tensor", ev), ("tensor", ov_done.get(B + s - 2, 0))])

        def emit_ov(s):
            jj, h = elems[s]
            par = s % 2
            ocol = (s % 4) * 128
            ev = None
            for wi, slot in enumerate(info[jj]["slots"]):
                ev = P.mm(lambda E, h=h, wi=wi, slot=slot, ocol=ocol, par=par:
                          E.matmul(
                    sb["pO"][0:64, ocol:ocol + 128],
                    sb["vr"][:, slot * D + h * 64: slot * D + h * 64 + 64],
                    sb["wT"][:, par * 384 + wi * 128: par * 384 + wi * 128 + 128],
                    start=(wi == 0), stop=(wi == 2)),
                    waits=([("scalar", wc_done[B + s]), ("vector", info[jj]["kv"]),
                            ("vector", oev_done.get(B + s - 4,
                                head_ev.get("pOx", 0)))]
                           if wi == 0 else []),
                    inc=(wi == 2))
            ov_done[B + s] = ev
            if s % 4 == 3:
                q0s = s - 3
                evq = None
                for sub in range(2):
                    jjq, hq = elems[q0s + sub]
                    aosb = sb["aosb0"] if jjq % 2 == 0 else sb["aosb1"]
                    evq = P.op("vector", lambda E, hq=hq, sub=sub, aosb=aosb:
                               E.tensor_copy(
                        aosb[sub * 64:sub * 64 + 64,
                             (hq // 2) * 128:(hq // 2) * 128 + 256]
                            .rearrange("p (a t) -> p a t", a=2),
                        sb["pO"][0:64, 0:512].rearrange("p (a t) -> p a t", a=4)
                            [:, sub:4:2, :]),
                        waits=[("tensor", ev)] if sub == 0 else [])
                for ss in range(q0s, s + 1):
                    oev_done[B + ss] = evq
                head_ev["pOx"] = evq

        for step in range(NS + 6):
            if step < NS:
                emit_scores(step)
            if 6 <= step <= NS + 5:
                emit_ov(step - 6)
            if 4 <= step <= NS + 3:
                emit_tp(step - 4)
            if 2 <= step <= NS + 1:
                emit_sm2(step - 2)
            if step < NS:
                emit_softmax(step)

        for jj in (j0, j0 + 1):
            par = jj % 2
            xt = sb["xt0"] if par == 0 else sb["xt1"]
            aosb = sb["aosb0"] if par == 0 else sb["aosb1"]
            last_oev = oev_done[B + (15 if jj == j0 else 31)]
            ev_xn = None
            for ec in range(NCH):
                bank = sb["psc0"] if ec % 2 == 0 else sb["psc1"]
                fw = [("vector", last_oev), ("vector", head_ev.get(("psc", ec % 2), 0)),
                      ("scalar", exp_done.get(B + 30, 0)),
                      ("scalar", exp_done.get(B + 31, 0))]
                for c in range(NCH):
                    ev_opm = P.mm(lambda E, ec=ec, c=c, bank=bank, aosb=aosb:
                                  E.matmul(
                        bank[:, 0:128],
                        sb["owtS"][:, c * D + ec * 128: c * D + ec * 128 + 128],
                        ccol(aosb, c, 128),
                        start=(c == 0), stop=(c == NCH - 1)),
                        waits=(fw if c == 0 else []), inc=(c == NCH - 1))
                ev_xn = P.op("vector", lambda E, ec=ec, bank=bank, xt=xt:
                             E.scalar_tensor_tensor(
                    ccol(xt, ec, 128), bank[:, 0:128], sb["ob_"][:, ec:ec + 1],
                    ccol(xt, ec, 128), op0=ALU.add, op1=ALU.add),
                    waits=[("tensor", ev_opm)])
                head_ev[("psc", ec % 2)] = ev_xn
            st["x_free"][par] = ev_xn
            st["xo"][par] = P.dma("sync", "s5_xo",
                                  xout3[:, :, jj * 128:(jj + 1) * 128],
                                  xt[:, :].rearrange("p (c t) -> p c t", c=NCH),
                                  waits=[("vector", ev_xn)])
            for h in range(16):
                head_ev[("q_free", par, h)] = P.ev("tensor")
        for h in range(16):
            kv_state["kslot_free"][((j0 + 2) % 4, h)] = P.ev("tensor")
            kv_state["kslot_free"][((j0 + 3) % 4, h)] = P.ev("tensor")
        for g in range(2):
            kv_state["vslot_free"][((j0 + 2) % 4, g)] = P.ev("tensor")
            kv_state["vslot_free"][((j0 + 3) % 4, g)] = P.ev("tensor")
    return st


def build_s5(t_core):
    decls = dict(CONST_DECL)
    decls.update({
        "xin": ((D, t_core), F32, "ExternalInput"),
        "xout": ((D, t_core), F32, "ExternalOutput"),
        "halo_x": ((D, 256), F32, "ExternalInput"),
        "maskA0": ((128, 384), F32, "ExternalInput"),
        "maskA1": ((128, 384), F32, "ExternalInput"),
        "maskB": ((128, 384), F32, "ExternalInput"),
        "qwt": ((D, D), F32, "ExternalInput"),
        "kwt": ((D, D), F32, "ExternalInput"),
        "vwt": ((D, D), F32, "ExternalInput"),
        "owt": ((D, D), F32, "ExternalInput"),
        "qb": ((128, NCH), F32, "ExternalInput"),
        "kb": ((128, NCH), F32, "ExternalInput"),
        "vbrow": ((1, D), F32, "ExternalInput"),
        "obias": ((128, NCH), F32, "ExternalInput"),
    })

    def emit(P, cm, aps, es, nc):
        sb = alloc_s5(nc, es, t_core)
        sb["eps_ln"] = cm["fconsts"][0:1, 0:1]
        emit_s5(P, sb, cm, aps, t_core)
    return build_launch(decls, emit, t_core)


def make_masks(first_half):
    qi = np.arange(128)[:, None]
    ki = np.arange(384)[None, :]
    band = (ki >= qi) & (ki <= qi + 256)
    mB = np.where(band, 0.0, NEG_BIG).astype(np.float32)
    if first_half:
        mA0 = np.where(band & (ki >= 256), 0.0, NEG_BIG).astype(np.float32)
        mA1 = np.where(band & (ki >= 128), 0.0, NEG_BIG).astype(np.float32)
    else:
        mA0 = mB.copy()
        mA1 = mB.copy()
    return mA0, mA1, mB



_PROG_CACHE = {}


def _prog(name, t_core):
    key = (name, t_core)
    if key not in _PROG_CACHE:
        builder = {"pre": lambda t: build_pre(t, reverse=False),
                   "post": lambda t: build_pre(t, reverse=True),
                   "s1": build_s1, "s2": build_s2, "s4": build_s4,
                   "s5": build_s5}[name]
        _PROG_CACHE[key] = builder(t_core)
    return _PROG_CACHE[key]


def _dev_runner(nc, in_maps):
    res = run_bass_kernel_spmd(nc, in_maps, core_ids=list(range(len(in_maps))))
    return res.results


def run_chain(inputs, n_cores, t_core, seq_len, runner):
    g = prep_host(inputs)
    x_full = np.asarray(inputs["L0_attn_Z"], np.float32)
    B = x_full.shape[0]
    xs = []
    for c in range(n_cores):
        b, h = c // 2, c % 2
        xs.append(np.ascontiguousarray(x_full[b, h * t_core:(h + 1) * t_core, :]))
    consts = {"identf": g["identf"], "ones_row": g["ones_row"],
              "ones_col": g["ones_col"],
              "fconsts": np.array([[LN_EPS, RMS_EPS, 0, 0, 0, 0, 0, 0]], np.float32)}
    zcarry = np.zeros((128, NCH), np.float32)
    zhalo = np.zeros((D, 256), np.float32)
    masks = {True: make_masks(True), False: make_masks(False)}

    res = runner(_prog("pre", t_core),
                 [{**consts, "xin": xs[c]} for c in range(n_cores)])
    xf = [np.ascontiguousarray(r["xout"]) for r in res]

    def s1_maps():
        return [{**consts, "xin": xf[c], "w1t": g["w1t"], "w2t": g["w2t"],
                 "b1s": g["b1s"], "b2s": g["b2s"], "nws": g["nws"],
                 "gwt": g["gwt"], "selrows": g["selrows"]} for c in range(n_cores)]

    res = runner(_prog("s1", t_core), s1_maps())
    xf = [np.ascontiguousarray(r["xout"]) for r in res]

    def s2_maps(carries):
        return [{**consts, "xin": xf[c], "carry_in": carries[c],
                 "l0wt": g["l0wt"], "l0b": g["l0b"], "l0g": g["l0g"],
                 "l0bt": g["l0bt"], "m1tri": g["m1tri"], "tri01": g["tri01"],
                 "iota_row": g["iota_row"], "decay": g["decay"]}
                for c in range(n_cores)]
    res = runner(_prog("s2", t_core), s2_maps([zcarry] * n_cores))
    carries = [np.ascontiguousarray(r["carry_out"]) for r in res]
    routed = [zcarry if c % 2 == 0 else carries[c - 1] for c in range(n_cores)]
    res = runner(_prog("s2", t_core), s2_maps(routed))
    xf = [np.ascontiguousarray(r["xout"]) for r in res]

    res = runner(_prog("s1", t_core), s1_maps())
    xf = [np.ascontiguousarray(r["xout"]) for r in res]

    def s4_maps():
        return [{**consts, "xin": xf[c], "f1t": g["f1t"], "f2t": g["f2t"],
                 "h1t": g["h1t"], "h2t": g["h2t"], "f3t": g["f3t"],
                 "s1s": g["s1s"], "s2s": g["s2s"], "sh1": g["sh1"],
                 "sh2": g["sh2"], "s3s": g["s3s"]} for c in range(n_cores)]
    res = runner(_prog("s4", t_core), s4_maps())
    xf = [np.ascontiguousarray(r["xout"]) for r in res]

    maps5 = []
    for c in range(n_cores):
        first = (c % 2 == 0)
        mA0, mA1, mB = masks[first]
        halo = zhalo if first else np.ascontiguousarray(xf[c - 1][:, t_core - 256:])
        maps5.append({**consts, "xin": xf[c], "halo_x": halo,
                      "maskA0": mA0, "maskA1": mA1, "maskB": mB,
                      "qwt": g["qwt"], "kwt": g["kwt"], "vwt": g["vwt"],
                      "owt": g["owt"], "qb": g["qb"], "kb": g["kb"],
                      "vbrow": np.ascontiguousarray(
                          np.asarray(inputs["L1_attn_v_b"], np.float32)[None, :]),
                      "obias": g["ob"]})
    res = runner(_prog("s5", t_core), maps5)
    xf = [np.ascontiguousarray(r["xout"]) for r in res]

    res = runner(_prog("s4", t_core), s4_maps())
    xf = [np.ascontiguousarray(r["xout"]) for r in res]

    res = runner(_prog("post", t_core),
                 [{**consts, "xin": xf[c]} for c in range(n_cores)])
    outs = [np.ascontiguousarray(r["xout"]) for r in res]
    full = np.empty((B, 2 * t_core, D), np.float32)
    for c in range(n_cores):
        b, h = c // 2, c % 2
        full[b, h * t_core:(h + 1) * t_core, :] = outs[c]
    return full


def kernel(**inputs):
    import os
    if os.environ.get("MK_UNFUSED"):
        return run_chain(inputs, NCORES, 4096, 8192, _dev_runner)
    return kernel_fused(**inputs)



from concourse.expressions import make_scalar_value as _msv


def _emit_mb_init(P, mb):
    ev = P.dma("sync", "mb_ctl", mb["ctl_sb"][:, :], mb["ctl_ap"])

    def closure(E):
        nc = P.nc
        ET = mybir.EngineType
        mb["my"] = P.nc.values_load(mb["ctl_sb"][0:1, 0:1], engines=[ET.SP],
                                    min_val=0, max_val=1,
                                    skip_runtime_bounds_check=True)
        mb["other"] = P.nc.values_load(mb["ctl_sb"][0:1, 1:2], engines=[ET.SP],
                                       min_val=0, max_val=1,
                                       skip_runtime_bounds_check=True)
        mb["nonce_c"] = P.nc.values_load(mb["ctl_sb"][0:1, 2:3], engines=[ET.SP],
                                         min_val=0, max_val=1 << 30,
                                         skip_runtime_bounds_check=True)
        mb["nonce_h"] = P.nc.values_load(mb["ctl_sb"][0:1, 3:4], engines=[ET.SP],
                                         min_val=0, max_val=1 << 30,
                                         skip_runtime_bounds_check=True)
        mb["skip"] = P.nc.values_load(mb["ctl_sb"][0:1, 4:5], engines=[ET.SP],
                                      min_val=0, max_val=1,
                                      skip_runtime_bounds_check=True)
        mb["src"] = P.nc.values_load(mb["ctl_sb"][0:1, 5:6], engines=[ET.SP],
                                     min_val=0, max_val=1,
                                     skip_runtime_bounds_check=True)
    P.raw("sync", closure, waits=[("mb_ctl", ev)])


def _emit_exchange(P, mb, tag, data_out_ap_fn, data_in_ap_fn, pre_waits):
    psem = mb["psem_" + tag]
    flag = mb["flag_" + tag]
    nonce_key = "nonce_" + tag

    def closure(E):
        myv, otv = mb["my"], mb["other"]
        dst, srcv = data_out_ap_fn(myv)
        E.dma_start(out=dst, in_=srcv).then_inc(psem, 16)
        E.wait_ge(psem, 16)
        E.dma_start(out=flag.ap()[bass.ds(myv, 1), 0:1],
                    in_=mb["ctl_sb"][0:1, (2 if tag == "c" else 3):
                                     (3 if tag == "c" else 4)]).then_inc(psem, 16)
        E.wait_ge(psem, 32)
        with E.register("exp_" + tag) as exp:
            E.reg_mov(exp, 32)

            def cond():
                E.reg_add(exp, exp, 16)
                E.dma_start(out=mb["ftmp"][0:1, 0:1],
                            in_=flag.ap()[bass.ds(otv, 1), 0:1]).then_inc(psem, 16)
                E.wait_ge(psem, _msv(exp, min_val=0, max_val=1 << 26))
                v = P.nc.values_load(mb["ftmp"][0:1, 0:1],
                                     engines=[mybir.EngineType.SP],
                                     min_val=-(1 << 30), max_val=1 << 30,
                                     skip_runtime_bounds_check=True)
                return ((v != mb[nonce_key]) & (mb["skip"] != 1) & (
                    _msv(exp, min_val=0, max_val=1 << 26) < 16 * 400000))
            with E.While(cond):
                E.nop()
            dsti, srci = data_in_ap_fn(mb["src"])
            E.reg_add(exp, exp, 16)
            E.dma_start(out=dsti, in_=srci).then_inc(psem, 16)
            E.wait_ge(psem, _msv(exp, min_val=0, max_val=1 << 26))
    P.raw("sync", closure, waits=pre_waits)
    return P.op("sync", lambda E: E.nop(), inc=P.chan("mb_" + tag), inc_k=1,
                chain=False)


def build_fused(t_core):
    decls = dict(CONST_DECL)
    decls.update({
        "xtm": ((t_core, D), F32, "ExternalInput"),
        "xout_tm": ((t_core, D), F32, "ExternalOutput"),
        "w1t": ((8, D, 2048), F32, "ExternalInput"),
        "w2t": ((8, 2048, D), F32, "ExternalInput"),
        "b1s": ((128, 128), F32, "ExternalInput"),
        "b2s": ((8, D), F32, "ExternalInput"),
        "nws": ((128, NCH), F32, "ExternalInput"),
        "gwt": ((128, 64), F32, "ExternalInput"),
        "selrows": ((8, 1024), F32, "ExternalInput"),
        "l0wt": ((D, D), F32, "ExternalInput"),
        "l0b": ((128, NCH), F32, "ExternalInput"),
        "l0g": ((128, NCH), F32, "ExternalInput"),
        "l0bt": ((128, NCH), F32, "ExternalInput"),
        "m1tri": ((128, 128), F32, "ExternalInput"),
        "tri01": ((128, 128), F32, "ExternalInput"),
        "iota_row": ((1, 4096), F32, "ExternalInput"),
        "decay": ((1, 1), F32, "ExternalInput"),
        "selcol": ((128, 1), F32, "ExternalInput"),
        "f1t": ((D, 4096), F32, "ExternalInput"),
        "f2t": ((D, 4096), F32, "ExternalInput"),
        "h1t": ((4096, 4096), F32, "ExternalInput"),
        "h2t": ((4096, 4096), F32, "ExternalInput"),
        "f3t": ((4096, D), F32, "ExternalInput"),
        "s1s": ((128, 32), F32, "ExternalInput"),
        "s2s": ((128, 32), F32, "ExternalInput"),
        "sh1": ((128, 32), F32, "ExternalInput"),
        "sh2": ((128, 32), F32, "ExternalInput"),
        "s3s": ((128, 8), F32, "ExternalInput"),
        "maskA0": ((128, 384), F32, "ExternalInput"),
        "maskA1": ((128, 384), F32, "ExternalInput"),
        "maskB": ((128, 384), F32, "ExternalInput"),
        "qwt": ((D, D), F32, "ExternalInput"),
        "kwt": ((D, D), F32, "ExternalInput"),
        "vwt": ((D, D), F32, "ExternalInput"),
        "owt": ((D, D), F32, "ExternalInput"),
        "qb": ((128, NCH), F32, "ExternalInput"),
        "kb": ((128, NCH), F32, "ExternalInput"),
        "vbrow": ((1, D), F32, "ExternalInput"),
        "obias": ((128, NCH), F32, "ExternalInput"),
        "mbctl": ((1, 8), I32, "ExternalInput"),
    })

    import os
    stages = set(os.environ.get(
        "MK_STAGES", "pre,s1,s2,s3,s4,s5,s6,post,mbc,mbh").split(","))

    def emit(P, cm, aps, es, nc):
        xres = nc.dram_tensor("xres", [D, t_core], F32).ap()
        u4 = nc.dram_tensor("u4", [D, t_core], BF16).ap()
        ha = nc.dram_tensor("ha", [4096, t_core], BF16).ap()
        hb = nc.dram_tensor("hb", [4096, t_core], BF16).ap()
        halo_loc = nc.dram_tensor("halo_loc", [D, 256], F32).ap()
        mcarry = nc.dram_tensor("mcarry", [2, 128, NCH], F32, addr_space="Shared")
        mhalo = nc.dram_tensor("mhalo", [2, D, 256], F32, addr_space="Shared")
        mflagc = nc.dram_tensor("mflagc", [2, 16], I32, addr_space="Shared")
        mflagh = nc.dram_tensor("mflagh", [2, 16], I32, addr_space="Shared")
        mb = {"flag_c": mflagc, "flag_h": mflagh, "ctl_ap": aps["mbctl"]}
        mb["psem_c"] = es.enter_context(nc.semaphore("poll_c"))
        mb["psem_h"] = es.enter_context(nc.semaphore("poll_h"))
        mb["ctl_sb"] = es.enter_context(nc.sbuf_tensor("mb_ctl", [1, 8], I32))
        mb["ftmp"] = es.enter_context(nc.sbuf_tensor("mb_ftmp", [1, 1], I32))
        selcol = es.enter_context(nc.sbuf_tensor("mb_selcol", [128, 1], F32))
        ev_sel = P.dma("sync", "c_const", selcol[:, :], aps["selcol"])
        _emit_mb_init(P, mb)
        eps_io = {"eps_ln": cm["fconsts"][0:1, 0:1],
                  "eps_rms": cm["fconsts"][0:1, 1:2]}

        if "pre" in stages:
            with ExitStack() as ses:
                sbp = alloc_pp(nc, ses, t_core, "_f0")
                emit_pre(P, sbp, cm, {"xin": aps["xtm"], "xout": xres}, t_core)
                P.barrier_all()
        io1 = {"xin": xres, "xout": xres, "w1t": aps["w1t"], "w2t": aps["w2t"],
               "b1s": aps["b1s"], "b2s": aps["b2s"], "nws": aps["nws"],
               "gwt": aps["gwt"], "selrows": aps["selrows"]}
        if "s1" in stages:
            with ExitStack() as ses:
                sb1 = alloc_s1(nc, ses, t_core, "_f1")
                sb1.update(eps_io)
                emit_s1(P, sb1, cm, io1, t_core)
                P.barrier_all()
        io2 = {"xin": xres, "xout": xres, "l0wt": aps["l0wt"], "l0b": aps["l0b"],
               "l0g": aps["l0g"], "l0bt": aps["l0bt"], "m1tri": aps["m1tri"],
               "tri01": aps["tri01"], "iota_row": aps["iota_row"],
               "decay": aps["decay"]}
        if "s2" in stages:
            with ExitStack() as ses:
                sb2 = alloc_s2(nc, ses, t_core, "_f2")
                sb2.update(eps_io)
                cin_hold = ses.enter_context(
                    nc.sbuf_tensor("mb_cin", [128, NCH], F32))
                if "mbc" in stages:
                    st2 = emit_s2(P, sb2, cm, io2, t_core, carry_only=True)
                    cfin = st2["cfin"]
                    ev_cin = _emit_exchange(
                        P, mb, "c",
                        lambda my: (mcarry.ap()[bass.ds(my, 1), :, :], cfin[:, :]),
                        lambda ot: (cin_hold[:, :], mcarry.ap()[bass.ds(ot, 1), :, :]),
                        pre_waits=[("vector", st2["cfin_ev"]), ("c_const", ev_sel)])
                    P.barrier_all()
                    ev_cinm = P.op("vector", lambda E: E.tensor_scalar(
                        sb2["cin"][:, :], cin_hold[:, :], selcol[:, 0:1], None,
                        op0=ALU.mult), waits=[("mb_c", ev_cin)])
                else:
                    P.op("vector", lambda E: E.memset(sb2["cin"][:, :], 0.0))
                emit_s2(P, sb2, cm, io2, t_core, cin_ev=None)
                P.barrier_all()
        if "s3" in stages:
            with ExitStack() as ses:
                sb3 = alloc_s1(nc, ses, t_core, "_f3")
                sb3.update(eps_io)
                emit_s1(P, sb3, cm, io1, t_core)
                P.barrier_all()
        io4 = {"xin": xres, "xout": xres, "f1t": aps["f1t"], "f2t": aps["f2t"],
               "h1t": aps["h1t"], "h2t": aps["h2t"], "f3t": aps["f3t"],
               "s1s": aps["s1s"], "s2s": aps["s2s"], "sh1": aps["sh1"],
               "sh2": aps["sh2"], "s3s": aps["s3s"], "u4": u4, "ha": ha,
               "hb": hb}
        if "s4" in stages:
            with ExitStack() as ses:
                sb4 = alloc_s4(nc, ses, t_core, "_f4")
                sb4.update(eps_io)
                emit_s4(P, sb4, cm, io4, t_core)
                P.barrier_all()
        if "mbh" in stages:
            ev_halo = _emit_exchange(
                P, mb, "h",
                lambda my: (mhalo.ap()[bass.ds(my, 1), :, :],
                            xres[:, t_core - 256:t_core]),
                lambda ot: (halo_loc[:, :], mhalo.ap()[bass.ds(ot, 1), :, :]),
                pre_waits=[])
        else:
            ev_fill = P.dma("sync", "mb_hl", halo_loc[:, :], xres[:, 0:256])
            ev_halo = P.op("sync", lambda E: E.nop(),
                           waits=[("mb_hl", ev_fill)], inc=P.chan("mb_h"),
                           inc_k=1, chain=False)
        io5 = {"xin": xres, "xout": xres, "halo_x": halo_loc,
               "maskA0": aps["maskA0"], "maskA1": aps["maskA1"],
               "maskB": aps["maskB"], "qwt": aps["qwt"], "kwt": aps["kwt"],
               "vwt": aps["vwt"], "owt": aps["owt"], "qb": aps["qb"],
               "kb": aps["kb"], "vbrow": aps["vbrow"], "obias": aps["obias"]}
        if "s5" in stages:
            with ExitStack() as ses:
                sb5 = alloc_s5(nc, ses, t_core, "_f5")
                sb5.update(eps_io)
                P.op("vector", lambda E: E.engine_nop(), waits=[("mb_h", ev_halo)])
                emit_s5(P, sb5, cm, io5, t_core)
                P.barrier_all()
        if "s6" in stages:
            with ExitStack() as ses:
                sb6 = alloc_s4(nc, ses, t_core, "_f6")
                sb6.update(eps_io)
                emit_s4(P, sb6, cm, io4, t_core)
                P.barrier_all()
        with ExitStack() as ses:
            sbq = alloc_pp(nc, ses, t_core, "_f7")
            emit_pre(P, sbq, cm, {"xin": xres, "xout": aps["xout_tm"]},
                     t_core, reverse=True)
            P.barrier_all()
    return build_launch(decls, emit, t_core)


def kernel_fused(**inputs):
    g = prep_host(inputs)
    t_core = 4096
    x_full = np.asarray(inputs["L0_attn_Z"], np.float32)
    B = x_full.shape[0]
    nc = _prog_fused(t_core)
    nonce_c = int(np.random.randint(1, 1 << 30))
    nonce_h = int(np.random.randint(1, 1 << 30))
    consts = {"identf": g["identf"], "ones_row": g["ones_row"],
              "ones_col": g["ones_col"],
              "fconsts": np.array([[LN_EPS, RMS_EPS, 0, 0, 0, 0, 0, 0]], np.float32)}
    in_maps = []
    for c in range(NCORES):
        b, h = c // 2, c % 2
        first = (h == 0)
        mA0, mA1, mB = make_masks(first)
        in_maps.append({
            **consts,
            "xtm": np.ascontiguousarray(x_full[b, h * t_core:(h + 1) * t_core, :]),
            "w1t": g["w1t"], "w2t": g["w2t"], "b1s": g["b1s"], "b2s": g["b2s"],
            "nws": g["nws"], "gwt": g["gwt"], "selrows": g["selrows"],
            "l0wt": g["l0wt"], "l0b": g["l0b"], "l0g": g["l0g"],
            "l0bt": g["l0bt"], "m1tri": g["m1tri"], "tri01": g["tri01"],
            "iota_row": g["iota_row"], "decay": g["decay"],
            "selcol": np.full((128, 1), 0.0 if first else 1.0, np.float32),
            "f1t": g["f1t"], "f2t": g["f2t"], "h1t": g["h1t"], "h2t": g["h2t"],
            "f3t": g["f3t"], "s1s": g["s1s"], "s2s": g["s2s"], "sh1": g["sh1"],
            "sh2": g["sh2"], "s3s": g["s3s"],
            "maskA0": mA0, "maskA1": mA1, "maskB": mB,
            "qwt": g["qwt"], "kwt": g["kwt"], "vwt": g["vwt"], "owt": g["owt"],
            "qb": g["qb"], "kb": g["kb"],
            "vbrow": np.ascontiguousarray(
                np.asarray(inputs["L1_attn_v_b"], np.float32)[None, :]),
            "obias": g["ob"],
            "mbctl": np.array([[h, 1 - h, nonce_c, nonce_h,
                                1 if h == 0 else 0,
                                h if h == 0 else 1 - h, 0, 0]], np.int32),
        })
    res = run_bass_kernel_spmd(nc, in_maps, core_ids=list(range(NCORES)))
    full = np.empty((B, 2 * t_core, D), np.float32)
    for c in range(NCORES):
        b, h = c // 2, c % 2
        full[b, h * t_core:(h + 1) * t_core, :] = res.results[c]["xout_tm"]
    return full


def _prog_fused(t_core):
    key = ("fused", t_core)
    if key not in _PROG_CACHE:
        _PROG_CACHE[key] = build_fused(t_core)
    return _PROG_CACHE[key]


# version 20
# speedup vs baseline: 1.0607x; 1.0607x over previous
import numpy as np
from contextlib import ExitStack

import concourse.bass as bass
import concourse.mybir as mybir
from concourse.bass_utils import run_bass_kernel_spmd

F32 = mybir.dt.float32
BF16 = mybir.dt.bfloat16
I32 = mybir.dt.int32
AF = mybir.ActivationFunctionType
ALU = mybir.AluOpType
AX = mybir.AxisListType

D = 1024
NCH = D // 128
NCORES = 8
LN_EPS = 1e-5
RMS_EPS = 1e-6
NEG_BIG = -1.0e30

ENGS = ("sync", "tensor", "vector", "scalar", "gpsimd")


class Prog:

    def __init__(self, nc):
        self.nc = nc
        self.q = {e: [] for e in ENGS}
        self.cnt = {}
        self.sems = {}
        self._es = ExitStack()
        for name in ("tensor", "vector", "scalar", "gpsimd"):
            self.sems[name] = self._es.enter_context(nc.semaphore("sem_" + name))
            self.cnt[name] = 0
        self._waited = {}

    def chan(self, name):
        if name not in self.sems:
            self.sems[name] = self._es.enter_context(self.nc.semaphore("ch_" + name))
            self.cnt[name] = 0
        return name

    def _push_wait(self, eng, sem_name, val):
        if val is None or val <= 0:
            return
        key = (eng, sem_name)
        if self._waited.get(key, -1) >= val:
            return
        self._waited[key] = val
        h = self.sems[sem_name]
        self.q[eng].append(lambda E, h=h, v=val: E.wait_ge(h, v))

    def op(self, eng, build, waits=(), inc="auto", inc_k=1, chain=True):
        if inc == "auto":
            inc = eng if eng in ("vector", "scalar", "gpsimd") else None
        if chain and eng in ("vector", "scalar", "gpsimd") and self.cnt[eng] > 0:
            self._push_wait(eng, eng, self.cnt[eng])
        for sem_name, val in waits:
            self._push_wait(eng, sem_name, val)
        ev = None
        h = None
        if inc is not None:
            self.cnt[inc] += inc_k
            ev = self.cnt[inc]
            h = self.sems[inc]

        def closure(E, build=build, h=h, k=inc_k):
            inst = build(E)
            if h is not None:
                inst.then_inc(h, k)
        self.q[eng].append(closure)
        return ev

    def mm(self, build, waits=(), inc=False):
        return self.op("tensor", build, waits=waits,
                       inc=("tensor" if inc else None), chain=False)

    def dma(self, queue, chan, out, in_, waits=()):
        self.chan(chan)
        waits = list(waits) + [(chan, self.cnt[chan])]
        return self.op(queue, lambda E, o=out, i=in_: E.dma_start(out=o, in_=i),
                       waits=waits, inc=chan, inc_k=16, chain=False)

    def raw(self, eng, fn, waits=()):
        for sem_name, val in waits:
            self._push_wait(eng, sem_name, val)
        self.q[eng].append(fn)

    def barrier_all(self):
        snap = dict(self.cnt)
        for eng in ENGS:
            for sem_name, val in snap.items():
                self._push_wait(eng, sem_name, val)

    def ev(self, sem_name):
        return self.cnt[sem_name]

    def finalize(self, block):
        block.sync(lambda E: [f(E) for f in self.q["sync"]])
        block.tensor(lambda E: [f(E) for f in self.q["tensor"]])
        block.vector(lambda E: [f(E) for f in self.q["vector"]])
        block.scalar(lambda E: [f(E) for f in self.q["scalar"]])
        block.gpsimd(lambda E: [f(E) for f in self.q["gpsimd"]])

    def close(self):
        self._es.close()


def bc8(ap2d, n, w):
    return ap2d.rearrange("p (a t) -> p a t", a=1).broadcast_to([128, n, w])


def v8(tile, n, w):
    return tile[:, 0:n * w].rearrange("p (a t) -> p a t", a=n)


def ccol(tile, c, width, sl=None):
    if sl is None:
        return tile[:, c * width:(c + 1) * width]
    start, stop = sl
    return tile[:, c * width + start: c * width + stop]



def _feat_tile(v):
    return np.ascontiguousarray(v.reshape(-1, 128).T)


def prep_host(inputs):
    g = {}
    g["w1t"] = np.ascontiguousarray(inputs["L0_ffn_w1"].transpose(0, 2, 1))
    g["w2t"] = np.ascontiguousarray(inputs["L0_ffn_w2"].transpose(0, 2, 1))
    b1 = np.asarray(inputs["L0_ffn_b1"])
    g["b1s"] = np.ascontiguousarray(
        b1.reshape(8, 16, 128).transpose(2, 0, 1).reshape(128, 128))
    g["b2s"] = np.ascontiguousarray(np.asarray(inputs["L0_ffn_b2"]))
    g["nws"] = _feat_tile(np.asarray(inputs["L0_ffn_norm_weight"]))
    gwT = np.asarray(inputs["L0_ffn_gate_w"]).T
    g["gwt"] = np.ascontiguousarray(gwT.reshape(8, 128, 8).transpose(1, 0, 2).reshape(128, 64))
    g["l0wt"] = np.ascontiguousarray(np.asarray(inputs["L0_attn_W"]).T)
    g["l0b"] = _feat_tile(np.asarray(inputs["L0_attn_b"]))
    g["l0g"] = _feat_tile(np.asarray(inputs["L0_attn_gamma"]))
    g["l0bt"] = _feat_tile(np.asarray(inputs["L0_attn_beta"]))
    g["decay"] = np.asarray(inputs["L0_attn_decay"], np.float32).reshape(1, 1)
    for k in ("q", "k", "v", "o"):
        g[f"{k}wt"] = np.ascontiguousarray(np.asarray(inputs[f"L1_attn_{k}_w"]).T)
        g[f"{k}b"] = _feat_tile(np.asarray(inputs[f"L1_attn_{k}_b"]))
    g["f1t"] = np.ascontiguousarray(np.asarray(inputs["L1_ffn_w1_w"]).T)
    g["f2t"] = np.ascontiguousarray(np.asarray(inputs["L1_ffn_w2_w"]).T)
    g["h1t"] = np.ascontiguousarray(np.asarray(inputs["L1_ffn_h1_w"]).T)
    g["h2t"] = np.ascontiguousarray(np.asarray(inputs["L1_ffn_h2_w"]).T)
    g["f3t"] = np.ascontiguousarray(np.asarray(inputs["L1_ffn_w3_w"]).T)
    g["s1s"] = _feat_tile(np.asarray(inputs["L1_ffn_w1_s"]))
    g["s2s"] = _feat_tile(np.asarray(inputs["L1_ffn_w2_s"]))
    g["sh1"] = _feat_tile(np.asarray(inputs["L1_ffn_h1_s"]))
    g["sh2"] = _feat_tile(np.asarray(inputs["L1_ffn_h2_s"]))
    g["s3s"] = _feat_tile(np.asarray(inputs["L1_ffn_w3_s"]))
    g["identf"] = np.eye(128, dtype=np.float32)
    tj = np.arange(128)[None, :] - np.arange(128)[:, None]
    tri01 = (tj >= 0).astype(np.float32)
    g["m1tri"] = (tj * tri01).astype(np.float32)
    g["tri01"] = tri01
    g["iota_row"] = (np.arange(4096, dtype=np.float32) + 1.0)[None, :]
    g["ones_col"] = np.ones((128, 1), np.float32)
    g["ones_row"] = np.ones((1, 128), np.float32)
    g["selrows"] = np.kron(np.eye(8, dtype=np.float32), np.ones((1, 128), np.float32))
    return g


def shard_x(x):
    out = []
    for c in range(NCORES):
        b, h = c // 2, c % 2
        out.append(np.ascontiguousarray(np.asarray(x)[b, h * 4096:(h + 1) * 4096, :]))
    return out


def fm_view(hbm_ap):
    return hbm_ap.rearrange("(c p) t -> p c t", p=128)



def emit_ln_stats(P, sb, xt, TT, pre_waits, eps_ap, extra_rms=False):
    P.op("scalar", lambda E: E.activation(
        sb["xq"][:, 0:NCH * TT], xt[:, 0:NCH * TT], AF.Copy), waits=pre_waits)
    ev_sc = P.op("scalar", lambda E: E.activation(
        sb["sq"][:, 0:NCH * TT], xt[:, 0:NCH * TT], AF.Square))
    ev_mm = None
    for c in range(NCH):
        P.mm(lambda E, c=c: E.matmul(
            sb["pst1"][0:1, 0:TT], sb["onesb"][:, :], ccol(sb["xq"], c, TT),
            start=(c == 0), stop=(c == NCH - 1)),
            waits=[("scalar", ev_sc)] + list(sb.get("stat_war", []))
                  + list(sb.get("const_waits", [])))
    for c in range(NCH):
        ev_mm = P.mm(lambda E, c=c: E.matmul(
            sb["pst2"][0:1, 0:TT], sb["onesb"][:, :], ccol(sb["sq"], c, TT),
            start=(c == 0), stop=(c == NCH - 1)),
            inc=(c == NCH - 1))
    rows = sb["rows"]
    mu = rows[0:1, 0:TT]
    musq = rows[0:1, TT:2 * TT]
    var = rows[0:1, 2 * TT:3 * TT]
    P.op("vector", lambda E: E.tensor_scalar(
        mu, sb["pst1"][0:1, 0:TT], 1.0 / D, None, op0=ALU.mult),
        waits=[("tensor", ev_mm)])
    P.op("vector", lambda E: E.tensor_mul(musq, mu, mu))
    ev_var = P.op("vector", lambda E: E.scalar_tensor_tensor(
        var, sb["pst2"][0:1, 0:TT], 1.0 / D, musq, op0=ALU.mult, op1=ALU.subtract))
    lnv = rows[0:1, 3 * TT:4 * TT]
    rstd = rows[0:1, 4 * TT:5 * TT]
    ev1 = P.op("scalar", lambda E: E.activation(lnv, var, AF.Ln, bias=eps_ap),
               waits=[("vector", ev_var)] + list(sb.get("const_waits", [])))
    ev_rstd = P.op("scalar", lambda E: E.activation(rstd, lnv, AF.Exp, scale=-0.5))
    out = {"mu": mu, "var": var, "rstd": rstd, "ev_rstd": ev_rstd,
           "ev_var": ev_var, "ev_stats_mm": ev_mm, "ev_rows_v": ev_var}
    if extra_rms:
        vr = rows[0:1, 5 * TT:6 * TT]
        rr = rows[0:1, 6 * TT:7 * TT]
        P.op("vector", lambda E: E.tensor_mul(vr, var, rstd),
             waits=[("scalar", ev_rstd)])
        ev3 = P.op("vector", lambda E: E.tensor_mul(vr, vr, rstd))
        ev4 = P.op("scalar", lambda E: E.activation(vr, vr, AF.Ln, bias=sb["eps_rms"]),
                   waits=[("vector", ev3)])
        P.op("scalar", lambda E: E.activation(vr, vr, AF.Exp, scale=-0.5))
        ev_rr = P.op("vector", lambda E: E.tensor_mul(rr, rstd, vr),
                     waits=[("scalar", P.ev("scalar"))])
        out["rr"] = rr
        out["ev_rr"] = ev_rr
    return out


def emit_bcast(P, ps_dst, row_ap, sb, TT, waits):
    return P.mm(lambda E: E.matmul(ps_dst[:, 0:TT], sb["ones_rowf"][:, :],
                                   row_ap, start=True, stop=True),
                waits=waits, inc=True)



def alloc_s1(nc, es, T, sfx=""):
    H2 = min(1024, T // 2)
    TT = min(512, H2)
    sb = {"H2": H2, "TT": TT, "NT": H2 // TT, "NG": T // H2}
    sb["tb"] = es.enter_context(nc.sbuf_tensor("s1_tb" + sfx, [128, NCH * H2], BF16))
    sb["accs"] = es.enter_context(nc.sbuf_tensor("s1_acc" + sfx, [128, NCH * H2], F32))
    sb["selrows"] = es.enter_context(nc.sbuf_tensor("s1_selrows" + sfx, [8, 1024], F32))
    sb["ctpack"] = es.enter_context(nc.sbuf_tensor("s1_ctpack" + sfx, [8, H2], F32))
    sb["w1sb"] = es.enter_context(nc.sbuf_tensor("s1_w1" + sfx, [128, NCH * 2048], BF16))
    sb["w2sb"] = es.enter_context(nc.sbuf_tensor("s1_w2" + sfx, [128, 16 * 1024], BF16))
    sb["b1sb"] = es.enter_context(nc.sbuf_tensor("s1_b1" + sfx, [128, 128], F32))
    sb["b2sb"] = es.enter_context(nc.sbuf_tensor("s1_b2" + sfx, [8, 1024], F32))
    sb["nwsb"] = es.enter_context(nc.sbuf_tensor("s1_nw" + sfx, [128, NCH], F32))
    sb["gwsb"] = es.enter_context(nc.sbuf_tensor("s1_gw" + sfx, [128, 64], BF16))
    sb["xt"] = es.enter_context(nc.sbuf_tensor("s1_xt" + sfx, [128, NCH * TT], F32))
    sb["xq"] = es.enter_context(nc.sbuf_tensor("s1_xq" + sfx, [128, NCH * TT], BF16))
    sb["sq"] = es.enter_context(nc.sbuf_tensor("s1_sq" + sfx, [128, NCH * TT], BF16))
    sb["sc"] = es.enter_context(nc.sbuf_tensor("s1_sc" + sfx, [128, TT], F32))
    sb["uc"] = es.enter_context(nc.sbuf_tensor("s1_uc" + sfx, [128, TT], F32))
    sb["ot"] = es.enter_context(nc.sbuf_tensor("s1_ot" + sfx, [128, TT], F32))
    sb["cbc"] = es.enter_context(nc.sbuf_tensor("s1_cbc" + sfx, [128, TT], F32))
    sb["he"] = es.enter_context(nc.sbuf_tensor("s1_he" + sfx, [128, 16 * TT], BF16))
    sb["rows"] = es.enter_context(nc.sbuf_tensor("s1_rows" + sfx, [1, 8 * TT], F32))
    sb["ctm"] = es.enter_context(nc.sbuf_tensor("s1_ctm" + sfx, [128, 8], F32))
    sb["gsc"] = es.enter_context(nc.sbuf_tensor("s1_gsc" + sfx, [128, 16], F32))
    for nm in ("pst1", "pst2", "pbc1", "pbc2", "ph0", "ph1", "po0", "po1"):
        sb[nm] = es.enter_context(nc.psum_tensor("s1_" + sfx + nm, [128, 512], F32))
    return sb


def emit_s1(P, sb, cm, io, T, tag=""):
    H2, TT, NT, NG = sb["H2"], sb["TT"], sb["NT"], sb["NG"]
    sb["onesb"] = cm["ones_colb"]
    sb["ones_rowf"] = cm["ones_rowf"]
    xin3 = fm_view(io["xin"])
    xout3 = fm_view(io["xout"])

    P.dma("sync", "c_const", sb["b1sb"][:, :], io["b1s"])
    P.dma("sync", "c_const", sb["b2sb"][:, :], io["b2s"])
    P.dma("sync", "c_const", sb["nwsb"][:, :], io["nws"])
    P.dma("sync", "c_const", sb["selrows"][:, :], io["selrows"])
    P.dma("gpsimd", "c_cg", sb["gwsb"][:, :].rearrange("p (c e) -> p c e", c=NCH),
          io["gwt"].rearrange("p (c e) -> p c e", c=NCH))
    cw = [("c_const", P.ev("c_const")), ("c_cg", P.ev("c_cg"))]
    sb["const_waits"] = cw

    def load_w1(e, waits):
        return P.dma("gpsimd", "s1_w1",
                     sb["w1sb"][:, :].rearrange("p (c m) -> p c m", c=NCH),
                     io["w1t"][e].rearrange("(c p) m -> p c m", p=128), waits=waits)

    def load_w2(e, waits):
        return P.dma("gpsimd", "s1_w2",
                     sb["w2sb"][:, :].rearrange("p (c m) -> p c m", c=16),
                     io["w2t"][e].rearrange("(c p) m -> p c m", p=128), waits=waits)

    ev_w1, ev_w2 = load_w1(0, []), load_w2(0, [])

    st = {"last_stats_mm": 0, "last_rows_v": 0, "xt_free_v": 0, "xt_free_s": 0,
          "xt_free_dma": 0, "rows_free_t": 0}

    for h in range(NG):
        half_off = h * H2
        ev_accz = P.op("gpsimd", lambda E: E.memset(sb["accs"][:, :], 0.0),
                       waits=[("vector", st.get("accs_read_v", 0))])
        evA_tb = 0
        evA_ct = 0
        ev_x_tiles = []
        for i in range(NT):
            toff = half_off + i * TT
            ev_x = P.dma("sync", "s1_x",
                         sb["xt"][:, :].rearrange("p (c t) -> p c t", c=NCH),
                         xin3[:, :, toff:toff + TT],
                         waits=[("vector", st["xt_free_v"]), ("scalar", st["xt_free_s"]),
                                ("s1_xo", st.get("xout_done", 0))])
            ev_x_tiles.append(ev_x)
            sb["stat_war"] = [("vector", st["last_rows_v"])]
            pre = [("s1_x", ev_x), ("tensor", st["last_stats_mm"]),
                   ("tensor", st["rows_free_t"])]
            si = emit_ln_stats(P, sb, sb["xt"], TT, pre, sb["eps_ln"], extra_rms=True)
            st["last_stats_mm"] = si["ev_stats_mm"]
            st["last_rows_v"] = si["ev_var"]
            ev_b1 = emit_bcast(P, sb["pbc1"], si["mu"], sb, TT,
                               waits=[("vector", si["ev_var"]),
                                      ("vector", st.get("pbc1_free", 0))] + cw)
            ev_b2 = emit_bcast(P, sb["pbc2"], si["rr"], sb, TT,
                               waits=[("vector", si["ev_rr"]),
                                      ("vector", st.get("pbc2_free", 0))])
            st["rows_free_t"] = ev_b2
            P.op("vector", lambda E: E.tensor_tensor(
                v8(sb["xt"], NCH, TT), v8(sb["xt"], NCH, TT),
                bc8(sb["pbc1"][:, 0:TT], NCH, TT), op=ALU.subtract),
                waits=[("tensor", ev_b2)])
            for c in range(NCH):
                ev_tb = P.op("vector", lambda E, c=c, o=c * H2 + i * TT: E.scalar_tensor_tensor(
                    sb["tb"][:, o: o + TT],
                    ccol(sb["xt"], c, TT), sb["nwsb"][:, c:c + 1], sb["pbc2"][:, 0:TT],
                    op0=ALU.mult, op1=ALU.mult),
                    waits=[("tensor", st.get("tb_free", 0))])
            st["pbc1_free"] = ev_tb
            st["pbc2_free"] = ev_tb
            st["xt_free_v"] = ev_tb
            st["xt_free_s"] = P.ev("scalar")
            evA_tb = ev_tb
            for s_ in range(TT // 128):
                suba = i * TT + s_ * 128
                lg = sb["po0"][:, s_ * 8:(s_ + 1) * 8]
                for c in range(NCH):
                    ev_lg = P.mm(lambda E, c=c, s_=s_, lg=lg, suba=suba: E.matmul(
                        lg, sb["tb"][:, c * H2 + suba: c * H2 + suba + 128],
                        sb["gwsb"][:, c * 8:(c + 1) * 8],
                        start=(c == 0), stop=(c == NCH - 1)),
                        waits=[("vector", ev_tb),
                               ("vector", st.get("po0_free", 0)),
                               ("scalar", st.get("po0_free_s", 0))] + cw,
                        inc=(c == NCH - 1))
                g = sb["gsc"]
                negm1, m1, m2, dm, em, den = (g[:, 0:1], g[:, 1:2], g[:, 2:3],
                                              g[:, 3:4], g[:, 4:5], g[:, 5:6])
                ge2 = sb["ctm"]
                P.op("vector", lambda E, lg=lg: E.reduce_max(
                    negm1, lg, axis=AX.X, negate=True), waits=[("tensor", ev_lg)])
                P.op("vector", lambda E, lg=lg: E.reduce_max(m1, lg, axis=AX.X))
                P.op("vector", lambda E, lg=lg: E.tensor_scalar(
                    sb["ctm"][:, :], lg, m1, None, op0=ALU.is_ge))
                P.op("vector", lambda E, lg=lg: E.scalar_tensor_tensor(
                    sb["ctm"][:, :], sb["ctm"][:, :], NEG_BIG, lg,
                    op0=ALU.mult, op1=ALU.add))
                P.op("vector", lambda E: E.reduce_max(m2, sb["ctm"][:, :], axis=AX.X))
                P.op("vector", lambda E: E.tensor_sub(dm, m2, m1))
                ev_v = P.op("vector", lambda E, lg=lg: E.tensor_scalar(
                    sb["ctm"][:, :], lg, m2, None, op0=ALU.is_ge))
                ev_em = P.op("scalar", lambda E: E.activation(em, dm, AF.Exp),
                             waits=[("vector", ev_v)])
                ev_num = P.op("scalar", lambda E, lg=lg, s_=s_: E.activation(
                    sb["gsc"][:, 8:16], lg, AF.Exp, bias=negm1))
                P.op("vector", lambda E: E.tensor_scalar(
                    den, em, 1.0, None, op0=ALU.add), waits=[("scalar", ev_em)])
                P.op("vector", lambda E: E.reciprocal(den, den))
                P.op("vector", lambda E: E.tensor_mul(
                    sb["ctm"][:, :], sb["ctm"][:, :], sb["gsc"][:, 8:16]),
                    waits=[("scalar", ev_num)])
                ev_ctm = P.op("vector", lambda E: E.tensor_scalar(
                    sb["ctm"][:, :], sb["ctm"][:, :], den, None, op0=ALU.mult))
                st["po0_free"] = ev_ctm
                st["po0_free_s"] = ev_num
                ev_tp = P.mm(lambda E: E.transpose(
                    sb["po1"][0:8, 0:128], sb["ctm"][:, :], cm["identf"][:, :]),
                    waits=[("vector", ev_ctm),
                           ("vector", st.get("po1_free", 0))] + cw, inc=True)
                ev_cp = P.op("vector", lambda E, suba=suba: E.tensor_copy(
                    sb["ctpack"][:, suba:suba + 128], sb["po1"][0:8, 0:128]),
                    waits=[("tensor", ev_tp), ("tensor", st.get("ctpack_free", 0))])
                st["po1_free"] = ev_cp
                evA_ct = ev_cp
        ph_evac = {}
        po_cons = {}
        for e in range(8):
            ev_w1f = None
            ev_w2f = None
            for i in range(NT):
                toff_h = i * TT
                ev_cb = P.mm(lambda E, e=e, toff_h=toff_h: E.matmul(
                    sb["pbc2"][:, 0:TT], sb["selrows"][:, e * 128:(e + 1) * 128],
                    sb["ctpack"][:, toff_h:toff_h + TT],
                    start=True, stop=True),
                    waits=[("vector", evA_ct), ("vector", st.get("pbc2_free", 0))] + cw,
                    inc=True)
                ev_cbsb = P.op("vector", lambda E: E.tensor_copy(
                    sb["cbc"][:, 0:TT], sb["pbc2"][:, 0:TT]),
                    waits=[("tensor", ev_cb)])
                st["pbc2_free"] = ev_cbsb
                ev_he_last = None
                for hc in range(16):
                    bank = sb["ph0"] if hc % 2 == 0 else sb["ph1"]
                    gidx = ((h * 8 + e) * NT + i) * 16 + hc
                    w_first = [("s1_w1", ev_w1), ("vector", evA_tb),
                               ("scalar", ph_evac.get(gidx - 2, 0))]
                    for c in range(NCH):
                        ev_hmm = P.mm(lambda E, hc=hc, c=c, bank=bank,
                                      toff_h=toff_h: E.matmul(
                            bank[:, 0:TT],
                            sb["w1sb"][:, c * 2048 + hc * 128: c * 2048 + hc * 128 + 128],
                            sb["tb"][:, c * H2 + toff_h: c * H2 + toff_h + TT],
                            start=(c == 0), stop=(c == NCH - 1)),
                            waits=(w_first if c == 0 else []),
                            inc=(c == NCH - 1))
                    ev_he_last = P.op("scalar", lambda E, hc=hc, bank=bank, e=e:
                                      E.activation(
                        ccol(sb["he"], hc, TT), bank[:, 0:TT], AF.Gelu,
                        bias=sb["b1sb"][:, e * 16 + hc: e * 16 + hc + 1]),
                        waits=[("tensor", ev_hmm),
                               ("tensor", st.get("he_free", 0))])
                    ph_evac[gidx] = ev_he_last
                    if i == NT - 1 and hc == 15:
                        ev_w1f = ev_hmm
                for c in range(NCH):
                    bidx = c % 2
                    bank = sb["po0"] if bidx == 0 else sb["po1"]
                    o_first = [("s1_w2", ev_w2), ("scalar", ev_he_last),
                               ("vector", po_cons.get(bidx, 0)),
                               ("scalar", st.get("po_free_s", 0))]
                    for hc in range(16):
                        ev_omm = P.mm(lambda E, c=c, hc=hc, bank=bank: E.matmul(
                            bank[:, 0:TT],
                            sb["w2sb"][:, hc * 1024 + c * 128: hc * 1024 + c * 128 + 128],
                            ccol(sb["he"], hc, TT),
                            start=(hc == 0), stop=(hc == 15)),
                            waits=(o_first if hc == 0 else []),
                            inc=(hc == 15))
                    P.op("vector", lambda E, bank=bank: E.tensor_mul(
                        sb["ot"][:, 0:TT], bank[:, 0:TT], sb["cbc"][:, 0:TT]),
                        waits=[("tensor", ev_omm), ("vector", ev_cbsb),
                               ("gpsimd", ev_accz)])
                    ev_acc = P.op("vector", lambda E, c=c, toff_h=toff_h: E.tensor_add(
                        sb["accs"][:, c * H2 + toff_h: c * H2 + toff_h + TT],
                        sb["accs"][:, c * H2 + toff_h: c * H2 + toff_h + TT],
                        sb["ot"][:, 0:TT]))
                    po_cons[bidx] = ev_acc
                    if i == NT - 1 and c == NCH - 1:
                        ev_w2f = ev_omm
                st["he_free"] = ev_omm
            ne = e + 1 if e < 7 else (0 if h < NG - 1 else None)
            if ne is not None:
                ev_w1 = load_w1(ne, [("tensor", ev_w1f)])
                ev_w2 = load_w2(ne, [("tensor", ev_w2f)])
        st["accs_write_done"] = po_cons.get(1, po_cons.get(0, 0))
        st["po0_free"] = po_cons.get(0, 0)
        st["po1_free"] = po_cons.get(1, 0)
        st["po0_free_s"] = 0
        st["ph_bank_free_s"] = max(ph_evac.values()) if ph_evac else 0
        for i in range(NT):
            toff = half_off + i * TT
            ev_x = P.dma("sync", "s1_x",
                         sb["xt"][:, :].rearrange("p (c t) -> p c t", c=NCH),
                         xin3[:, :, toff:toff + TT],
                         waits=[("vector", st["xt_free_v"]), ("scalar", st["xt_free_s"]),
                                ("s1_xo", st.get("xout_done", 0))])
            sb["stat_war"] = [("vector", st["last_rows_v"])]
            pre = [("s1_x", ev_x), ("tensor", st["last_stats_mm"]),
                   ("tensor", st["rows_free_t"])]
            si = emit_ln_stats(P, sb, sb["xt"], TT, pre, sb["eps_ln"], extra_rms=False)
            st["last_stats_mm"] = si["ev_stats_mm"]
            st["last_rows_v"] = si["ev_var"]
            prow = sb["rows"][0:1, 5 * TT:6 * TT]
            qrow = sb["rows"][0:1, 6 * TT:7 * TT]
            P.op("vector", lambda E: E.tensor_scalar(
                prow, si["rstd"], 0.5, 1.0, op0=ALU.mult, op1=ALU.add),
                waits=[("scalar", si["ev_rstd"])])
            evq = P.op("vector", lambda E: E.tensor_mul(qrow, si["mu"], si["rstd"]))
            evq = P.op("vector", lambda E: E.tensor_scalar(
                qrow, qrow, -0.5, None, op0=ALU.mult))
            ev_b1 = emit_bcast(P, sb["pbc1"], prow, sb, TT,
                               waits=[("vector", evq),
                                      ("vector", st.get("pbc1_free", 0))])
            ev_b2 = emit_bcast(P, sb["pbc2"], qrow, sb, TT,
                               waits=[("vector", st.get("pbc2_free", 0))])
            st["rows_free_t"] = ev_b2
            accv = sb["accs"][:, i * TT: i * TT + TT].rearrange(
                "p t -> p 1 t").broadcast_to([128, 1, TT]) if False else None
            acc8 = sb["accs"][:, :].rearrange("p (c g) -> p c g", c=NCH)[
                :, :, i * TT:(i + 1) * TT]
            P.op("vector", lambda E: E.tensor_tensor(
                v8(sb["xt"], NCH, TT), v8(sb["xt"], NCH, TT),
                bc8(sb["pbc1"][:, 0:TT], NCH, TT), op=ALU.mult),
                waits=[("tensor", ev_b2), ("vector", st["accs_write_done"])])
            P.op("vector", lambda E, acc8=acc8: E.scalar_tensor_tensor(
                v8(sb["xt"], NCH, TT), acc8, 0.5, v8(sb["xt"], NCH, TT),
                op0=ALU.mult, op1=ALU.add))
            ev_base = P.op("vector", lambda E: E.tensor_tensor(
                v8(sb["xt"], NCH, TT), v8(sb["xt"], NCH, TT),
                bc8(sb["pbc2"][:, 0:TT], NCH, TT), op=ALU.add))
            ev_xn = None
            for c in range(NCH):
                pbias = sb["pst1"] if c % 2 == 0 else sb["pst2"]
                ev_bias = P.mm(lambda E, c=c, i=i, pbias=pbias: E.matmul(
                    pbias[:, 0:TT], sb["b2sb"][:, c * 128:(c + 1) * 128],
                    sb["ctpack"][:, i * TT:(i + 1) * TT],
                    start=True, stop=True),
                    waits=[("vector", evA_ct), ("vector", si["ev_var"]),
                           ("vector", st.get("pbias_free_%d" % (c % 2), 0))],
                    inc=True)
                ev_xn = P.op("vector", lambda E, c=c, pbias=pbias: E.scalar_tensor_tensor(
                    ccol(sb["xt"], c, TT), pbias[:, 0:TT], 0.5, ccol(sb["xt"], c, TT),
                    op0=ALU.mult, op1=ALU.add),
                    waits=[("tensor", ev_bias)])
                st["pbias_free_%d" % (c % 2)] = ev_xn
            st["pbc1_free"] = ev_xn
            st["pbc2_free"] = ev_xn
            st["accs_read_v"] = ev_xn
            st["last_rows_v"] = ev_xn
            st["xt_free_v"] = ev_xn
            st["xt_free_s"] = P.ev("scalar")
            ev_store = P.dma("sync", "s1_xo", xout3[:, :, toff:toff + TT],
                             sb["xt"][:, :].rearrange("p (c t) -> p c t", c=NCH),
                             waits=[("vector", ev_xn)])
            st["xout_done"] = ev_store
            st["xt_free_dma"] = ev_store
        st["ctpack_free"] = P.ev("tensor")
        st["tb_free"] = P.ev("tensor")
        st["prev_w2_last"] = st.get("w2_free", 0)
    return st



def alloc_pp(nc, es, T, sfx=""):
    sb = {}
    for p in range(2):
        sb[f"xtm{p}"] = es.enter_context(nc.sbuf_tensor(f"pp_xtm{p}" + sfx, [128, D], F32))
        sb[f"xfm{p}"] = es.enter_context(nc.sbuf_tensor(f"pp_xfm{p}" + sfx, [128, D], F32))
    for b in range(4):
        sb[f"pt{b}"] = es.enter_context(nc.psum_tensor(f"pp_pt{b}" + sfx, [128, 512], F32))
    return sb


def emit_pre(P, sb, cm, io, T, reverse=False):
    nt = T // 128
    ld = {}
    tp_done = {0: 0, 1: 0}
    cp_done = {0: 0, 1: 0}
    st_done = {0: 0, 1: 0}

    def load(jj):
        par = jj % 2
        xtm = sb[f"xtm{par}"]
        if reverse:
            srcv = fm_view(io["xin"])[:, :, jj * 128:(jj + 1) * 128]
            dstv = xtm[:, :].rearrange("p (c t) -> p c t", c=NCH)
        else:
            srcv = io["xin"][jj * 128:(jj + 1) * 128, :]
            dstv = xtm[:, :]
        ld[jj] = P.dma("sync", f"pp_x{par}", dstv, srcv,
                       waits=[("tensor", tp_done[par])])

    load(0)
    load(1)
    for jj in range(nt):
        par = jj % 2
        xtm, xfm = sb[f"xtm{par}"], sb[f"xfm{par}"]
        banks = (sb["pt0"], sb["pt1"]) if par == 0 else (sb["pt2"], sb["pt3"])
        evs = []
        ev_t4 = {}
        for half in range(2):
            bank = banks[half]
            ev_t = None
            for q in range(4):
                c = half * 4 + q
                ev_t = P.mm(lambda E, c=c, bank=bank, q=q, xtm=xtm: E.transpose(
                    bank[:, q * 128:q * 128 + 128], ccol(xtm, c, 128),
                    cm["identf"][:, :]),
                    waits=([(f"pp_x{par}", ld[jj]), ("vector", cp_done[par])]
                           + cm["const_waits"] if q == 0 else []),
                    inc=(q == 3))
            ev_t4[half] = ev_t
            evs.append(P.op("vector", lambda E, half=half, bank=bank, xfm=xfm:
                            E.tensor_copy(
                xfm[:, half * 512:half * 512 + 512], bank[:, 0:512]),
                waits=[("tensor", ev_t), (f"pp_o{par}", st_done[par])]))
        tp_done[par] = P.ev("tensor")
        cp_done[par] = evs[-1]
        if jj + 2 < nt:
            load(jj + 2)
        if reverse:
            st_done[par] = P.dma("sync", f"pp_o{par}",
                                 io["xout"][jj * 128:(jj + 1) * 128, :],
                                 xfm[:, :], waits=[("vector", evs[-1])])
        else:
            st_done[par] = P.dma("sync", f"pp_o{par}",
                                 fm_view(io["xout"])[:, :, jj * 128:(jj + 1) * 128],
                                 xfm[:, :].rearrange("p (c t) -> p c t", c=NCH),
                                 waits=[("vector", evs[-1])])
    return st_done[(nt - 1) % 2]


def alloc_common(nc, es):
    cm = {}
    cm["identf"] = es.enter_context(nc.sbuf_tensor("cm_identf", [128, 128], F32))
    cm["ones_rowf"] = es.enter_context(nc.sbuf_tensor("cm_onesrow", [1, 128], F32))
    cm["ones_colb"] = es.enter_context(nc.sbuf_tensor("cm_onescol", [128, 1], BF16))
    cm["fconsts"] = es.enter_context(nc.sbuf_tensor("cm_fconsts", [1, 8], F32))
    cm["identb"] = es.enter_context(nc.sbuf_tensor("cm_identb", [128, 128], BF16))
    return cm



def build_launch(decls, emit, t_core, debug=False):
    nc = bass.Bass("TRN2", target_bir_lowering=False, debug=debug)
    aps = {}
    for name, (shape, dt_, kind) in decls.items():
        h = nc.dram_tensor(name, list(shape), dt_, kind=kind)
        aps[name] = h.ap()
    es = ExitStack()
    P = Prog(nc)
    try:
        cm = alloc_common(nc, es)
        with nc.Block() as blk:
            P.dma("sync", "c_const", cm["identf"][:, :], aps["identf"])
            P.dma("sync", "c_const", cm["ones_rowf"][:, :], aps["ones_row"])
            P.dma("gpsimd", "c_cg", cm["ones_colb"][:, :], aps["ones_col"])
            P.dma("gpsimd", "c_cg", cm["identb"][:, :], aps["identf"])
            P.dma("sync", "c_const", cm["fconsts"][:, :], aps["fconsts"])
            cm["const_waits"] = [("c_const", P.ev("c_const")),
                                 ("c_cg", P.ev("c_cg"))]
            emit(P, cm, aps, es, nc)
            P.barrier_all()
            P.finalize(blk)
    finally:
        es.close()
        P.close()
    return nc


CONST_DECL = {
    "identf": ((128, 128), F32, "ExternalInput"),
    "ones_row": ((1, 128), F32, "ExternalInput"),
    "ones_col": ((128, 1), F32, "ExternalInput"),
    "fconsts": ((1, 8), F32, "ExternalInput"),
}


def build_pre(t_core, reverse=False):
    decls = dict(CONST_DECL)
    if reverse:
        decls["xin"] = ((D, t_core), F32, "ExternalInput")
        decls["xout"] = ((t_core, D), F32, "ExternalOutput")
    else:
        decls["xin"] = ((t_core, D), F32, "ExternalInput")
        decls["xout"] = ((D, t_core), F32, "ExternalOutput")

    def emit(P, cm, aps, es, nc):
        sb = alloc_pp(nc, es, t_core)
        emit_pre(P, sb, cm, aps, t_core, reverse=reverse)
    return build_launch(decls, emit, t_core)


def build_s1(t_core):
    decls = dict(CONST_DECL)
    decls.update({
        "xin": ((D, t_core), F32, "ExternalInput"),
        "xout": ((D, t_core), F32, "ExternalOutput"),
        "w1t": ((8, D, 2048), F32, "ExternalInput"),
        "w2t": ((8, 2048, D), F32, "ExternalInput"),
        "b1s": ((128, 128), F32, "ExternalInput"),
        "b2s": ((8, D), F32, "ExternalInput"),
        "nws": ((128, NCH), F32, "ExternalInput"),
        "gwt": ((128, 64), F32, "ExternalInput"),
        "selrows": ((8, 1024), F32, "ExternalInput"),
    })

    def emit(P, cm, aps, es, nc):
        sb = alloc_s1(nc, es, t_core)
        sb["const_waits"] = cm["const_waits"]
        sb["eps_ln"] = cm["fconsts"][0:1, 0:1]
        sb["eps_rms"] = cm["fconsts"][0:1, 1:2]
        emit_s1(P, sb, cm, aps, t_core)
    return build_launch(decls, emit, t_core)



def alloc_s2(nc, es, T, sfx=""):
    sb = {"T": T}
    TT = min(512, T)
    sb["TT"] = TT
    sb["xt0"] = es.enter_context(nc.sbuf_tensor("s2_xt0" + sfx, [128, NCH * TT], F32))
    sb["xt1"] = es.enter_context(nc.sbuf_tensor("s2_xt1" + sfx, [128, NCH * TT], F32))
    sb["xq"] = es.enter_context(nc.sbuf_tensor("s2_xq" + sfx, [128, NCH * TT], BF16))
    sb["sq"] = es.enter_context(nc.sbuf_tensor("s2_sq" + sfx, [128, NCH * TT], BF16))
    sb["u2f"] = es.enter_context(nc.sbuf_tensor("s2_u2f" + sfx, [128, NCH * TT], F32))
    sb["ztm"] = es.enter_context(nc.sbuf_tensor("s2_ztm" + sfx, [128, 4 * D], BF16))
    sb["ssb"] = es.enter_context(nc.sbuf_tensor("s2_ssb" + sfx, [128, NCH * TT], BF16))
    sb["ysb"] = es.enter_context(nc.sbuf_tensor("s2_ysb" + sfx, [128, NCH * TT], F32))
    sb["sc"] = es.enter_context(nc.sbuf_tensor("s2_sc" + sfx, [128, TT], F32))
    sb["uc"] = es.enter_context(nc.sbuf_tensor("s2_uc" + sfx, [128, TT], F32))
    sb["rows"] = es.enter_context(nc.sbuf_tensor("s2_rows" + sfx, [1, 7 * TT], F32))
    sb["wl0"] = es.enter_context(nc.sbuf_tensor("s2_wl0" + sfx, [128, NCH * D], BF16))
    sb["l0b"] = es.enter_context(nc.sbuf_tensor("s2_l0b" + sfx, [128, NCH], F32))
    sb["l0g"] = es.enter_context(nc.sbuf_tensor("s2_l0g" + sfx, [128, NCH], F32))
    sb["l0bt"] = es.enter_context(nc.sbuf_tensor("s2_l0bt" + sfx, [128, NCH], F32))
    sb["m1tri"] = es.enter_context(nc.sbuf_tensor("s2_m1tri" + sfx, [128, 128], F32))
    sb["tri01"] = es.enter_context(nc.sbuf_tensor("s2_tri01" + sfx, [128, 128], F32))
    sb["iota"] = es.enter_context(nc.sbuf_tensor("s2_iota" + sfx, [1, T], F32))
    sb["avec"] = es.enter_context(nc.sbuf_tensor("s2_avec" + sfx, [1, T], F32))
    sb["amatb"] = es.enter_context(nc.sbuf_tensor("s2_amatb" + sfx, [128, 128], BF16))
    sb["avbc"] = es.enter_context(nc.sbuf_tensor("s2_avbc" + sfx, [128, 128], F32))
    sb["agt"] = es.enter_context(nc.sbuf_tensor("s2_agt" + sfx, [128, TT], F32))
    sb["scal"] = es.enter_context(nc.sbuf_tensor("s2_scal" + sfx, [1, 8], F32))
    sb["lnac"] = es.enter_context(nc.sbuf_tensor("s2_lnac" + sfx, [128, 1], F32))
    sb["a128c"] = es.enter_context(nc.sbuf_tensor("s2_a128c" + sfx, [128, 1], F32))
    sb["car0"] = es.enter_context(nc.sbuf_tensor("s2_car0" + sfx, [128, NCH], F32))
    sb["car1"] = es.enter_context(nc.sbuf_tensor("s2_car1" + sfx, [128, NCH], F32))
    sb["cin"] = es.enter_context(nc.sbuf_tensor("s2_cin" + sfx, [128, NCH], F32))
    sb["cinb"] = es.enter_context(nc.sbuf_tensor("s2_cinb" + sfx, [128, NCH], BF16))
    sb["wssb"] = es.enter_context(nc.sbuf_tensor("s2_wssb" + sfx, [128, NCH], F32))
    sb["amatf"] = es.enter_context(nc.sbuf_tensor("s2_amatf" + sfx, [128, 128], F32))
    for nm in ("pst1", "pst2", "pbc1", "pbc2", "pS0", "pS1", "pY0", "pY1"):
        sb[nm] = es.enter_context(nc.psum_tensor("s2_" + nm + sfx, [128, 512], F32))
    return sb


def emit_s2(P, sb, cm, io, T, carry_only=False, cin_ev=None):
    TT = sb["TT"]
    NT2 = T // TT
    NSB = TT // 128
    sb["onesb"] = cm["ones_colb"]
    sb["ones_rowf"] = cm["ones_rowf"]
    xin3 = fm_view(io["xin"])
    xout3 = fm_view(io["xout"])
    P.dma("sync", "c_const", sb["l0b"][:, :], io["l0b"])
    P.dma("sync", "c_const", sb["l0g"][:, :], io["l0g"])
    P.dma("sync", "c_const", sb["l0bt"][:, :], io["l0bt"])
    P.dma("sync", "c_const", sb["m1tri"][:, :], io["m1tri"])
    P.dma("sync", "c_const", sb["tri01"][:, :], io["tri01"])
    P.dma("sync", "c_const", sb["iota"][:, :], io["iota_row"][0:1, 0:T])
    P.dma("sync", "c_const", sb["scal"][0:1, 0:1], io["decay"])
    if "carry_in" in io:
        P.dma("sync", "c_const", sb["cin"][:, :], io["carry_in"])
    P.dma("gpsimd", "c_cg", sb["wl0"][:, :].rearrange("p (c e) -> p c e", c=NCH),
          io["l0wt"].rearrange("(c p) e -> p c e", p=128))
    cw = [("c_const", P.ev("c_const")), ("c_cg", P.ev("c_cg"))]
    sb["const_waits"] = cw

    a11 = sb["scal"][0:1, 1:2]
    lna = sb["scal"][0:1, 2:3]
    P.op("scalar", lambda E: E.activation(a11, sb["scal"][0:1, 0:1], AF.Sigmoid),
         waits=cw)
    ev_lna = P.op("scalar", lambda E: E.activation(lna, a11, AF.Ln))
    ev_bl = P.mm(lambda E: E.matmul(sb["pbc1"][:, 0:1], cm["ones_rowf"][:, :], lna,
                                    start=True, stop=True),
                 waits=[("scalar", ev_lna)] + cw, inc=True)
    ev_lnac = P.op("vector", lambda E: E.tensor_copy(sb["lnac"][:, :],
                                                     sb["pbc1"][:, 0:1]),
                   waits=[("tensor", ev_bl)])
    P.op("vector", lambda E: E.tensor_scalar(
        sb["amatf"][:, :], sb["m1tri"][:, :], sb["lnac"][:, 0:1], None, op0=ALU.mult))
    ev_am = P.op("scalar", lambda E: E.activation(sb["amatf"][:, :], sb["amatf"][:, :],
                                                  AF.Exp),
                 waits=[("vector", P.ev("vector"))])
    ev_amb = P.op("vector", lambda E: E.tensor_mul(sb["amatb"][:, :], sb["amatf"][:, :],
                                                   sb["tri01"][:, :]),
                  waits=[("scalar", ev_am)])
    P.op("vector", lambda E: E.tensor_scalar(
        sb["avec"][:, :], sb["iota"][:, :], lna, None, op0=ALU.mult),
        waits=[("scalar", ev_lna)])
    ev_av = P.op("scalar", lambda E: E.activation(sb["avec"][:, :], sb["avec"][:, :],
                                                  AF.Exp),
                 waits=[("vector", P.ev("vector"))])
    ev_b1 = P.mm(lambda E: E.matmul(sb["pbc1"][:, 0:128], cm["ones_rowf"][:, :],
                                    sb["avec"][0:1, 0:128], start=True, stop=True),
                 waits=[("scalar", ev_av), ("vector", ev_lnac)], inc=True)
    ev_avbc = P.op("vector", lambda E: E.tensor_copy(sb["avbc"][:, :],
                                                     sb["pbc1"][:, 0:128]),
                   waits=[("tensor", ev_b1)])
    ev_a128 = P.op("vector", lambda E: E.tensor_copy(sb["a128c"][:, :],
                                                     sb["avbc"][:, 127:128]))
    P.op("vector", lambda E: E.memset(sb["car0"][:, :], 0.0))
    P.op("vector", lambda E: E.memset(sb["car1"][:, :], 0.0))
    if carry_only:
        ev_ws = P.ev("vector")
    else:
        ev_ws = P.op("vector", lambda E: E.tensor_copy(sb["cinb"][:, :],
                                                       sb["cin"][:, :]),
                     waits=([("mb_c", cin_ev)] if cin_ev else []))
    ev_cz = ev_ws
    if not carry_only:
        ev_ws_mm = None
        for ec in range(NCH):
            for c in range(NCH):
                ev_ws_mm = P.mm(lambda E, ec=ec, c=c: E.matmul(
                    sb["pS0"][:, ec:ec + 1],
                    sb["wl0"][:, c * D + ec * 128: c * D + ec * 128 + 128],
                    sb["cinb"][:, c:c + 1],
                    start=(c == 0), stop=(c == NCH - 1)),
                    waits=([("vector", ev_cz)] + cw if ec == 0 and c == 0 else []),
                    inc=(ec == NCH - 1 and c == NCH - 1))
        ev_ws = P.op("vector", lambda E: E.tensor_copy(sb["wssb"][:, :],
                                                       sb["pS0"][:, 0:NCH]),
                     waits=[("tensor", ev_ws_mm)])

    st = {"stats_mm": 0, "rows_v": 0, "rows_t": 0,
          "pbc1_free": ("vector", ev_avbc), "pbc2_free": ("vector", 0),
          "pS_free": ("vector", ev_ws), "pY_free": ("vector", 0),
          "x_free": {0: ev_a128, 1: ev_a128},
          "x_free_s": 0, "xo": 0, "ztm_free": 0, "u2f_free_t": 0,
          "ssb_free_t": 0, "ysb_free_s": 0, "xbufs": (sb["xt0"], sb["xt1"])}
    gsb = [0]

    for j in range(NT2):
        toff = j * TT
        par = j % 2
        xt = st["xbufs"][par]
        xch = "s2_x0" if par == 0 else "s2_x1"
        ev_x = P.dma("sync", xch,
                     xt[:, :].rearrange("p (c t) -> p c t", c=NCH),
                     xin3[:, :, toff:toff + TT],
                     waits=[("vector", st["x_free"][par]),
                            ("scalar", st["x_free_s"]), ("s2_xo", st["xo"])])
        sb["stat_war"] = [("vector", st["rows_v"])]
        pre = [(xch, ev_x), ("tensor", st["stats_mm"]), ("tensor", st["rows_t"])]
        si = emit_ln_stats(P, sb, xt, TT, pre, sb["eps_ln"])
        st["stats_mm"] = si["ev_stats_mm"]
        st["rows_v"] = si["ev_var"]
        evb1 = emit_bcast(P, sb["pbc1"], si["mu"], sb, TT,
                          waits=[("vector", si["ev_var"]), st["pbc1_free"]])
        evb2 = emit_bcast(P, sb["pbc2"], si["rstd"], sb, TT,
                          waits=[("scalar", si["ev_rstd"]), st["pbc2_free"]])
        st["rows_t"] = evb2
        P.op("vector", lambda E, xt=xt: E.tensor_tensor(
            v8(sb["u2f"], NCH, TT), v8(xt, NCH, TT),
            bc8(sb["pbc1"][:, 0:TT], NCH, TT), op=ALU.subtract),
            waits=[("tensor", evb2), ("tensor", st["u2f_free_t"])])
        ev_u2 = P.op("vector", lambda E: E.tensor_tensor(
            v8(sb["u2f"], NCH, TT), v8(sb["u2f"], NCH, TT),
            bc8(sb["pbc2"][:, 0:TT], NCH, TT), op=ALU.mult))
        st["pbc1_free"] = ("vector", ev_u2)
        ev_ztm = {}
        for grp in range(NSB * 2):
            s_ = grp // 2
            half = grp % 2
            bank = sb["pbc1"] if grp % 2 == 0 else sb["pbc2"]
            ev_tp = None
            for q in range(4):
                c = half * 4 + q
                ev_tp = P.mm(lambda E, c=c, s_=s_, bank=bank, q=q,
                             xoff=c * TT + s_ * 128: E.transpose(
                    bank[:, q * 128:q * 128 + 128], sb["u2f"][:, xoff:xoff + 128],
                    cm["identf"][:, :]),
                    waits=([("vector", ev_u2),
                            st.get(("tpc", grp % 2), ("vector", 0))]
                           if q == 0 else []),
                    inc=(q == 3))
            evc = P.op("scalar", lambda E, s_=s_, half=half, bank=bank:
                       E.activation(
                sb["ztm"][:, s_ * D + half * 512: s_ * D + half * 512 + 512],
                bank[:, 0:512], AF.Copy),
                waits=[("tensor", ev_tp), ("tensor", st["ztm_free"])])
            st[("tpc", grp % 2)] = ("scalar", evc)
            ev_ztm[s_] = evc
        st["u2f_free_t"] = P.ev("tensor")
        st["pbc1_free"] = ("scalar", P.ev("scalar"))
        st["pbc2_free"] = ("scalar", P.ev("scalar"))
        ev_sevac_last = None
        for s_ in range(NSB):
            banks = (sb["pS0"], sb["pS1"]) if s_ % 2 == 0 else (sb["pY0"], sb["pY1"])
            bfree = "pS_free" if s_ % 2 == 0 else "pY_free"
            ev_sm = None
            for c in range(NCH):
                bank = banks[0] if c < 4 else banks[1]
                ev_sm = P.mm(lambda E, c=c, s_=s_, bank=bank: E.matmul(
                    bank[:, (c % 4) * 128:(c % 4) * 128 + 128],
                    sb["ztm"][:, s_ * D + c * 128: s_ * D + c * 128 + 128],
                    sb["amatb"][:, :],
                    start=True, stop=True),
                    waits=[("scalar", ev_ztm[s_]), ("vector", ev_amb),
                           st[bfree]],
                    inc=(c == NCH - 1))
            g = gsb[0]
            cprev = sb["car0"] if g % 2 == 0 else sb["car1"]
            cnext = sb["car1"] if g % 2 == 0 else sb["car0"]
            gsb[0] += 1
            ev_cu = None
            for half in range(2):
                c0 = half * 4
                bank = banks[half]
                ev_cu = P.op("vector", lambda E, bank=bank, c0=c0, cn=cnext,
                             cp=cprev: E.scalar_tensor_tensor(
                    cn[:, c0:c0 + 4], cp[:, c0:c0 + 4], sb["a128c"][:, 0:1],
                    bank[:, 127:512:128], op0=ALU.mult, op1=ALU.add),
                    waits=[("tensor", ev_sm)])
            if carry_only:
                st[bfree] = ("vector", ev_cu)
                ev_sevac_last = ev_cu
                continue
            for c in range(NCH):
                bank = banks[0] if c < 4 else banks[1]
                col = (c % 4) * 128
                ev_sevac_last = P.op("vector", lambda E, c=c, s_=s_, bank=bank,
                                     col=col, cp=cprev: E.scalar_tensor_tensor(
                    ccol(sb["ssb"], c, TT)[:, s_ * 128:s_ * 128 + 128],
                    sb["avbc"][:, :], cp[:, c:c + 1],
                    bank[:, col:col + 128], op0=ALU.mult, op1=ALU.add),
                    waits=[("tensor", st["ssb_free_t"])])
            st[bfree] = ("vector", ev_sevac_last)
        st["ztm_free"] = P.ev("tensor")
        if carry_only:
            st["x_free"][par] = ev_u2
            st["x_free_s"] = P.ev("scalar")
            continue
        ev_ag = P.mm(lambda E, toff=toff: E.matmul(
            sb["pbc2"][:, 0:TT], cm["ones_rowf"][:, :],
            sb["avec"][0:1, toff:toff + TT], start=True, stop=True),
            waits=[st["pbc2_free"]], inc=True)
        ev_agt = P.op("vector", lambda E: E.tensor_copy(sb["agt"][:, :],
                                                        sb["pbc2"][:, 0:TT]),
                      waits=[("tensor", ev_ag)])
        st["pbc2_free"] = ("vector", ev_agt)
        ev_yev = None
        for ec in range(NCH):
            bank = sb["pS0"] if ec % 2 == 0 else sb["pS1"]
            ev_ym = None
            for c in range(NCH):
                ev_ym = P.mm(lambda E, ec=ec, c=c, bank=bank: E.matmul(
                    bank[:, 0:TT],
                    sb["wl0"][:, c * D + ec * 128: c * D + ec * 128 + 128],
                    ccol(sb["ssb"], c, TT),
                    start=(c == 0), stop=(c == NCH - 1)),
                    waits=([("vector", ev_sevac_last), st["pS_free"],
                            ("vector", st.get(("yf", ec % 2), 0)),
                            ("scalar", st.get(("yfs", ec % 2), 0))]
                           if c == 0 else []),
                    inc=(c == NCH - 1))
            ev_ye1 = P.op("vector", lambda E, ec=ec, bank=bank: E.scalar_tensor_tensor(
                ccol(sb["ysb"], ec, TT), sb["agt"][:, :], sb["wssb"][:, ec:ec + 1],
                bank[:, 0:TT], op0=ALU.mult, op1=ALU.add),
                waits=[("tensor", ev_ym), ("vector", ev_agt), ("vector", ev_ws),
                       ("scalar", st["ysb_free_s"])])
            st[("yf", ec % 2)] = ev_ye1
            ev_yev = P.op("scalar", lambda E, ec=ec: E.activation(
                ccol(sb["ysb"], ec, TT), ccol(sb["ysb"], ec, TT), AF.Identity,
                bias=sb["l0b"][:, ec:ec + 1]),
                waits=[("vector", ev_ye1)])
            st[("yfs", ec % 2)] = ev_yev
        st["pS_free"] = ("scalar", ev_yev)
        st["ssb_free_t"] = P.ev("tensor")
        sb["stat_war"] = [("vector", st["rows_v"])]
        pre2 = [("scalar", ev_yev), ("tensor", st["stats_mm"]),
                ("tensor", st["rows_t"])]
        si2 = emit_ln_stats(P, sb, sb["ysb"], TT, pre2, sb["eps_ln"])
        st["stats_mm"] = si2["ev_stats_mm"]
        st["rows_v"] = si2["ev_var"]
        st["ysb_free_s"] = P.ev("scalar")
        evc1 = emit_bcast(P, sb["pbc1"], si2["mu"], sb, TT,
                          waits=[("vector", si2["ev_var"]),
                                 st["pbc1_free"]])
        evc2 = emit_bcast(P, sb["pbc2"], si2["rstd"], sb, TT,
                          waits=[("scalar", si2["ev_rstd"]),
                                 st["pbc2_free"]])
        st["rows_t"] = evc2
        P.op("vector", lambda E: E.tensor_tensor(
            v8(sb["ysb"], NCH, TT), v8(sb["ysb"], NCH, TT),
            bc8(sb["pbc1"][:, 0:TT], NCH, TT), op=ALU.subtract),
            waits=[("tensor", evc2), ("scalar", P.ev("scalar"))])
        P.op("vector", lambda E: E.tensor_tensor(
            v8(sb["ysb"], NCH, TT), v8(sb["ysb"], NCH, TT),
            bc8(sb["pbc2"][:, 0:TT], NCH, TT), op=ALU.mult))
        ev_xn = None
        for ec in range(NCH):
            P.op("scalar", lambda E, ec=ec: E.activation(
                ccol(sb["ysb"], ec, TT), ccol(sb["ysb"], ec, TT), AF.Identity,
                scale=sb["l0g"][:, ec:ec + 1], bias=sb["l0bt"][:, ec:ec + 1]),
                waits=[("vector", P.ev("vector"))])
            ev_xn = P.op("vector", lambda E, ec=ec, xt=xt: E.tensor_add(
                ccol(xt, ec, TT), ccol(xt, ec, TT), ccol(sb["ysb"], ec, TT)),
                waits=[("scalar", P.ev("scalar"))])
        st["pbc1_free"] = ("vector", ev_xn)
        st["pbc2_free"] = ("vector", ev_xn)
        st["rows_v"] = ev_xn
        st["x_free"][par] = ev_xn
        st["x_free_s"] = P.ev("scalar")
        st["xo"] = P.dma("sync", "s2_xo", xout3[:, :, toff:toff + TT],
                         xt[:, :].rearrange("p (c t) -> p c t", c=NCH),
                         waits=[("vector", ev_xn)])
    cfin = sb["car0"] if gsb[0] % 2 == 0 else sb["car1"]
    if "carry_out" in io:
        P.dma("sync", "s2_co", io["carry_out"], cfin[:, :],
              waits=[("vector", P.ev("vector"))])
    st["cfin"] = cfin
    st["cfin_ev"] = P.ev("vector")
    return st


def build_s2(t_core):
    decls = dict(CONST_DECL)
    decls.update({
        "xin": ((D, t_core), F32, "ExternalInput"),
        "xout": ((D, t_core), F32, "ExternalOutput"),
        "carry_in": ((128, NCH), F32, "ExternalInput"),
        "carry_out": ((128, NCH), F32, "ExternalOutput"),
        "l0wt": ((D, D), F32, "ExternalInput"),
        "l0b": ((128, NCH), F32, "ExternalInput"),
        "l0g": ((128, NCH), F32, "ExternalInput"),
        "l0bt": ((128, NCH), F32, "ExternalInput"),
        "m1tri": ((128, 128), F32, "ExternalInput"),
        "tri01": ((128, 128), F32, "ExternalInput"),
        "iota_row": ((1, 4096), F32, "ExternalInput"),
        "decay": ((1, 1), F32, "ExternalInput"),
    })
    import os
    if os.environ.get("S2_DEBUG"):
        decls["dbg_y"] = ((D, t_core), F32, "ExternalOutput")
        decls["dbg_s"] = ((D, t_core), BF16, "ExternalOutput")
        decls["dbg_u"] = ((D, t_core), BF16, "ExternalOutput")

    def emit(P, cm, aps, es, nc):
        sb = alloc_s2(nc, es, t_core)
        sb["eps_ln"] = cm["fconsts"][0:1, 0:1]
        sb["eps_rms"] = cm["fconsts"][0:1, 1:2]
        emit_s2(P, sb, cm, aps, t_core)
    return build_launch(decls, emit, t_core)



def alloc_s4(nc, es, T, sfx=""):
    sb = {"T": T}
    sb["w"] = es.enter_context(nc.sbuf_tensor("s4_w" + sfx, [128, 32 * 1024], BF16))
    sb["inp"] = es.enter_context(nc.sbuf_tensor("s4_in" + sfx, [128, 32 * 512], BF16))
    sb["inp2"] = es.enter_context(nc.sbuf_tensor("s4_in2" + sfx, [128, 32 * 512], BF16))
    sb["xt"] = es.enter_context(nc.sbuf_tensor("s4_xt" + sfx, [128, NCH * 512], F32))
    sb["xq"] = es.enter_context(nc.sbuf_tensor("s4_xq" + sfx, [128, NCH * 512], BF16))
    sb["sq"] = es.enter_context(nc.sbuf_tensor("s4_sq" + sfx, [128, NCH * 512], BF16))
    sb["rows"] = es.enter_context(nc.sbuf_tensor("s4_rows" + sfx, [1, 8 * 512], F32))
    sb["sc"] = es.enter_context(nc.sbuf_tensor("s4_sc" + sfx, [128, 512], F32))
    sb["va"] = es.enter_context(nc.sbuf_tensor("s4_va" + sfx, [128, 512], F32))
    sb["ob0"] = es.enter_context(nc.sbuf_tensor("s4_ob0" + sfx, [128, 512], BF16))
    sb["ob1"] = es.enter_context(nc.sbuf_tensor("s4_ob1" + sfx, [128, 512], BF16))
    sb["scl"] = es.enter_context(nc.sbuf_tensor("s4_scl" + sfx, [128, 4 * 32 + 16], F32))
    for nm in ("pst1", "pst2", "pbc1", "pbc2", "pA0", "pA1", "pB0", "pB1"):
        sb[nm] = es.enter_context(nc.psum_tensor("s4_" + sfx + nm, [128, 512], F32))
    return sb


def emit_s4(P, sb, cm, io, T):
    sb["onesb"] = cm["ones_colb"]
    sb["ones_rowf"] = cm["ones_rowf"]
    NTT = T // 512
    xin3 = fm_view(io["xin"])
    xout3 = fm_view(io["xout"])
    u43 = io["u4"].rearrange("(c p) t -> p c t", p=128)
    ha3 = io["ha"].rearrange("(c p) t -> p c t", p=128)
    hb3 = io["hb"].rearrange("(c p) t -> p c t", p=128)
    P.dma("sync", "c_const", sb["scl"][:, 0:32], io["s1s"])
    P.dma("sync", "c_const", sb["scl"][:, 32:64], io["s2s"])
    P.dma("sync", "c_const", sb["scl"][:, 64:96], io["sh1"])
    P.dma("sync", "c_const", sb["scl"][:, 96:128], io["sh2"])
    P.dma("sync", "c_const", sb["scl"][:, 128:136], io["s3s"])
    cw = [("c_const", P.ev("c_const")), ("c_cg", P.ev("c_cg"))]
    sb["const_waits"] = cw
    ev_s3h = P.op("vector", lambda E: E.tensor_scalar(
        sb["scl"][:, 128:136], sb["scl"][:, 128:136], 0.5, None, op0=ALU.mult),
        waits=cw)

    st = {"x_free_v": 0, "x_free_s": 0, "stats_mm": 0, "rows_v": 0, "rows_t": 0,
          "pbc1_free": 0, "pbc2_free": 0, "in_free": 0, "w_free": 0}

    WHALF = 32 * 512
    wbufs = (sb["w"][:, 0:WHALF], sb["w"][:, WHALF:2 * WHALF])
    wch = ("s4_wA", "s4_wB")
    wread = {0: 0, 1: 0}
    wev = {}

    def mk_load_s1(ob):
        def go(wb, par):
            P.dma("gpsimd", wch[par],
                  wb[:, 0:NCH * 512].rearrange("p (c m) -> p c m", c=NCH),
                  io["f1t"][:, ob * 512:(ob + 1) * 512].rearrange(
                      "(c p) m -> p c m", p=128),
                  waits=[("tensor", wread[par])])
            return P.dma("gpsimd", wch[par],
                  wb[:, NCH * 512:NCH * 1024].rearrange("p (c m) -> p c m", c=NCH),
                  io["f2t"][:, ob * 512:(ob + 1) * 512].rearrange(
                      "(c p) m -> p c m", p=128))
        return ("s1", ob), go

    def mk_load_mat(skey, wt_nm, ob):
        def go(wb, par):
            return P.dma("gpsimd", wch[par],
                  wb[:, 0:32 * 512].rearrange("p (c m) -> p c m", c=32),
                  io[wt_nm][:, ob * 512:(ob + 1) * 512].rearrange(
                      "(c p) m -> p c m", p=128),
                  waits=[("tensor", wread[par])])
        return (skey, ob), go

    def mk_load_w3(half):
        def go(wb, par):
            return P.dma("gpsimd", wch[par],
                  wb[:, 0:16 * 1024].rearrange("p (c m) -> p c m", c=16),
                  io["f3t"][half * 2048:(half + 1) * 2048, :].rearrange(
                      "(c p) m -> p c m", p=128),
                  waits=[("tensor", wread[par])])
        return ("w3", half), go

    wplan = []
    for ob in range(8):
        wplan.append(mk_load_s1(ob))
    for ob in range(8):
        wplan.append(mk_load_mat("s2", "h1t", ob))
    for ob in range(8):
        wplan.append(mk_load_mat("s3", "h2t", ob))
    wplan.append(mk_load_w3(0))
    wplan.append(mk_load_w3(1))
    wp_idx = [0]
    WDEFER = {9, 17, 25}

    def issue_next_w(flush=False):
        if wp_idx[0] >= len(wplan):
            return
        i = wp_idx[0]
        if (i in WDEFER) != flush:
            return
        wp_idx[0] += 1
        key, go = wplan[i]
        wev[key] = go(wbufs[i % 2], i % 2)

    issue_next_w()
    issue_next_w()

    gin = {"iread": {0: 0, 1: 0}, "ev": {}}

    def load_in_g(key, src, nch, ob, tt, extra_waits=()):
        ipar = (ob * NTT + tt) % 2
        buf = sb["inp"] if ipar == 0 else sb["inp2"]
        gin["ev"][(key, ob, tt)] = P.dma(
            "gpsimd", "s4_iA" if ipar == 0 else "s4_iB",
            buf[:, 0:nch * 512].rearrange("p (c t) -> p c t", c=nch),
            src[:, :, tt * 512:tt * 512 + 512],
            waits=[("tensor", gin["iread"][ipar])] + list(extra_waits))

    ev_u_store = 0
    for tt in range(NTT):
        toff = tt * 512
        ev_x = P.dma("sync", "s4_x",
                     sb["xt"][:, :].rearrange("p (c t) -> p c t", c=NCH),
                     xin3[:, :, toff:toff + 512],
                     waits=[("vector", st["x_free_v"]), ("scalar", st["x_free_s"])])
        sb["stat_war"] = [("vector", st["rows_v"])]
        pre = [("s4_x", ev_x), ("tensor", st["stats_mm"]), ("tensor", st["rows_t"])]
        si = emit_ln_stats(P, sb, sb["xt"], 512, pre, sb["eps_ln"])
        st["stats_mm"] = si["ev_stats_mm"]
        st["rows_v"] = si["ev_var"]
        evb1 = emit_bcast(P, sb["pbc1"], si["mu"], sb, 512,
                          waits=[("vector", si["ev_var"]), ("vector", st["pbc1_free"])])
        evb2 = emit_bcast(P, sb["pbc2"], si["rstd"], sb, 512,
                          waits=[("scalar", si["ev_rstd"]), ("vector", st["pbc2_free"])])
        st["rows_t"] = evb2
        P.op("vector", lambda E: E.tensor_tensor(
            v8(sb["xt"], NCH, 512), v8(sb["xt"], NCH, 512),
            bc8(sb["pbc1"][:, 0:512], NCH, 512), op=ALU.subtract),
            waits=[("tensor", evb2), ("s4_us", ev_u_store)])
        ev_u = P.op("vector", lambda E: E.tensor_tensor(
            v8(sb["xq"], NCH, 512), v8(sb["xt"], NCH, 512),
            bc8(sb["pbc2"][:, 0:512], NCH, 512), op=ALU.mult))
        st["pbc1_free"] = ev_u
        st["pbc2_free"] = ev_u
        st["x_free_v"] = ev_u
        st["x_free_s"] = P.ev("scalar")
        ev_u_store = P.dma("sync", "s4_us", u43[:, :, toff:toff + 512],
                           sb["xq"][:, :].rearrange("p (c t) -> p c t", c=NCH),
                           waits=[("vector", ev_u)])
        if tt == 0:
            load_in_g("s1", u43, NCH, 0, 0,
                      extra_waits=[("s4_us", ev_u_store)])
    P.barrier_all()

    def mm_stage(skey, src3, nch_in, dst3, scol, act, extra=None, nxt=None):
        TO = 512
        NB = 4096 // TO
        inch = ("s4_iA", "s4_iB")
        inbufs = (sb["inp"][:, 0:nch_in * 512], sb["inp2"][:, 0:nch_in * 512])
        iread = gin["iread"]
        obev = {}
        st0ev = [None]
        issue_next_w(flush=True)
        if (skey, 0, 0) not in gin["ev"]:
            load_in_g(skey, src3, nch_in, 0, 0)
        for ob in range(NB):
            par = ob % 2
            wb = wbufs[par]
            for tt in range(NTT):
                toff = tt * 512
                ipar = (ob * NTT + tt) % 2
                if tt + 1 < NTT:
                    load_in_g(skey, src3, nch_in, ob, tt + 1)
                elif ob + 1 < NB:
                    load_in_g(skey, src3, nch_in, ob + 1, 0)
                elif nxt is not None and st0ev[0] is not None:
                    load_in_g(nxt[0], nxt[1], nxt[2], 0, 0,
                              extra_waits=[("s4_os", st0ev[0])])
                ev_in = gin["ev"][(skey, ob, tt)]
                for oc in range(TO // 128):
                    pa = sb["pA0"] if oc % 2 == 0 else sb["pA1"]
                    fw = [(wch[par], wev[(skey, ob)]), (inch[ipar], ev_in),
                          ("scalar", obev.get(("ev_sA", oc % 2), 0)),
                          ("vector", obev.get(("ev_o", oc % 2), 0))]
                    ib_cur = inbufs[ipar]
                    for c in range(nch_in):
                        ev_mm = P.mm(lambda E, c=c, oc=oc, pa=pa, wb=wb,
                                     ib=ib_cur: E.matmul(
                            pa[:, 0:512],
                            wb[:, c * TO + oc * 128: c * TO + oc * 128 + 128],
                            ib[:, c * 512:(c + 1) * 512],
                            start=(c == 0), stop=(c == nch_in - 1)),
                            waits=(fw if c == 0 else []),
                            inc=(c == nch_in - 1))
                    if extra == "stage1":
                        pb = sb["pB0"] if oc % 2 == 0 else sb["pB1"]
                        for c in range(nch_in):
                            ev_mm2 = P.mm(lambda E, c=c, oc=oc, pb=pb, wb=wb,
                                          ib=ib_cur: E.matmul(
                                pb[:, 0:512],
                                wb[:, (nch_in + c) * TO + oc * 128:
                                    (nch_in + c) * TO + oc * 128 + 128],
                                ib[:, c * 512:(c + 1) * 512],
                                start=(c == 0), stop=(c == nch_in - 1)),
                                waits=([("vector", obev.get(("ev_o", oc % 2), 0))]
                                       if c == 0 else []),
                                inc=(c == nch_in - 1))
                        ev_sA = P.op("scalar", lambda E, oc=oc, ob=ob, pa=pa:
                                     E.activation(
                            sb["va"][:, :], pa[:, 0:512], AF.Silu,
                            scale=sb["scl"][:, ob * 4 + oc: ob * 4 + oc + 1]),
                            waits=[("tensor", ev_mm),
                                   ("vector", obev.get("last_o", 0))])
                        obuf = sb["ob0"] if (tt + oc) % 2 == 0 else sb["ob1"]
                        ev_o = P.op("vector", lambda E, oc=oc, ob=ob, pb=pb,
                                    obuf=obuf: E.scalar_tensor_tensor(
                            obuf[:, :], pb[:, 0:512],
                            sb["scl"][:, 32 + ob * 4 + oc: 32 + ob * 4 + oc + 1],
                            sb["va"][:, :], op0=ALU.mult, op1=ALU.mult),
                            waits=[("tensor", ev_mm2), ("scalar", ev_sA),
                                   ("s4_os", obev.get(("st", obuf is sb["ob1"]), 0))])
                    else:
                        obuf = sb["ob0"] if (tt + oc) % 2 == 0 else sb["ob1"]
                        ev_sA = None
                        ev_o = P.op("scalar", lambda E, oc=oc, ob=ob, pa=pa,
                                    obuf=obuf: E.activation(
                            obuf[:, :], pa[:, 0:512], act,
                            scale=sb["scl"][:, scol + ob * 4 + oc:
                                            scol + ob * 4 + oc + 1]),
                            waits=[("tensor", ev_mm),
                                   ("s4_os", obev.get(("st", obuf is sb["ob1"]), 0))])
                    obev[("ev_sA", oc % 2)] = P.ev("scalar")
                    obev[("ev_o", oc % 2)] = P.ev("vector")
                    obev["last_o"] = P.ev("vector")
                    who = "vector" if extra == "stage1" else "scalar"
                    ev_st = P.dma("sync", "s4_os",
                                  dst3[:, ob * 4 + oc, toff:toff + 512],
                                  obuf[:, :], waits=[(who, ev_o)])
                    obev[("st", obuf is sb["ob1"])] = ev_st
                if ob == NB - 1 and tt == 0:
                    st0ev[0] = ev_st
                iread[ipar] = P.ev("tensor")
            wread[par] = P.ev("tensor")
            issue_next_w()
        P.barrier_all()

    mm_stage("s1", u43, NCH, ha3, 0, None, extra="stage1", nxt=("s2", ha3, 32))
    mm_stage("s2", ha3, 32, hb3, 64, AF.Gelu, nxt=("s3", hb3, 32))
    mm_stage("s3", hb3, 32, ha3, 96, AF.Gelu, nxt=("s4", ha3, 32))

    issue_next_w(flush=True)
    ev_store = 0
    if ("s4", 0, 0) not in gin["ev"]:
        load_in_g("s4", ha3, 32, 0, 0)
    for tt in range(NTT):
        toff = tt * 512
        if tt + 1 < NTT:
            load_in_g("s4", ha3, 32, 0, tt + 1)
        ev_in = gin["ev"][("s4", 0, tt)]
        ib4 = sb["inp"] if tt % 2 == 0 else sb["inp2"]
        ev_x = P.dma("sync", "s4_x",
                     sb["xt"][:, :].rearrange("p (c t) -> p c t", c=NCH),
                     xin3[:, :, toff:toff + 512],
                     waits=[("vector", st.get("x4_free", 0)), ("s4_xo", ev_store)])
        ev_xn = None
        for oc in range(8):
            pa = sb["pA0"] if oc % 2 == 0 else sb["pA1"]
            fw = [("s4_wA", wev[("w3", 0)]), ("s4_wB", wev[("w3", 1)]),
                  ("s4_iA" if tt % 2 == 0 else "s4_iB", ev_in),
                  ("vector", st.get(("s4o", oc % 2), 0))]
            for c in range(32):
                ev_mm = P.mm(lambda E, c=c, oc=oc, pa=pa, ib=ib4: E.matmul(
                    pa[:, 0:512],
                    sb["w"][:, c * 1024 + oc * 128: c * 1024 + oc * 128 + 128],
                    ib[:, c * 512:(c + 1) * 512],
                    start=(c == 0), stop=(c == 31)),
                    waits=(fw if c == 0 else []),
                    inc=(c == 31))
            ev_xn = P.op("vector", lambda E, oc=oc, pa=pa: E.scalar_tensor_tensor(
                ccol(sb["xt"], oc, 512), pa[:, 0:512],
                sb["scl"][:, 128 + oc:128 + oc + 1],
                ccol(sb["xt"], oc, 512), op0=ALU.mult, op1=ALU.add),
                waits=[("tensor", ev_mm), ("s4_x", ev_x), ("vector", ev_s3h)])
            st[("s4o", oc % 2)] = ev_xn
        st["x4_free"] = ev_xn
        gin["iread"][tt % 2] = P.ev("tensor")
        ev_store = P.dma("sync", "s4_xo", xout3[:, :, toff:toff + 512],
                         sb["xt"][:, :].rearrange("p (c t) -> p c t", c=NCH),
                         waits=[("vector", ev_xn)])
    return st


def build_s4(t_core):
    decls = dict(CONST_DECL)
    decls.update({
        "xin": ((D, t_core), F32, "ExternalInput"),
        "xout": ((D, t_core), F32, "ExternalOutput"),
        "f1t": ((D, 4096), F32, "ExternalInput"),
        "f2t": ((D, 4096), F32, "ExternalInput"),
        "h1t": ((4096, 4096), F32, "ExternalInput"),
        "h2t": ((4096, 4096), F32, "ExternalInput"),
        "f3t": ((4096, D), F32, "ExternalInput"),
        "s1s": ((128, 32), F32, "ExternalInput"),
        "s2s": ((128, 32), F32, "ExternalInput"),
        "sh1": ((128, 32), F32, "ExternalInput"),
        "sh2": ((128, 32), F32, "ExternalInput"),
        "s3s": ((128, 8), F32, "ExternalInput"),
    })

    def emit(P, cm, aps, es, nc):
        aps = dict(aps)
        aps["u4"] = nc.dram_tensor("u4", [D, t_core], BF16).ap()
        aps["ha"] = nc.dram_tensor("ha", [4096, t_core], BF16).ap()
        aps["hb"] = nc.dram_tensor("hb", [4096, t_core], BF16).ap()
        sb = alloc_s4(nc, es, t_core)
        sb["eps_ln"] = cm["fconsts"][0:1, 0:1]
        sb["eps_rms"] = cm["fconsts"][0:1, 1:2]
        emit_s4(P, sb, cm, aps, t_core)
    return build_launch(decls, emit, t_core)



def alloc_s5(nc, es, T, sfx=""):
    sb = {"T": T}
    for w in ("qwtS", "kwtS", "vwtS", "owtS"):
        sb[w] = es.enter_context(nc.sbuf_tensor("s5_" + sfx + w, [128, NCH * D], BF16))
    sb["mA0"] = es.enter_context(nc.sbuf_tensor("s5_mA0" + sfx, [128, 384], BF16))
    sb["mA1"] = es.enter_context(nc.sbuf_tensor("s5_mA1" + sfx, [128, 384], BF16))
    sb["mB"] = es.enter_context(nc.sbuf_tensor("s5_mB" + sfx, [128, 384], BF16))
    sb["qb8"] = es.enter_context(nc.sbuf_tensor("s5_qb8" + sfx, [128, NCH], F32))
    sb["kb"] = es.enter_context(nc.sbuf_tensor("s5_kb" + sfx, [128, NCH], F32))
    sb["ob_"] = es.enter_context(nc.sbuf_tensor("s5_ob" + sfx, [128, NCH], F32))
    sb["vbrow"] = es.enter_context(nc.sbuf_tensor("s5_vbrow" + sfx, [1, D], F32))
    sb["vbb"] = es.enter_context(nc.sbuf_tensor("s5_vbb" + sfx, [128, D], F32))
    sb["xt0"] = es.enter_context(nc.sbuf_tensor("s5_xt0" + sfx, [128, D], F32))
    sb["xt1"] = es.enter_context(nc.sbuf_tensor("s5_xt1" + sfx, [128, D], F32))
    sb["xq"] = es.enter_context(nc.sbuf_tensor("s5_xq" + sfx, [128, D], BF16))
    sb["sq"] = es.enter_context(nc.sbuf_tensor("s5_sq" + sfx, [128, D], BF16))
    sb["u5b0"] = es.enter_context(nc.sbuf_tensor("s5_u5b0" + sfx, [128, D], BF16))
    sb["u5b1"] = es.enter_context(nc.sbuf_tensor("s5_u5b1" + sfx, [128, D], BF16))
    sb["rows"] = es.enter_context(nc.sbuf_tensor("s5_rows" + sfx, [1, 7 * 128], F32))
    sb["sc"] = es.enter_context(nc.sbuf_tensor("s5_sc" + sfx, [128, 128], F32))
    sb["qsb0"] = es.enter_context(nc.sbuf_tensor("s5_qsb0" + sfx, [64, 16 * 128], BF16))
    sb["qsb1"] = es.enter_context(nc.sbuf_tensor("s5_qsb1" + sfx, [64, 16 * 128], BF16))
    sb["kr"] = es.enter_context(nc.sbuf_tensor("s5_kr" + sfx, [64, 4 * 16 * 128], BF16))
    sb["vr"] = es.enter_context(nc.sbuf_tensor("s5_vr" + sfx, [128, 4 * D], BF16))
    sb["aosb0"] = es.enter_context(nc.sbuf_tensor("s5_aosb0" + sfx, [128, D], BF16))
    sb["aosb1"] = es.enter_context(nc.sbuf_tensor("s5_aosb1" + sfx, [128, D], BF16))
    sb["pw"] = es.enter_context(nc.sbuf_tensor("s5_pw" + sfx, [128, 4 * 384], BF16))
    sb["wT"] = es.enter_context(nc.sbuf_tensor("s5_wT" + sfx, [128, 2 * 384], BF16))
    sb["rsc"] = es.enter_context(nc.sbuf_tensor("s5_rsc" + sfx, [128, 16], F32))
    for nm in ("pst1", "pQK0", "pQK1", "psc0", "psc1", "pO"):
        sb[nm] = es.enter_context(nc.psum_tensor("s5_" + sfx + nm, [128, 512], F32))
    sb["pst2"] = sb["pst1"][:, 256:512]
    sb["ptb5a"] = es.enter_context(nc.psum_tensor("s5_ptb5a" + sfx, [128, 1024], BF16))
    sb["ptb5b"] = es.enter_context(nc.psum_tensor("s5_ptb5b" + sfx, [128, 1024], BF16))
    return sb


def emit_s5(P, sb, cm, io, T):
    sb["onesb"] = cm["ones_colb"]
    sb["ones_rowf"] = cm["ones_rowf"]
    NT5 = T // 128
    xin3 = fm_view(io["xin"])
    xout3 = fm_view(io["xout"])
    halo3 = fm_view(io["halo_x"])
    for nm, ap in (("qb8", io["qb"]), ("kb", io["kb"]), ("ob_", io["obias"]),
                   ("vbrow", io["vbrow"])):
        P.dma("sync", "c_const", sb[nm][:, :], ap)
    for nm, ap in (("mA0", io["maskA0"]), ("mA1", io["maskA1"]), ("mB", io["maskB"])):
        P.dma("gpsimd", "c_cg", sb[nm][:, :], ap)
    for w, ap in (("qwtS", io["qwt"]), ("kwtS", io["kwt"]),
                  ("vwtS", io["vwt"]), ("owtS", io["owt"])):
        P.dma("gpsimd", "c_cg", sb[w][:, :].rearrange("p (c e) -> p c e", c=NCH),
              ap.rearrange("(c p) e -> p c e", p=128))
    cw = [("c_const", P.ev("c_const")), ("c_cg", P.ev("c_cg"))]
    sb["const_waits"] = cw
    ev_qb8 = P.op("vector", lambda E: E.tensor_scalar(
        sb["qb8"][:, :], sb["qb8"][:, :], 0.125, None, op0=ALU.mult), waits=cw)
    ev_vb1 = P.mm(lambda E: E.matmul(sb["pQK0"][:, 0:512], cm["ones_rowf"][:, :],
                                     sb["vbrow"][0:1, 0:512], start=True, stop=True),
                  waits=cw, inc=True)
    ev_vb2 = P.mm(lambda E: E.matmul(sb["pQK1"][:, 0:512], cm["ones_rowf"][:, :],
                                     sb["vbrow"][0:1, 512:1024], start=True, stop=True),
                  inc=True)
    P.op("vector", lambda E: E.tensor_copy(sb["vbb"][:, 0:512], sb["pQK0"][:, 0:512]),
         waits=[("tensor", ev_vb1)])
    ev_vbb = P.op("vector", lambda E: E.tensor_copy(sb["vbb"][:, 512:1024],
                                                    sb["pQK1"][:, 0:512]),
                  waits=[("tensor", ev_vb2)])

    st = {"stats_mm": 0, "rows_v": 0, "rows_t": 0,
          "pbc1_free": ev_vbb, "pbc2_free": ev_vbb,
          "x_free": {0: 0, 1: 0}, "x_free_s": {0: 0, 1: 0},
          "xo": {0: 0, 1: 0}, "u5_free_t": {0: 0, 1: 0}}
    sb["pbc1"] = sb["pQK0"]
    sb["pbc2"] = sb["pQK1"]
    kv_state = {"kslot_free": {}, "vslot_free": {}, "pQK_free_s": 0,
                "pQK_free_v": ev_vbb}
    head_ev = {}

    def emit_ln_u5(src3, toff, par):
        xt = sb["xt0"] if par == 0 else sb["xt1"]
        u5b = sb["u5b0"] if par == 0 else sb["u5b1"]
        xch = "s5_x0" if par == 0 else "s5_x1"
        ev_x = P.dma("sync", xch,
                     xt[:, :].rearrange("p (c t) -> p c t", c=NCH),
                     src3[:, :, toff:toff + 128],
                     waits=[("vector", st["x_free"][par]),
                            ("scalar", st["x_free_s"][par]),
                            ("s5_xo", st["xo"][par])])
        sb["stat_war"] = [("vector", st["rows_v"])]
        pre = [(xch, ev_x), ("tensor", st["stats_mm"]), ("tensor", st["rows_t"])]
        si = emit_ln_stats(P, sb, xt, 128, pre, sb["eps_ln"])
        st["stats_mm"] = si["ev_stats_mm"]
        st["rows_v"] = si["ev_var"]
        evb1 = emit_bcast(P, sb["pbc1"], si["mu"], sb, 128,
                          waits=[("vector", si["ev_var"]), ("vector", st["pbc1_free"]),
                                 ("scalar", kv_state["pQK_free_s"])])
        evb2 = emit_bcast(P, sb["pbc2"], si["rstd"], sb, 128,
                          waits=[("scalar", si["ev_rstd"]), ("vector", st["pbc2_free"])])
        st["rows_t"] = evb2
        P.op("vector", lambda E, xt=xt, u5b=u5b: E.tensor_tensor(
            v8(u5b, NCH, 128), v8(xt, NCH, 128),
            bc8(sb["pbc1"][:, 0:128], NCH, 128), op=ALU.subtract),
            waits=[("tensor", evb2), ("tensor", st["u5_free_t"][par])])
        ev_u = P.op("vector", lambda E, u5b=u5b: E.tensor_tensor(
            v8(u5b, NCH, 128), v8(u5b, NCH, 128),
            bc8(sb["pbc2"][:, 0:128], NCH, 128), op=ALU.mult))
        st["pbc1_free"] = ev_u
        st["pbc2_free"] = ev_u
        st["x_free"][par] = ev_u
        st["x_free_s"][par] = P.ev("scalar")
        return ev_u

    def emit_kv(ev_u, slot, par):
        u5b = sb["u5b0"] if par == 0 else sb["u5b1"]
        ev_ke = None
        bank_s = {0: kv_state["pQK_free_s"], 1: kv_state["pQK_free_s"]}
        for ec in range(NCH):
            bank = sb["pQK0"] if ec % 2 == 0 else sb["pQK1"]
            fw = [("vector", ev_u), ("scalar", bank_s[ec % 2]),
                  ("vector", kv_state["pQK_free_v"]),
                  ("vector", st["pbc1_free"])] + cw
            for c in range(NCH):
                ev_km = P.mm(lambda E, ec=ec, c=c, bank=bank, u5b=u5b: E.matmul(
                    bank[:, 0:128],
                    sb["kwtS"][:, c * D + ec * 128: c * D + ec * 128 + 128],
                    ccol(u5b, c, 128),
                    start=(c == 0), stop=(c == NCH - 1)),
                    waits=(fw if c == 0 else []), inc=(c == NCH - 1))
            for hh in range(2):
                h = 2 * ec + hh
                ev_ke = P.op("scalar", lambda E, ec=ec, hh=hh, h=h, slot=slot,
                             bank=bank: E.activation(
                    sb["kr"][:, slot * 2048 + h * 128: slot * 2048 + h * 128 + 128],
                    bank[hh * 64:(hh + 1) * 64, 0:128], AF.Identity,
                    bias=sb["kb"][hh * 64:(hh + 1) * 64, ec:ec + 1]),
                    waits=[("tensor", ev_km),
                           ("tensor", kv_state["kslot_free"].get((slot, h), 0))])
            bank_s[ec % 2] = ev_ke
        kv_state["pQK_free_s"] = ev_ke
        ev_ve = None
        for g in range(2):
            bank = sb["pQK0"] if g == 0 else sb["pQK1"]
            for c in range(NCH):
                ev_vm = P.mm(lambda E, g=g, c=c, bank=bank, u5b=u5b: E.matmul(
                    bank[:, 0:512],
                    ccol(u5b, c, 128),
                    sb["vwtS"][:, c * D + g * 512: c * D + g * 512 + 512],
                    start=(c == 0), stop=(c == NCH - 1)),
                    waits=([("scalar", ev_ke), ("scalar", bank_s[g]),
                            ("tensor", kv_state["vslot_free"].get((slot, g), 0))]
                           if c == 0 else []),
                    inc=(c == NCH - 1))
            ev_ve = P.op("vector", lambda E, g=g, slot=slot, bank=bank: E.tensor_add(
                sb["vr"][:, slot * D + g * 512: slot * D + g * 512 + 512],
                bank[:, 0:512], sb["vbb"][:, g * 512:(g + 1) * 512]),
                waits=[("tensor", ev_vm), ("vector", ev_vbb)])
        kv_state["pQK_free_v"] = ev_ve
        st["u5_free_t"][par] = P.ev("tensor")
        return ev_ve

    def emit_qproj(par):
        u5b = sb["u5b0"] if par == 0 else sb["u5b1"]
        qsb = sb["qsb0"] if par == 0 else sb["qsb1"]
        ev_qe = None
        qbank_s = {0: kv_state["pQK_free_s"], 1: kv_state["pQK_free_s"]}
        for ec in range(NCH):
            bank = sb["pQK0"] if ec % 2 == 0 else sb["pQK1"]
            fw = [("scalar", qbank_s[ec % 2]), ("vector", kv_state["pQK_free_v"])]
            for c in range(NCH):
                ev_qm = P.mm(lambda E, ec=ec, c=c, bank=bank, u5b=u5b: E.matmul(
                    bank[:, 0:128],
                    sb["qwtS"][:, c * D + ec * 128: c * D + ec * 128 + 128],
                    ccol(u5b, c, 128),
                    start=(c == 0), stop=(c == NCH - 1)),
                    waits=(fw if c == 0 else []), inc=(c == NCH - 1))
            for hh in range(2):
                h = 2 * ec + hh
                ev_qe = P.op("scalar", lambda E, ec=ec, hh=hh, h=h, bank=bank,
                             qsb=qsb: E.activation(
                    qsb[:, h * 128:(h + 1) * 128],
                    bank[hh * 64:(hh + 1) * 64, 0:128], AF.Identity, scale=0.125,
                    bias=sb["qb8"][hh * 64:(hh + 1) * 64, ec:ec + 1]),
                    waits=[("tensor", ev_qm), ("vector", ev_qb8),
                           ("tensor", head_ev.get(("q_free", par, h), 0))])
            qbank_s[ec % 2] = ev_qe
        kv_state["pQK_free_s"] = ev_qe
        return ev_qe

    for hb in range(2):
        ev_u = emit_ln_u5(halo3, hb * 128, hb)
        emit_kv(ev_u, 2 + hb, hb)

    for pair in range(NT5 // 2):
        j0 = pair * 2
        info = {}
        for jj in (j0, j0 + 1):
            par = jj % 2
            ev_u = emit_ln_u5(xin3, jj * 128, par)
            ev_kv = emit_kv(ev_u, jj % 4, par)
            ev_qe = emit_qproj(par)
            info[jj] = {"kv": ev_kv, "qe": ev_qe,
                        "mask": sb["mA0"] if jj == 0 else (sb["mA1"] if jj == 1
                                                           else sb["mB"]),
                        "slots": [(jj - 2) % 4, (jj - 1) % 4, jj % 4]}
        elems = [(j0, h) for h in range(16)] + [(j0 + 1, h) for h in range(16)]
        NS = len(elems)
        if pair == 0:
            sb["_pipe"] = dict(sc={}, red={}, rc={}, tp={}, ov={}, wc={},
                               pw2={}, ma={}, exp={}, oev={})
        pp = sb["_pipe"]
        B = pair * NS
        sc_done, red_done, rc_done = pp["sc"], pp["red"], pp["rc"]
        tp_done, ov_done, wc_done = pp["tp"], pp["ov"], pp["wc"]
        pw2_done, ma_done, exp_done, oev_done = (pp["pw2"], pp["ma"],
                                                 pp["exp"], pp["oev"])

        def emit_scores(s):
            jj, h = elems[s]
            psc = sb["psc0"] if s % 2 == 0 else sb["psc1"]
            qsb = sb["qsb0"] if jj % 2 == 0 else sb["qsb1"]
            fw0 = [("scalar", info[jj]["qe"]),
                   ("scalar", exp_done.get(B + s - 2, 0)),
                   ("vector", head_ev.get(("psc", s % 2), 0))]
            if B + s < 2:
                fw0 += sb["const_waits"]
            ev = None
            for wi, slot in enumerate(info[jj]["slots"]):
                P.mm(lambda E, h=h, wi=wi, slot=slot, psc=psc, qsb=qsb:
                     E.matmul(
                    psc[:, wi * 128:(wi + 1) * 128],
                    qsb[:, h * 128:(h + 1) * 128],
                    sb["kr"][:, slot * 2048 + h * 128: slot * 2048 + h * 128 + 128],
                    start=True, stop=False),
                    waits=(fw0 if wi == 0 else []), inc=False)
                ev = P.mm(lambda E, wi=wi, psc=psc, mk=info[jj]["mask"]: E.matmul(
                    psc[:, wi * 128:(wi + 1) * 128], cm["identb"][:, :],
                    mk[:, wi * 128:(wi + 1) * 128],
                    start=False, stop=True), inc=(wi == 2))
            sc_done[B + s] = ev

        def emit_softmax(s):
            jj, h = elems[s]
            psc = sb["psc0"] if s % 2 == 0 else sb["psc1"]
            par = s % 4
            pw = sb["pw"][:, par * 384:(par + 1) * 384]
            nx = sb["rsc"][:, par * 4:par * 4 + 1]
            rs = sb["rsc"][:, par * 4 + 1:par * 4 + 2]
            ri = sb["rsc"][:, par * 4 + 2:par * 4 + 3]
            red_done[B + s] = P.op("vector", lambda E, psc=psc, nx=nx: E.reduce_max(
                nx, psc[:, 0:384], axis=AX.X, negate=True),
                waits=[("tensor", sc_done[B + s]),
                       ("scalar", exp_done.get(B + s - 4, 0))])
            exp_done[B + s] = P.op("scalar", lambda E, psc=psc, pw=pw, nx=nx, rs=rs:
                               E.activation(pw, psc[:, 0:384], AF.Exp, bias=nx,
                                            accum_out=rs),
                               waits=[("vector", red_done[B + s]),
                                      ("vector", rc_done.get(B + s - 4, 0)),
                                      ("tensor", tp_done.get(B + s - 4, 0))])
        def emit_sm2(s):
            par = s % 4
            pw = sb["pw"][:, par * 384:(par + 1) * 384]
            rs = sb["rsc"][:, par * 4 + 1:par * 4 + 2]
            ri = sb["rsc"][:, par * 4 + 2:par * 4 + 3]
            rc_done[B + s] = P.op("vector", lambda E, rs=rs, ri=ri: E.reciprocal(ri, rs),
                              waits=[("scalar", exp_done[B + s]),
                                     ("vector", pw2_done.get(B + s - 4, 0))])
            pw2_done[B + s] = P.op("vector", lambda E, pw=pw, ri=ri: E.tensor_scalar(
                pw, pw, ri, None, op0=ALU.mult))

        def emit_tp(s):
            par = s % 2
            bank = sb["ptb5a"] if par == 0 else sb["ptb5b"]
            pw = sb["pw"][:, (s % 4) * 384:(s % 4) * 384 + 384]
            ev = None
            for wi in range(3):
                ev = P.mm(lambda E, wi=wi, bank=bank, pw=pw: E.transpose(
                    bank[:, wi * 128:(wi + 1) * 128],
                    pw[:, wi * 128:(wi + 1) * 128], cm["identb"][:, :]),
                    waits=([("vector", pw2_done[B + s]),
                            ("scalar", wc_done.get(B + s - 2, 0))] if wi == 0 else []),
                    inc=(wi == 2))
            tp_done[B + s] = ev
            wc_done[B + s] = P.op("scalar", lambda E, par=par, bank=bank: E.activation(
                sb["wT"][:, par * 384:(par + 1) * 384], bank[:, 0:384], AF.Copy),
                waits=[("tensor", ev), ("tensor", ov_done.get(B + s - 2, 0))])

        def emit_ov(s):
            jj, h = elems[s]
            par = s % 2
            ocol = (s % 4) * 128
            ev = None
            for wi, slot in enumerate(info[jj]["slots"]):
                ev = P.mm(lambda E, h=h, wi=wi, slot=slot, ocol=ocol, par=par:
                          E.matmul(
                    sb["pO"][0:64, ocol:ocol + 128],
                    sb["vr"][:, slot * D + h * 64: slot * D + h * 64 + 64],
                    sb["wT"][:, par * 384 + wi * 128: par * 384 + wi * 128 + 128],
                    start=(wi == 0), stop=(wi == 2)),
                    waits=([("scalar", wc_done[B + s]), ("vector", info[jj]["kv"]),
                            ("vector", oev_done.get(B + s - 4,
                                head_ev.get("pOx", 0)))]
                           if wi == 0 else []),
                    inc=(wi == 2))
            ov_done[B + s] = ev
            if s % 4 == 3:
                q0s = s - 3
                evq = None
                for sub in range(2):
                    jjq, hq = elems[q0s + sub]
                    aosb = sb["aosb0"] if jjq % 2 == 0 else sb["aosb1"]
                    evq = P.op("vector", lambda E, hq=hq, sub=sub, aosb=aosb:
                               E.tensor_copy(
                        aosb[sub * 64:sub * 64 + 64,
                             (hq // 2) * 128:(hq // 2) * 128 + 256]
                            .rearrange("p (a t) -> p a t", a=2),
                        sb["pO"][0:64, 0:512].rearrange("p (a t) -> p a t", a=4)
                            [:, sub:4:2, :]),
                        waits=[("tensor", ev)] if sub == 0 else [])
                for ss in range(q0s, s + 1):
                    oev_done[B + ss] = evq
                head_ev["pOx"] = evq

        for step in range(NS + 6):
            if step < NS:
                emit_scores(step)
            if 6 <= step <= NS + 5:
                emit_ov(step - 6)
            if 4 <= step <= NS + 3:
                emit_tp(step - 4)
            if 2 <= step <= NS + 1:
                emit_sm2(step - 2)
            if step < NS:
                emit_softmax(step)

        for jj in (j0, j0 + 1):
            par = jj % 2
            xt = sb["xt0"] if par == 0 else sb["xt1"]
            aosb = sb["aosb0"] if par == 0 else sb["aosb1"]
            last_oev = oev_done[B + (15 if jj == j0 else 31)]
            ev_xn = None
            for ec in range(NCH):
                bank = sb["psc0"] if ec % 2 == 0 else sb["psc1"]
                fw = [("vector", last_oev), ("vector", head_ev.get(("psc", ec % 2), 0)),
                      ("scalar", exp_done.get(B + 30, 0)),
                      ("scalar", exp_done.get(B + 31, 0))]
                for c in range(NCH):
                    ev_opm = P.mm(lambda E, ec=ec, c=c, bank=bank, aosb=aosb:
                                  E.matmul(
                        bank[:, 0:128],
                        sb["owtS"][:, c * D + ec * 128: c * D + ec * 128 + 128],
                        ccol(aosb, c, 128),
                        start=(c == 0), stop=(c == NCH - 1)),
                        waits=(fw if c == 0 else []), inc=(c == NCH - 1))
                ev_xn = P.op("vector", lambda E, ec=ec, bank=bank, xt=xt:
                             E.scalar_tensor_tensor(
                    ccol(xt, ec, 128), bank[:, 0:128], sb["ob_"][:, ec:ec + 1],
                    ccol(xt, ec, 128), op0=ALU.add, op1=ALU.add),
                    waits=[("tensor", ev_opm)])
                head_ev[("psc", ec % 2)] = ev_xn
            st["x_free"][par] = ev_xn
            st["xo"][par] = P.dma("sync", "s5_xo",
                                  xout3[:, :, jj * 128:(jj + 1) * 128],
                                  xt[:, :].rearrange("p (c t) -> p c t", c=NCH),
                                  waits=[("vector", ev_xn)])
            for h in range(16):
                head_ev[("q_free", par, h)] = P.ev("tensor")
        for h in range(16):
            kv_state["kslot_free"][((j0 + 2) % 4, h)] = P.ev("tensor")
            kv_state["kslot_free"][((j0 + 3) % 4, h)] = P.ev("tensor")
        for g in range(2):
            kv_state["vslot_free"][((j0 + 2) % 4, g)] = P.ev("tensor")
            kv_state["vslot_free"][((j0 + 3) % 4, g)] = P.ev("tensor")
    return st


def build_s5(t_core):
    decls = dict(CONST_DECL)
    decls.update({
        "xin": ((D, t_core), F32, "ExternalInput"),
        "xout": ((D, t_core), F32, "ExternalOutput"),
        "halo_x": ((D, 256), F32, "ExternalInput"),
        "maskA0": ((128, 384), F32, "ExternalInput"),
        "maskA1": ((128, 384), F32, "ExternalInput"),
        "maskB": ((128, 384), F32, "ExternalInput"),
        "qwt": ((D, D), F32, "ExternalInput"),
        "kwt": ((D, D), F32, "ExternalInput"),
        "vwt": ((D, D), F32, "ExternalInput"),
        "owt": ((D, D), F32, "ExternalInput"),
        "qb": ((128, NCH), F32, "ExternalInput"),
        "kb": ((128, NCH), F32, "ExternalInput"),
        "vbrow": ((1, D), F32, "ExternalInput"),
        "obias": ((128, NCH), F32, "ExternalInput"),
    })

    def emit(P, cm, aps, es, nc):
        sb = alloc_s5(nc, es, t_core)
        sb["eps_ln"] = cm["fconsts"][0:1, 0:1]
        emit_s5(P, sb, cm, aps, t_core)
    return build_launch(decls, emit, t_core)


def make_masks(first_half):
    qi = np.arange(128)[:, None]
    ki = np.arange(384)[None, :]
    band = (ki >= qi) & (ki <= qi + 256)
    mB = np.where(band, 0.0, NEG_BIG).astype(np.float32)
    if first_half:
        mA0 = np.where(band & (ki >= 256), 0.0, NEG_BIG).astype(np.float32)
        mA1 = np.where(band & (ki >= 128), 0.0, NEG_BIG).astype(np.float32)
    else:
        mA0 = mB.copy()
        mA1 = mB.copy()
    return mA0, mA1, mB



_PROG_CACHE = {}


def _prog(name, t_core):
    key = (name, t_core)
    if key not in _PROG_CACHE:
        builder = {"pre": lambda t: build_pre(t, reverse=False),
                   "post": lambda t: build_pre(t, reverse=True),
                   "s1": build_s1, "s2": build_s2, "s4": build_s4,
                   "s5": build_s5}[name]
        _PROG_CACHE[key] = builder(t_core)
    return _PROG_CACHE[key]


def _dev_runner(nc, in_maps):
    res = run_bass_kernel_spmd(nc, in_maps, core_ids=list(range(len(in_maps))))
    return res.results


def run_chain(inputs, n_cores, t_core, seq_len, runner):
    g = prep_host(inputs)
    x_full = np.asarray(inputs["L0_attn_Z"], np.float32)
    B = x_full.shape[0]
    xs = []
    for c in range(n_cores):
        b, h = c // 2, c % 2
        xs.append(np.ascontiguousarray(x_full[b, h * t_core:(h + 1) * t_core, :]))
    consts = {"identf": g["identf"], "ones_row": g["ones_row"],
              "ones_col": g["ones_col"],
              "fconsts": np.array([[LN_EPS, RMS_EPS, 0, 0, 0, 0, 0, 0]], np.float32)}
    zcarry = np.zeros((128, NCH), np.float32)
    zhalo = np.zeros((D, 256), np.float32)
    masks = {True: make_masks(True), False: make_masks(False)}

    res = runner(_prog("pre", t_core),
                 [{**consts, "xin": xs[c]} for c in range(n_cores)])
    xf = [np.ascontiguousarray(r["xout"]) for r in res]

    def s1_maps():
        return [{**consts, "xin": xf[c], "w1t": g["w1t"], "w2t": g["w2t"],
                 "b1s": g["b1s"], "b2s": g["b2s"], "nws": g["nws"],
                 "gwt": g["gwt"], "selrows": g["selrows"]} for c in range(n_cores)]

    res = runner(_prog("s1", t_core), s1_maps())
    xf = [np.ascontiguousarray(r["xout"]) for r in res]

    def s2_maps(carries):
        return [{**consts, "xin": xf[c], "carry_in": carries[c],
                 "l0wt": g["l0wt"], "l0b": g["l0b"], "l0g": g["l0g"],
                 "l0bt": g["l0bt"], "m1tri": g["m1tri"], "tri01": g["tri01"],
                 "iota_row": g["iota_row"], "decay": g["decay"]}
                for c in range(n_cores)]
    res = runner(_prog("s2", t_core), s2_maps([zcarry] * n_cores))
    carries = [np.ascontiguousarray(r["carry_out"]) for r in res]
    routed = [zcarry if c % 2 == 0 else carries[c - 1] for c in range(n_cores)]
    res = runner(_prog("s2", t_core), s2_maps(routed))
    xf = [np.ascontiguousarray(r["xout"]) for r in res]

    res = runner(_prog("s1", t_core), s1_maps())
    xf = [np.ascontiguousarray(r["xout"]) for r in res]

    def s4_maps():
        return [{**consts, "xin": xf[c], "f1t": g["f1t"], "f2t": g["f2t"],
                 "h1t": g["h1t"], "h2t": g["h2t"], "f3t": g["f3t"],
                 "s1s": g["s1s"], "s2s": g["s2s"], "sh1": g["sh1"],
                 "sh2": g["sh2"], "s3s": g["s3s"]} for c in range(n_cores)]
    res = runner(_prog("s4", t_core), s4_maps())
    xf = [np.ascontiguousarray(r["xout"]) for r in res]

    maps5 = []
    for c in range(n_cores):
        first = (c % 2 == 0)
        mA0, mA1, mB = masks[first]
        halo = zhalo if first else np.ascontiguousarray(xf[c - 1][:, t_core - 256:])
        maps5.append({**consts, "xin": xf[c], "halo_x": halo,
                      "maskA0": mA0, "maskA1": mA1, "maskB": mB,
                      "qwt": g["qwt"], "kwt": g["kwt"], "vwt": g["vwt"],
                      "owt": g["owt"], "qb": g["qb"], "kb": g["kb"],
                      "vbrow": np.ascontiguousarray(
                          np.asarray(inputs["L1_attn_v_b"], np.float32)[None, :]),
                      "obias": g["ob"]})
    res = runner(_prog("s5", t_core), maps5)
    xf = [np.ascontiguousarray(r["xout"]) for r in res]

    res = runner(_prog("s4", t_core), s4_maps())
    xf = [np.ascontiguousarray(r["xout"]) for r in res]

    res = runner(_prog("post", t_core),
                 [{**consts, "xin": xf[c]} for c in range(n_cores)])
    outs = [np.ascontiguousarray(r["xout"]) for r in res]
    full = np.empty((B, 2 * t_core, D), np.float32)
    for c in range(n_cores):
        b, h = c // 2, c % 2
        full[b, h * t_core:(h + 1) * t_core, :] = outs[c]
    return full


def kernel(**inputs):
    import os
    if os.environ.get("MK_UNFUSED"):
        return run_chain(inputs, NCORES, 4096, 8192, _dev_runner)
    return kernel_fused(**inputs)



from concourse.expressions import make_scalar_value as _msv


def _emit_mb_init(P, mb):
    ev = P.dma("sync", "mb_ctl", mb["ctl_sb"][:, :], mb["ctl_ap"])

    def closure(E):
        nc = P.nc
        ET = mybir.EngineType
        mb["my"] = P.nc.values_load(mb["ctl_sb"][0:1, 0:1], engines=[ET.SP],
                                    min_val=0, max_val=1,
                                    skip_runtime_bounds_check=True)
        mb["other"] = P.nc.values_load(mb["ctl_sb"][0:1, 1:2], engines=[ET.SP],
                                       min_val=0, max_val=1,
                                       skip_runtime_bounds_check=True)
        mb["nonce_c"] = P.nc.values_load(mb["ctl_sb"][0:1, 2:3], engines=[ET.SP],
                                         min_val=0, max_val=1 << 30,
                                         skip_runtime_bounds_check=True)
        mb["nonce_h"] = P.nc.values_load(mb["ctl_sb"][0:1, 3:4], engines=[ET.SP],
                                         min_val=0, max_val=1 << 30,
                                         skip_runtime_bounds_check=True)
        mb["skip"] = P.nc.values_load(mb["ctl_sb"][0:1, 4:5], engines=[ET.SP],
                                      min_val=0, max_val=1,
                                      skip_runtime_bounds_check=True)
        mb["src"] = P.nc.values_load(mb["ctl_sb"][0:1, 5:6], engines=[ET.SP],
                                     min_val=0, max_val=1,
                                     skip_runtime_bounds_check=True)
    P.raw("sync", closure, waits=[("mb_ctl", ev)])


def _emit_exchange(P, mb, tag, data_out_ap_fn, data_in_ap_fn, pre_waits):
    psem = mb["psem_" + tag]
    flag = mb["flag_" + tag]
    nonce_key = "nonce_" + tag

    def closure(E):
        myv, otv = mb["my"], mb["other"]
        dst, srcv = data_out_ap_fn(myv)
        E.dma_start(out=dst, in_=srcv).then_inc(psem, 16)
        E.wait_ge(psem, 16)
        E.dma_start(out=flag.ap()[bass.ds(myv, 1), 0:1],
                    in_=mb["ctl_sb"][0:1, (2 if tag == "c" else 3):
                                     (3 if tag == "c" else 4)]).then_inc(psem, 16)
        E.wait_ge(psem, 32)
        with E.register("exp_" + tag) as exp:
            E.reg_mov(exp, 32)

            def cond():
                E.reg_add(exp, exp, 16)
                E.dma_start(out=mb["ftmp"][0:1, 0:1],
                            in_=flag.ap()[bass.ds(otv, 1), 0:1]).then_inc(psem, 16)
                E.wait_ge(psem, _msv(exp, min_val=0, max_val=1 << 26))
                v = P.nc.values_load(mb["ftmp"][0:1, 0:1],
                                     engines=[mybir.EngineType.SP],
                                     min_val=-(1 << 30), max_val=1 << 30,
                                     skip_runtime_bounds_check=True)
                return ((v != mb[nonce_key]) & (mb["skip"] != 1) & (
                    _msv(exp, min_val=0, max_val=1 << 26) < 16 * 400000))
            with E.While(cond):
                E.nop()
            dsti, srci = data_in_ap_fn(mb["src"])
            E.reg_add(exp, exp, 16)
            E.dma_start(out=dsti, in_=srci).then_inc(psem, 16)
            E.wait_ge(psem, _msv(exp, min_val=0, max_val=1 << 26))
    P.raw("sync", closure, waits=pre_waits)
    return P.op("sync", lambda E: E.nop(), inc=P.chan("mb_" + tag), inc_k=1,
                chain=False)


def build_fused(t_core):
    decls = dict(CONST_DECL)
    decls.update({
        "xtm": ((t_core, D), F32, "ExternalInput"),
        "xout_tm": ((t_core, D), F32, "ExternalOutput"),
        "w1t": ((8, D, 2048), F32, "ExternalInput"),
        "w2t": ((8, 2048, D), F32, "ExternalInput"),
        "b1s": ((128, 128), F32, "ExternalInput"),
        "b2s": ((8, D), F32, "ExternalInput"),
        "nws": ((128, NCH), F32, "ExternalInput"),
        "gwt": ((128, 64), F32, "ExternalInput"),
        "selrows": ((8, 1024), F32, "ExternalInput"),
        "l0wt": ((D, D), F32, "ExternalInput"),
        "l0b": ((128, NCH), F32, "ExternalInput"),
        "l0g": ((128, NCH), F32, "ExternalInput"),
        "l0bt": ((128, NCH), F32, "ExternalInput"),
        "m1tri": ((128, 128), F32, "ExternalInput"),
        "tri01": ((128, 128), F32, "ExternalInput"),
        "iota_row": ((1, 4096), F32, "ExternalInput"),
        "decay": ((1, 1), F32, "ExternalInput"),
        "selcol": ((128, 1), F32, "ExternalInput"),
        "f1t": ((D, 4096), F32, "ExternalInput"),
        "f2t": ((D, 4096), F32, "ExternalInput"),
        "h1t": ((4096, 4096), F32, "ExternalInput"),
        "h2t": ((4096, 4096), F32, "ExternalInput"),
        "f3t": ((4096, D), F32, "ExternalInput"),
        "s1s": ((128, 32), F32, "ExternalInput"),
        "s2s": ((128, 32), F32, "ExternalInput"),
        "sh1": ((128, 32), F32, "ExternalInput"),
        "sh2": ((128, 32), F32, "ExternalInput"),
        "s3s": ((128, 8), F32, "ExternalInput"),
        "maskA0": ((128, 384), F32, "ExternalInput"),
        "maskA1": ((128, 384), F32, "ExternalInput"),
        "maskB": ((128, 384), F32, "ExternalInput"),
        "qwt": ((D, D), F32, "ExternalInput"),
        "kwt": ((D, D), F32, "ExternalInput"),
        "vwt": ((D, D), F32, "ExternalInput"),
        "owt": ((D, D), F32, "ExternalInput"),
        "qb": ((128, NCH), F32, "ExternalInput"),
        "kb": ((128, NCH), F32, "ExternalInput"),
        "vbrow": ((1, D), F32, "ExternalInput"),
        "obias": ((128, NCH), F32, "ExternalInput"),
        "mbctl": ((1, 8), I32, "ExternalInput"),
    })

    import os
    stages = set(os.environ.get(
        "MK_STAGES", "pre,s1,s2,s3,s4,s5,s6,post,mbc,mbh").split(","))

    def emit(P, cm, aps, es, nc):
        xres = nc.dram_tensor("xres", [D, t_core], F32).ap()
        u4 = nc.dram_tensor("u4", [D, t_core], BF16).ap()
        ha = nc.dram_tensor("ha", [4096, t_core], BF16).ap()
        hb = nc.dram_tensor("hb", [4096, t_core], BF16).ap()
        halo_loc = nc.dram_tensor("halo_loc", [D, 256], F32).ap()
        mcarry = nc.dram_tensor("mcarry", [2, 128, NCH], F32, addr_space="Shared")
        mhalo = nc.dram_tensor("mhalo", [2, D, 256], F32, addr_space="Shared")
        mflagc = nc.dram_tensor("mflagc", [2, 16], I32, addr_space="Shared")
        mflagh = nc.dram_tensor("mflagh", [2, 16], I32, addr_space="Shared")
        mb = {"flag_c": mflagc, "flag_h": mflagh, "ctl_ap": aps["mbctl"]}
        mb["psem_c"] = es.enter_context(nc.semaphore("poll_c"))
        mb["psem_h"] = es.enter_context(nc.semaphore("poll_h"))
        mb["ctl_sb"] = es.enter_context(nc.sbuf_tensor("mb_ctl", [1, 8], I32))
        mb["ftmp"] = es.enter_context(nc.sbuf_tensor("mb_ftmp", [1, 1], I32))
        selcol = es.enter_context(nc.sbuf_tensor("mb_selcol", [128, 1], F32))
        ev_sel = P.dma("sync", "c_const", selcol[:, :], aps["selcol"])
        _emit_mb_init(P, mb)
        eps_io = {"eps_ln": cm["fconsts"][0:1, 0:1],
                  "eps_rms": cm["fconsts"][0:1, 1:2]}

        if "pre" in stages:
            with ExitStack() as ses:
                sbp = alloc_pp(nc, ses, t_core, "_f0")
                emit_pre(P, sbp, cm, {"xin": aps["xtm"], "xout": xres}, t_core)
                P.barrier_all()
        io1 = {"xin": xres, "xout": xres, "w1t": aps["w1t"], "w2t": aps["w2t"],
               "b1s": aps["b1s"], "b2s": aps["b2s"], "nws": aps["nws"],
               "gwt": aps["gwt"], "selrows": aps["selrows"]}
        if "s1" in stages:
            with ExitStack() as ses:
                sb1 = alloc_s1(nc, ses, t_core, "_f1")
                sb1.update(eps_io)
                emit_s1(P, sb1, cm, io1, t_core)
                P.barrier_all()
        io2 = {"xin": xres, "xout": xres, "l0wt": aps["l0wt"], "l0b": aps["l0b"],
               "l0g": aps["l0g"], "l0bt": aps["l0bt"], "m1tri": aps["m1tri"],
               "tri01": aps["tri01"], "iota_row": aps["iota_row"],
               "decay": aps["decay"]}
        if "s2" in stages:
            with ExitStack() as ses:
                sb2 = alloc_s2(nc, ses, t_core, "_f2")
                sb2.update(eps_io)
                cin_hold = ses.enter_context(
                    nc.sbuf_tensor("mb_cin", [128, NCH], F32))
                if "mbc" in stages:
                    st2 = emit_s2(P, sb2, cm, io2, t_core, carry_only=True)
                    cfin = st2["cfin"]
                    ev_cin = _emit_exchange(
                        P, mb, "c",
                        lambda my: (mcarry.ap()[bass.ds(my, 1), :, :], cfin[:, :]),
                        lambda ot: (cin_hold[:, :], mcarry.ap()[bass.ds(ot, 1), :, :]),
                        pre_waits=[("vector", st2["cfin_ev"]), ("c_const", ev_sel)])
                    P.barrier_all()
                    ev_cinm = P.op("vector", lambda E: E.tensor_scalar(
                        sb2["cin"][:, :], cin_hold[:, :], selcol[:, 0:1], None,
                        op0=ALU.mult), waits=[("mb_c", ev_cin)])
                else:
                    P.op("vector", lambda E: E.memset(sb2["cin"][:, :], 0.0))
                emit_s2(P, sb2, cm, io2, t_core, cin_ev=None)
                P.barrier_all()
        if "s3" in stages:
            with ExitStack() as ses:
                sb3 = alloc_s1(nc, ses, t_core, "_f3")
                sb3.update(eps_io)
                emit_s1(P, sb3, cm, io1, t_core)
                P.barrier_all()
        io4 = {"xin": xres, "xout": xres, "f1t": aps["f1t"], "f2t": aps["f2t"],
               "h1t": aps["h1t"], "h2t": aps["h2t"], "f3t": aps["f3t"],
               "s1s": aps["s1s"], "s2s": aps["s2s"], "sh1": aps["sh1"],
               "sh2": aps["sh2"], "s3s": aps["s3s"], "u4": u4, "ha": ha,
               "hb": hb}
        if "s4" in stages:
            with ExitStack() as ses:
                sb4 = alloc_s4(nc, ses, t_core, "_f4")
                sb4.update(eps_io)
                emit_s4(P, sb4, cm, io4, t_core)
                P.barrier_all()
        if "mbh" in stages:
            ev_halo = _emit_exchange(
                P, mb, "h",
                lambda my: (mhalo.ap()[bass.ds(my, 1), :, :],
                            xres[:, t_core - 256:t_core]),
                lambda ot: (halo_loc[:, :], mhalo.ap()[bass.ds(ot, 1), :, :]),
                pre_waits=[])
        else:
            ev_fill = P.dma("sync", "mb_hl", halo_loc[:, :], xres[:, 0:256])
            ev_halo = P.op("sync", lambda E: E.nop(),
                           waits=[("mb_hl", ev_fill)], inc=P.chan("mb_h"),
                           inc_k=1, chain=False)
        io5 = {"xin": xres, "xout": xres, "halo_x": halo_loc,
               "maskA0": aps["maskA0"], "maskA1": aps["maskA1"],
               "maskB": aps["maskB"], "qwt": aps["qwt"], "kwt": aps["kwt"],
               "vwt": aps["vwt"], "owt": aps["owt"], "qb": aps["qb"],
               "kb": aps["kb"], "vbrow": aps["vbrow"], "obias": aps["obias"]}
        if "s5" in stages:
            with ExitStack() as ses:
                sb5 = alloc_s5(nc, ses, t_core, "_f5")
                sb5.update(eps_io)
                P.op("vector", lambda E: E.engine_nop(), waits=[("mb_h", ev_halo)])
                emit_s5(P, sb5, cm, io5, t_core)
                P.barrier_all()
        if "s6" in stages:
            with ExitStack() as ses:
                sb6 = alloc_s4(nc, ses, t_core, "_f6")
                sb6.update(eps_io)
                emit_s4(P, sb6, cm, io4, t_core)
                P.barrier_all()
        with ExitStack() as ses:
            sbq = alloc_pp(nc, ses, t_core, "_f7")
            emit_pre(P, sbq, cm, {"xin": xres, "xout": aps["xout_tm"]},
                     t_core, reverse=True)
            P.barrier_all()
    return build_launch(decls, emit, t_core)


def kernel_fused(**inputs):
    g = prep_host(inputs)
    t_core = 4096
    x_full = np.asarray(inputs["L0_attn_Z"], np.float32)
    B = x_full.shape[0]
    nc = _prog_fused(t_core)
    nonce_c = int(np.random.randint(1, 1 << 30))
    nonce_h = int(np.random.randint(1, 1 << 30))
    consts = {"identf": g["identf"], "ones_row": g["ones_row"],
              "ones_col": g["ones_col"],
              "fconsts": np.array([[LN_EPS, RMS_EPS, 0, 0, 0, 0, 0, 0]], np.float32)}
    in_maps = []
    for c in range(NCORES):
        b, h = c // 2, c % 2
        first = (h == 0)
        mA0, mA1, mB = make_masks(first)
        in_maps.append({
            **consts,
            "xtm": np.ascontiguousarray(x_full[b, h * t_core:(h + 1) * t_core, :]),
            "w1t": g["w1t"], "w2t": g["w2t"], "b1s": g["b1s"], "b2s": g["b2s"],
            "nws": g["nws"], "gwt": g["gwt"], "selrows": g["selrows"],
            "l0wt": g["l0wt"], "l0b": g["l0b"], "l0g": g["l0g"],
            "l0bt": g["l0bt"], "m1tri": g["m1tri"], "tri01": g["tri01"],
            "iota_row": g["iota_row"], "decay": g["decay"],
            "selcol": np.full((128, 1), 0.0 if first else 1.0, np.float32),
            "f1t": g["f1t"], "f2t": g["f2t"], "h1t": g["h1t"], "h2t": g["h2t"],
            "f3t": g["f3t"], "s1s": g["s1s"], "s2s": g["s2s"], "sh1": g["sh1"],
            "sh2": g["sh2"], "s3s": g["s3s"],
            "maskA0": mA0, "maskA1": mA1, "maskB": mB,
            "qwt": g["qwt"], "kwt": g["kwt"], "vwt": g["vwt"], "owt": g["owt"],
            "qb": g["qb"], "kb": g["kb"],
            "vbrow": np.ascontiguousarray(
                np.asarray(inputs["L1_attn_v_b"], np.float32)[None, :]),
            "obias": g["ob"],
            "mbctl": np.array([[h, 1 - h, nonce_c, nonce_h,
                                1 if h == 0 else 0,
                                h if h == 0 else 1 - h, 0, 0]], np.int32),
        })
    res = run_bass_kernel_spmd(nc, in_maps, core_ids=list(range(NCORES)))
    full = np.empty((B, 2 * t_core, D), np.float32)
    for c in range(NCORES):
        b, h = c // 2, c % 2
        full[b, h * t_core:(h + 1) * t_core, :] = res.results[c]["xout_tm"]
    return full


def _prog_fused(t_core):
    key = ("fused", t_core)
    if key not in _PROG_CACHE:
        _PROG_CACHE[key] = build_fused(t_core)
    return _PROG_CACHE[key]
